# version 04
# baseline (speedup 1.0000x reference)
import numpy as np
import ml_dtypes
import concourse.bass as bass
import concourse.mybir as mybir
from concourse.bass_utils import run_bass_kernel_spmd

F32 = mybir.dt.float32
BF16 = mybir.dt.bfloat16
I32 = mybir.dt.int32
AF = mybir.ActivationFunctionType
ALU = mybir.AluOpType
AX = mybir.AxisListType

P = 128
H = 4096
DS = 4096
NH = 64
HD = 64
CONV = 4352
PROJ = 8512
L = 2048
EPS = 1e-5
QBHI = 128.0 - 1e-5

TCH = 1024
NCH = L // TCH
NTT = TCH // P
NCV = CONV // P
NKC = H // P
CVW = 3 + TCH
NJ1 = 35
NJ2 = 32
OB = 256
NOB = H // OB
NG = 8
NBLK = TCH // P


class Plan:
    ENGS = ("sy", "gp", "te", "sc", "ve")

    def __init__(self, sems):
        self.ops = {e: [] for e in self.ENGS}
        self.sems = sems
        self.cnt = {k: 0 for k in sems}
        self.last_wait = {}

    def emit(self, eng, fn):
        self.ops[eng].append(fn)

    def wait(self, eng, sem, val):
        if val is None or val <= 0:
            return
        key = (eng, sem)
        if self.last_wait.get(key, 0) >= val:
            return
        self.last_wait[key] = val
        h = self.sems[sem]
        self.emit(eng, lambda e, _h=h, _v=val: e.wait_ge(_h, _v))

    DRAIN_ENGS = ("ve",)
    ENG_OF = {"gp_c": "gp"}

    def sig(self, eng):
        self.cnt[eng] += 1
        h = self.sems[eng]
        stream = self.ENG_OF.get(eng, eng)
        if eng in self.DRAIN_ENGS:
            self.emit(stream, lambda e, _h=h: e.drain(fusable=False).then_inc(_h, 1))
        else:
            fn = self.ops[stream].pop()
            self.ops[stream].append(lambda e, _f=fn, _h=h: _f(e).then_inc(_h, 1))
        return self.cnt[eng]

    def dma(self, queue, sem, out_ap, in_ap, **kw):
        self.cnt[sem] += 16
        h = self.sems[sem]
        self.emit(queue, lambda e, _o=out_ap, _i=in_ap, _h=h:
                  e.dma_start(_o, _i, **kw).then_inc(_h, 16))
        return self.cnt[sem]


def build_program(n_dbg=0, dump_spec=()):
    nc = bass.Bass("TRN2", target_bir_lowering=False, debug=False)

    x_hbm = nc.dram_tensor("x", [L, H], F32, kind="ExternalInput").ap()
    w_in = nc.dram_tensor("w_in", [H, PROJ], F32, kind="ExternalInput").ap()
    w_out = nc.dram_tensor("w_out", [DS, H], F32, kind="ExternalInput").ap()
    w_ffn = nc.dram_tensor("w_ffn", [H, H], F32, kind="ExternalInput").ap()
    ln_w_fm = nc.dram_tensor("ln_w_fm", [P, NKC], F32, kind="ExternalInput").ap()
    norm_w_fm = nc.dram_tensor("norm_w_fm", [P, NKC], F32, kind="ExternalInput").ap()
    mup_fm = nc.dram_tensor("mup_fm", [P, 67], F32, kind="ExternalInput").ap()
    conv_w_fm = nc.dram_tensor("conv_w_fm", [P, NCV * 4], F32, kind="ExternalInput").ap()
    d_fm = nc.dram_tensor("d_fm", [P, DS // P], F32, kind="ExternalInput").ap()
    dt_bias_c = nc.dram_tensor("dt_bias_c", [NH, 1], F32, kind="ExternalInput").ap()
    a_log_c = nc.dram_tensor("a_log_c", [NH, 1], F32, kind="ExternalInput").ap()
    id_bf = nc.dram_tensor("id_bf", [P, P], BF16, kind="ExternalInput").ap()
    id_f32 = nc.dram_tensor("id_f32", [P, 64], F32, kind="ExternalInput").ap()
    ones_r = nc.dram_tensor("ones_r", [1, P], F32, kind="ExternalInput").ap()
    ones_cf = nc.dram_tensor("ones_cf", [P, 1], F32, kind="ExternalInput").ap()
    ones_bc = nc.dram_tensor("ones_bc", [P, 1], BF16, kind="ExternalInput").ap()
    lt_bf = nc.dram_tensor("lt_bf", [P, P], BF16, kind="ExternalInput").ap()
    epsc_in = nc.dram_tensor("epsc", [P, 1], F32, kind="ExternalInput").ap()
    neg1c_in = nc.dram_tensor("neg1c", [P, 1], F32, kind="ExternalInput").ap()
    out_hbm = nc.dram_tensor("out", [L, H], F32, kind="ExternalOutput").ap()
    wqt_hbm = nc.dram_tensor("wqt_scr", [H, H], BF16, kind="Internal").ap()
    dbg = [nc.dram_tensor(f"dbg{i}", [P, 8192], F32, kind="ExternalOutput").ap()
           for i in range(n_dbg)]
    dbg_names = []
    want = set(dump_spec)

    import contextlib
    with contextlib.ExitStack() as _st:
        en = _st.enter_context
        convin = en(nc.sbuf_tensor("convin", [P, NCV * CVW], BF16))
        xrq = en(nc.sbuf_tensor("xrq", [P, NKC * TCH], BF16))
        h_st = en(nc.sbuf_tensor("h_st", [P, NH * HD], F32))
        w1a = en(nc.sbuf_tensor("w1a", [P, NKC * P], BF16))
        w2 = en(nc.sbuf_tensor("w2", [P, 10240], BF16))
        wqa = en(nc.sbuf_tensor("wqa", [P, 8192], BF16))
        dtb = en(nc.sbuf_tensor("dtb", [P, 2 * TCH], F32))
        csts = en(nc.sbuf_tensor("csts", [P, 264], BF16))
        cstf = en(nc.sbuf_tensor("cstf", [P, 363], F32))
        stats = en(nc.sbuf_tensor("stats", [P, 96], F32))
        small = en(nc.sbuf_tensor("small", [P, 512], F32))
        resb = dtb[:, 0:512]
        pb0 = en(nc.psum_tensor("pb0", [P, 512], F32))
        pb1 = en(nc.psum_tensor("pb1", [P, 512], F32))
        pb2 = en(nc.psum_tensor("pb2", [P, 512], F32))
        pb3 = en(nc.psum_tensor("pb3", [P, 512], F32))
        pb4 = en(nc.psum_tensor("pb4", [P, 512], F32))
        pb5 = en(nc.psum_tensor("pb5", [P, 512], F32))
        pb6 = en(nc.psum_tensor("pb6", [P, 512], F32))
        pb7 = en(nc.psum_tensor("pb7", [P, 512], F32))
        s_te = en(nc.semaphore("s_te"))
        s_sc = en(nc.semaphore("s_sc"))
        s_ve = en(nc.semaphore("s_ve"))
        s_gpc = en(nc.semaphore("s_gpc"))
        s_syd = en(nc.semaphore("s_syd"))
        s_gpd = en(nc.semaphore("s_gpd"))
        s_scd = en(nc.semaphore("s_scd"))
        sems = {"te": s_te, "sc": s_sc, "ve": s_ve, "gp_c": s_gpc,
                "sy_d": s_syd, "gp_d": s_gpd, "sc_d": s_scd}
        pl = Plan(sems)
        banks = [pb0, pb1, pb2, pb3, pb4, pb5, pb6, pb7]
        bank_free = {i: ("te", 0) for i in range(8)}

        def bank_wait(eng, i):
            sem, val = bank_free[i]
            pl.wait(eng, sem, val)

        ID_BF = csts[:, 0:128]
        LT = csts[:, 128:256]
        ONES_BC = csts[:, 256:257]
        ID_F32 = cstf[:, 0:64]
        ONES_R = cstf[:, 64:192]
        ONES_CF = cstf[:, 192:193]
        LNW = cstf[:, 193:225]
        NRMW = cstf[:, 225:257]
        DFM = cstf[:, 257:289]
        MUP = cstf[:, 289:356]
        DTBI = cstf[:, 356:357]
        ALOG = cstf[:, 357:358]
        AC = cstf[:, 358:359]
        RWC = cstf[:, 359:360]
        SWC = cstf[:, 360:361]
        EPSC = cstf[:, 361:362]
        NEG1C = cstf[:, 362:363]
        CW4 = small[:, 0:136]
        ABSS = small[:, 136:168]
        BNO = small[:, 168:216]
        TC = small[:, 216:248]
        SPARE = small[:, 248:512]

        def stc(ch, tt, k):
            i = (ch * NTT + tt) * 6 + k
            return stats[:, i:i + 1]

        def vsp():
            pl.emit("ve", lambda e: e.tensor_copy(TC[:, 30:31], EPSC))

        def w2f32(off_b, n):
            return w2[:, off_b // 2: off_b // 2 + 2 * n].bitcast(F32)

        def w2bf(off_b, n):
            return w2[:, off_b // 2: off_b // 2 + n]

        def w2i32(off_b, n):
            return w2[:, off_b // 2: off_b // 2 + 2 * n].bitcast(I32)

        def wqf32(off_b, n):
            return wqa[:, off_b // 2: off_b // 2 + 2 * n].bitcast(F32)

        def wqbf(off_b, n):
            return wqa[:, off_b // 2: off_b // 2 + n]

        def cv(ft, c0, n):
            return convin[:, ft * CVW + c0: ft * CVW + c0 + n]

        def dump(tag, ap, waits):
            if len(dbg_names) >= len(dbg):
                return
            i = len(dbg_names)
            dbg_names.append(tag)
            for sem, val in waits:
                pl.wait("gp", sem, val)
            rows = ap.shape[0]
            tot = 1
            for s_ in ap.shape[1:]:
                tot *= s_
            pl.dma("gp", "gp_d", dbg[i][0:rows, 0:tot], ap)

        pl.dma("sy", "sy_d", ID_BF, id_bf)
        pl.dma("sy", "sy_d", LT, lt_bf)
        pl.dma("sy", "sy_d", ONES_BC, ones_bc)
        pl.dma("sy", "sy_d", ID_F32, id_f32)
        pl.dma("sy", "sy_d", ONES_R[0:1, 0:128], ones_r)
        pl.dma("sy", "sy_d", ONES_CF, ones_cf)
        pl.dma("sy", "sy_d", LNW, ln_w_fm)
        pl.dma("sy", "sy_d", NRMW, norm_w_fm)
        pl.dma("sy", "sy_d", DFM, d_fm)
        pl.dma("sy", "sy_d", MUP, mup_fm)
        pl.dma("sy", "sy_d", DTBI[0:NH, :], dt_bias_c)
        pl.dma("sy", "sy_d", CW4, conv_w_fm)
        pl.dma("sy", "sy_d", EPSC, epsc_in)
        pl.dma("sy", "sy_d", NEG1C, neg1c_in)
        v_ld = pl.dma("sy", "sy_d", ALOG[0:NH, :], a_log_c)
        pl.wait("sc", "sy_d", v_ld)
        pl.emit("sc", lambda e: e.activation(AC[0:NH, :], ALOG[0:NH, :], AF.Exp))
        pl.wait("ve", "sc", pl.sig("sc"))
        pl.emit("ve", lambda e: e.tensor_scalar_mul(AC[0:NH, :], AC[0:NH, :], -1.0))
        for ft in range(NCV):
            pl.emit("ve", lambda e, _f=ft: e.memset(cv(_f, 0, 3), 0.0))
        v_init = pl.sig("ve")

        v_prefill = pl.dma("sc", "sc_d", out_hbm, x_hbm)

        wl0_pre = pl.dma("gp", "gp_d",
                         w1a[:, 0:NKC * P].rearrange("p (kc j) -> p kc j", kc=NKC),
                         w_in[:, DS:DS + P].rearrange("(kc p) j -> p kc j", p=P))

        WSTG = h_st[:, 0:4096]
        WQR = dtb[:, 0:2048].bitcast(BF16)
        TSG = wqbf(8192, 4096)
        q_sig = {}
        st_sig = {}
        red_sig = {}
        for rb in range(NKC):
            if rb >= 1:
                pl.wait("sy", "sc", red_sig[rb - 1])
            vl = pl.dma("sy", "sy_d", WSTG, w_ffn[rb * P:(rb + 1) * P, :])
            pl.wait("sc", "sy_d", vl)
            pl.emit("sc", lambda e, _rb=rb: e.activation(
                TSG[:, 0:4096], WSTG, AF.Abs, accum_out=ABSS[:, _rb:_rb + 1]))
            red_sig[rb] = pl.sig("sc")
        pl.wait("ve", "sc", red_sig[NKC - 1])
        pl.emit("ve", lambda e: e.tensor_reduce(TC[:, 0:1], ABSS, AX.X, ALU.add))
        pl.wait("te", "ve", pl.sig("ve"))
        pl.emit("te", lambda e: e.matmul(pb7[0:1, 0:1], ONES_CF, TC[:, 0:1],
                                         start=True, stop=True))
        pl.wait("sc", "te", pl.sig("te"))
        pl.emit("sc", lambda e: e.activation(TC[0:1, 1:2], pb7[0:1, 0:1], AF.Identity,
                                             scale=1.0 / float(H * H)))
        pl.wait("ve", "sc", pl.sig("sc"))
        pl.emit("ve", lambda e: e.tensor_tensor(
            TC[0:1, 1:2], TC[0:1, 1:2], EPSC[0:1, :], ALU.add))
        pl.emit("ve", lambda e: e.tensor_copy(TC[:, 30:31], EPSC))
        pl.emit("ve", lambda e: e.tensor_copy(TC[:, 30:31], EPSC))
        pl.emit("ve", lambda e: e.reciprocal(TC[0:1, 2:3], TC[0:1, 1:2]))
        pl.wait("te", "ve", pl.sig("ve"))
        pl.emit("te", lambda e: e.matmul(pb7[0:P, 2:3], ONES_R[0:1, 0:128],
                                         TC[0:1, 2:3], start=True, stop=True))
        pl.emit("te", lambda e: e.matmul(pb7[0:P, 3:4], ONES_R[0:1, 0:128],
                                         TC[0:1, 1:2], start=False, stop=True,
                                         skip_group_check=True))
        pl.wait("sc", "te", pl.sig("te"))
        pl.emit("sc", lambda e: e.activation(RWC, pb7[0:P, 2:3], AF.Identity))
        pl.emit("sc", lambda e: e.activation(SWC, pb7[0:P, 3:4], AF.Identity))
        v_rwc = pl.sig("sc")
        pl.wait("ve", "sc", v_rwc)
        pl.wait("gp", "sc", v_rwc)
        bank_free[7] = ("sc", v_rwc)

        for rb in range(NKC):
            if rb >= 1:
                pl.wait("sy", "gp_c", q_sig[rb - 1])
            vl = pl.dma("sy", "sy_d", WSTG, w_ffn[rb * P:(rb + 1) * P, :])
            pl.wait("gp", "sy_d", vl)
            if rb >= 1:
                pl.wait("gp", "te", st_sig[rb - 1][0])
            st = WSTG
            pl.emit("gp", lambda e, _s=st: e.tensor_scalar_mul(_s, _s, RWC))
            pl.emit("gp", lambda e, _s=st: e.tensor_copy(_s.bitcast(I32), _s))
            pl.emit("gp", lambda e, _s=st: e.tensor_copy(_s, _s.bitcast(I32)))
            pl.emit("gp", lambda e, _s=st: e.tensor_scalar(
                WQR, _s, 1.0, -1.0, ALU.min, ALU.max))
            q_sig[rb] = pl.sig("gp_c")
            pl.wait("te", "gp_c", q_sig[rb])
            if rb >= 1:
                pl.wait("te", "sy_d", st_sig[rb - 1][1])
            last_te = 0
            v_e = 0
            for half in range(4):
                bk = 4 + (half % 2)
                bank_wait("te", bk)
                pbk = banks[bk][:].bitcast(BF16)
                for k in range(8):
                    kc = half * 8 + k
                    pl.emit("te", lambda e, _p=pbk, _k=k, _kc=kc: e.transpose(
                        _p[:, _k * P:(_k + 1) * P],
                        WQR[:, _kc * P:(_kc + 1) * P], ID_BF))
                v_t = pl.sig("te")
                last_te = v_t
                pl.wait("sc", "te", v_t)
                pl.emit("sc", lambda e, _p=pbk, _h=half: e.activation(
                    TSG[:, _h * 1024:(_h + 1) * 1024], _p, AF.Identity))
                v_e = pl.sig("sc")
                bank_free[bk] = ("sc", v_e)
            pl.wait("sy", "sc", v_e)
            dst = wqt_hbm[:, rb * P:(rb + 1) * P].rearrange(
                "(kc p) o -> p kc o", p=P)
            v_st = pl.dma("sy", "sy_d", dst,
                          TSG[:].rearrange("p (kc o) -> p kc o", kc=NKC))
            st_sig[rb] = (last_te, v_st)
        v_prep_te = pl.cnt["te"]
        v_prep_store = pl.cnt["sy_d"]
        pl.emit("gp", lambda e: e.memset(h_st[:], 0.0))
        v_hmem = pl.sig("gp_c")

        def tm_to_fm(src_tm, tt):
            last = 0
            last_te = 0
            for half in range(4):
                bk = 4 + (half % 2)
                bank_wait("te", bk)
                pbk = banks[bk][:].bitcast(BF16)
                for k in range(8):
                    kc = half * 8 + k
                    pl.emit("te", lambda e, _p=pbk, _k=k, _kc=kc, _s=src_tm:
                            e.transpose(_p[:, _k * P:(_k + 1) * P],
                                        _s[:, _kc * P:(_kc + 1) * P], ID_BF))
                v_t = pl.sig("te")
                last_te = v_t
                pl.wait("sc", "te", v_t)
                dstv = xrq[:].rearrange("p (kc t) -> p kc t", kc=NKC)[
                    :, half * 8:(half + 1) * 8, tt * P:(tt + 1) * P]
                pl.emit("sc", lambda e, _p=pbk, _d=dstv: e.activation(
                    _d, _p, AF.Identity))
                v_e = pl.sig("sc")
                bank_free[bk] = ("sc", v_e)
                last = v_e
            return last, last_te

        XA = w2f32(0, 4096)
        XRT = wqbf(0, 4096)
        wbufs = [w1a, w2bf(12288, 4096)]
        w_last = {0: 0, 1: 0}
        h_upd_last = 0
        v_f2_te_last = 0
        v_f2_ve_last = 0
        v_f2_gpd_last = 0
        res_slot = {0: 0, 1: 0}
        a_done_prev = 0
        blk_te = 0
        blk_ve = 0

        for ch in range(NCH):
            t0 = ch * TCH

            a_done = {}
            v_xrt_prev = 0
            v_tm_te_prev = 0
            pl.wait("sc", "te", v_f2_te_last)
            pl.wait("sc", "ve", v_f2_ve_last)
            for tt in range(NTT):
                xa = XA
                xrt = XRT
                vl = pl.dma("sc", "sc_d", xa,
                            x_hbm[t0 + tt * P: t0 + (tt + 1) * P, :])
                pl.wait("ve", "sc_d", vl)
                pl.wait("ve", "sc", v_xrt_prev)
                for sl in range(8):
                    pl.emit("ve", lambda e, _x=xa, _s=sl: e.bn_stats(
                        BNO[:, _s * 6:(_s + 1) * 6], _x[:, _s * 512:(_s + 1) * 512]))
                pl.emit("ve", lambda e: e.bn_aggr(TC[:, 4:6], BNO))
                pl.emit("ve", lambda e, _x=xa: e.tensor_reduce(
                    TC[:, 7:8], _x, AX.X, ALU.max))
                pl.emit("ve", lambda e: e.tensor_tensor(
                    TC[:, 6:7], TC[:, 5:6], EPSC, ALU.add))
                pl.emit("ve", lambda e, _x=xa: e.tensor_reduce(
                    TC[:, 8:9], _x, AX.X, ALU.min))
                pl.emit("ve", lambda e: e.reciprocal(TC[:, 14:15], TC[:, 6:7]))
                vsp()
                pl.emit("ve", lambda e: e.tensor_sub(TC[:, 10:11], TC[:, 7:8], TC[:, 4:5]))
                vsp()
                pl.emit("ve", lambda e: e.tensor_sub(TC[:, 11:12], TC[:, 4:5], TC[:, 8:9]))
                vsp()
                pl.emit("ve", lambda e: e.tensor_max(TC[:, 10:11], TC[:, 10:11], TC[:, 11:12]))
                vsp()
                pl.emit("ve", lambda e: e.tensor_mul(TC[:, 15:16], TC[:, 5:6], TC[:, 14:15]))
                vsp()
                pl.emit("ve", lambda e: e.tensor_tensor(
                    TC[:, 15:16], TC[:, 15:16], EPSC, ALU.add))
                v_st1 = pl.sig("ve")
                pl.wait("sc", "ve", v_st1)
                pl.emit("sc", lambda e: e.activation(TC[:, 9:10], TC[:, 6:7], AF.Ln))
                pl.emit("sc", lambda e, _c=ch, _t=tt: e.activation(
                    stc(_c, _t, 1), TC[:, 9:10], AF.Exp, scale=-0.5))
                pl.emit("sc", lambda e: e.activation(TC[:, 16:17], TC[:, 15:16], AF.Ln))
                pl.emit("sc", lambda e: e.activation(TC[:, 16:17], TC[:, 16:17],
                                                     AF.Exp, scale=-0.5))
                v_sc1 = pl.sig("sc")
                pl.wait("ve", "sc", v_sc1)
                pl.emit("ve", lambda e, _c=ch, _t=tt: e.tensor_copy(
                    stc(_c, _t, 0), TC[:, 4:5]))
                vsp()
                pl.emit("ve", lambda e, _c=ch, _t=tt: e.scalar_tensor_tensor(
                    TC[:, 12:13], TC[:, 10:11], 1.0, stc(_c, _t, 1),
                    ALU.mult, ALU.mult))
                vsp()
                pl.emit("ve", lambda e: e.tensor_tensor(
                    TC[:, 13:14], TC[:, 12:13], EPSC, ALU.add))
                vsp()
                pl.emit("ve", lambda e, _c=ch, _t=tt: e.tensor_mul(
                    TC[:, 17:18], stc(_c, _t, 1), TC[:, 16:17]))
                vsp()
                pl.emit("ve", lambda e: e.reciprocal(TC[:, 19:20], TC[:, 13:14]))
                vsp()
                pl.emit("ve", lambda e, _c=ch, _t=tt: e.tensor_scalar(
                    stc(_c, _t, 3), TC[:, 13:14], SWC, 1.0 / 128.0,
                    ALU.mult, ALU.mult))
                vsp()
                pl.emit("ve", lambda e, _c=ch, _t=tt: e.tensor_scalar_mul(
                    stc(_c, _t, 2), TC[:, 19:20], 128.0))
                vsp()
                pl.emit("ve", lambda e: e.tensor_mul(TC[:, 18:19], TC[:, 4:5], TC[:, 17:18]))
                vsp()
                pl.emit("ve", lambda e: e.tensor_tensor(
                    TC[:, 18:19], TC[:, 18:19], NEG1C, ALU.mult))
                vsp()
                v_ve3 = pl.sig("ve")
                if "a0" in want and ch == 0 and tt == 0:
                    dump("a0", small[:, 0:512], [("ve", v_ve3)])
                    dump("a0s", stats[:, 0:96], [("ve", v_ve3)])
                pl.wait("sc", "ve", v_ve3)
                pl.wait("sc", "te", v_tm_te_prev)
                pl.emit("sc", lambda e, _x=xa, _r=xrt: e.activation(
                    _r, _x, AF.Identity, scale=TC[:, 17:18], bias=TC[:, 18:19]))
                v_xrt_prev = pl.sig("sc")
                if "xrt0" in want and ch == 0 and tt == 0:
                    dump("xrt0", xrt[:, 0:4096], [("sc", v_xrt_prev)])
                pl.wait("te", "sc", v_xrt_prev)
                a_done[tt], v_tm_te_prev = tm_to_fm(xrt, tt)
            a_done_prev = a_done[NTT - 1]
            w_last[1] = v_tm_te_prev
            pl.wait("ve", "sc", a_done_prev)
            for kc in range(NKC):
                pl.emit("ve", lambda e, _k=kc: e.tensor_scalar_mul(
                    xrq[:, _k * TCH:(_k + 1) * TCH],
                    xrq[:, _k * TCH:(_k + 1) * TCH], LNW[:, _k:_k + 1]))
            v_xr_ready = pl.sig("ve")
            if "xr" in want and ch == 0:
                dump("xr", xrq[:, 0:8192], [("ve", v_xr_ready)])

            CACC = [w2f32(0, 1024), w2f32(4096, 1024)]
            mm_done = {}
            ev_done = {}
            cv_done = {}
            cv_ve = {}
            wl = {}
            pl.wait("te", "ve", v_xr_ready)

            def b_load(jt):
                if ch == 0 and jt == 0:
                    wl[0] = wl0_pre
                    return
                width = P if jt < 34 else 64
                buf = jt % 2
                if jt >= 2:
                    pl.wait("gp", "te", mm_done[jt - 2])
                else:
                    pl.wait("gp", "te", w_last[buf])
                src = w_in[:, DS + jt * P: DS + jt * P + width].rearrange(
                    "(kc p) j -> p kc j", p=P)
                dstw = wbufs[buf][:, 0:NKC * width].rearrange(
                    "p (kc j) -> p kc j", kc=NKC)
                wl[jt] = pl.dma("gp", "gp_d", dstw, src)

            b_load(0)
            b_load(1)
            for jt in range(NJ1):
                width = P if jt < 34 else 64
                wb = wbufs[jt % 2]
                pl.wait("te", "gp_d", wl[jt])
                for tg in range(2):
                    bk = (2 * jt + tg) % 6
                    bank_wait("te", bk)
                    outp = banks[bk][0:width, 0:512]
                    for kc in range(NKC):
                        pl.emit("te", lambda e, _o=outp, _w=wb, _kc=kc, _tg=tg,
                                _wd=width: e.matmul(
                            _o, _w[:, _kc * _wd:(_kc + 1) * _wd],
                            xrq[:, _kc * TCH + _tg * 512: _kc * TCH + (_tg + 1) * 512],
                            start=(_kc == 0), stop=(_kc == NKC - 1)))
                    v_mm = pl.sig("te")
                    if tg == 1:
                        mm_done[jt] = v_mm
                        w_last[jt % 2] = v_mm
                    pl.wait("sc", "te", v_mm)
                    if jt < 34:
                        pl.emit("sc", lambda e, _b=banks[bk], _jt=jt, _tg=tg:
                                e.activation(cv(_jt, 3 + _tg * 512, 512),
                                             _b[:, 0:512], AF.Identity,
                                             scale=MUP[:, 32 + _jt:33 + _jt]))
                    else:
                        pl.wait("sc", "te", v_prep_te)
                        dtr_t = w2f32(8192, 1024)
                        pl.emit("sc", lambda e, _b=banks[bk], _tg=tg, _d=dtr_t:
                                e.activation(
                                    _d[0:NH, _tg * 512:(_tg + 1) * 512],
                                    _b[0:NH, 0:512], AF.Identity,
                                    scale=MUP[0:NH, 66:67]))
                    v_ev = pl.sig("sc")
                    bank_free[bk] = ("sc", v_ev)
                    ev_done[(jt, tg)] = v_ev
                if jt + 2 < NJ1:
                    b_load(jt + 2)
                if jt < 34:
                    pl.wait("ve", "sc", ev_done[(jt, 1)])
                    if jt >= 2:
                        pl.wait("ve", "sc", cv_done.get(jt - 2, 0))
                    acc = CACC[jt % 2]
                    pl.emit("ve", lambda e, _j=jt, _a=acc: e.tensor_scalar_mul(
                        _a, cv(_j, 0, TCH), CW4[:, _j * 4:_j * 4 + 1]))
                    for k in range(1, 4):
                        pl.emit("ve", lambda e, _j=jt, _a=acc, _k=k:
                                e.scalar_tensor_tensor(
                                    _a, cv(_j, _k, TCH),
                                    CW4[:, _j * 4 + _k:_j * 4 + _k + 1], _a,
                                    ALU.mult, ALU.add))
                    pl.emit("ve", lambda e, _j=jt: e.tensor_copy(
                        cv(_j, 0, 3), cv(_j, TCH, 3)))
                    cv_ve[jt] = pl.sig("ve")
                lag = jt - 1
                if 0 <= lag < 34:
                    pl.wait("sc", "ve", cv_ve[lag])
                    pl.emit("sc", lambda e, _j=lag, _a=CACC[lag % 2]: e.activation(
                        cv(_j, 3, TCH), _a, AF.Silu))
                    cv_done[lag] = pl.sig("sc")
                if jt == NJ1 - 1:
                    pl.wait("sc", "ve", cv_ve[33])
                    pl.emit("sc", lambda e, _a=CACC[33 % 2]: e.activation(
                        cv(33, 3, TCH), _a, AF.Silu))
                    cv_done[33] = pl.sig("sc")
            DTP = dtb[0:NH, 0:TCH]
            SEG = dtb[0:NH, TCH:2 * TCH]
            DTR = w2f32(8192, 1024)[0:NH, :]
            pl.wait("sc", "gp_d", v_f2_gpd_last)
            pl.wait("ve", "gp_d", v_f2_gpd_last)
            pl.emit("sc", lambda e: e.activation(DTR, DTR, AF.Identity, bias=DTBI[0:NH, :]))
            pl.emit("sc", lambda e: e.activation(SEG, DTR, AF.Abs))
            pl.emit("sc", lambda e: e.activation(SEG, SEG, AF.Exp, scale=-1.0))
            pl.emit("sc", lambda e: e.activation(SEG, SEG, AF.Ln, bias=1.0))
            v_sp = pl.sig("sc")
            pl.wait("ve", "sc", v_sp)
            pl.emit("ve", lambda e: e.tensor_scalar_max(DTP, DTR, 0.0))
            pl.emit("ve", lambda e: e.tensor_add(DTP, DTP, SEG))
            pl.emit("ve", lambda e: e.tensor_mul(
                DTR, DTP, AC[0:NH, :].broadcast_to([NH, TCH])))
            for qb in range(NBLK):
                pl.emit("ve", lambda e, _q=qb: e.tensor_tensor_scan(
                    SEG[:, _q * P:(_q + 1) * P], DTR[:, _q * P:(_q + 1) * P],
                    DTR[:, _q * P:(_q + 1) * P], 0.0, ALU.add, ALU.bypass))
            v_seg = pl.sig("ve")
            if "convout" in want and ch == 0:
                dump("convout", convin[:, 0:8192], [("sc", cv_done[33])])
            if "dt" in want and ch == 0:
                dump("dt", dtb[0:NH, 0:2048], [("ve", v_seg)])

            XT = w2bf(0, 4096)
            E8 = [w2bf(8192, 1024), w2bf(10240, 1024)]
            M8 = E8
            Y2S = [w2f32(12288, 512), w2f32(14336, 512)]
            XW = [w2bf(16384, 512), w2bf(17408, 512)]
            YTM = [w2bf(18432, 512), w2bf(19456, 512)]
            HBF = [wqbf(0, 512), wqbf(1024, 512)]
            CBM = wqbf(2048, 128)
            BT = wqbf(2304, 128)
            SEGT = wqf32(2560, 64)
            DTT = wqf32(2816, 64)
            WST = wqf32(3072, 64)
            ESGT = wqf32(3328, 64)
            ENDR = wqf32(3584, 64)
            B64 = wqf32(3840, 64)
            SGC = [wqf32(4096, 1024), wqf32(8192, 1024)]
            WSB = SPARE[0:NH, 0:128]
            ESGF = SPARE[0:NH, 128:256]

            for qb in range(NBLK):
                tok = 3 + qb * P
                segb = SEG[:, qb * P:(qb + 1) * P]
                dtpb = DTP[:, qb * P:(qb + 1) * P]
                wsb = WSB
                pl.wait("ve", "te", blk_te)
                pl.emit("ve", lambda e, _s=segb, _w=wsb, _q=qb: e.tensor_scalar(
                    _w, _s, SEG[0:NH, (_q + 1) * P - 1:(_q + 1) * P], None,
                    ALU.subtract))
                v_w1 = pl.sig("ve")
                pl.wait("sc", "ve", v_w1)
                pl.wait("sc", "te", blk_te)
                pl.emit("sc", lambda e, _w=wsb: e.activation(_w, _w, AF.Exp, scale=-1.0))
                v_w2s = pl.sig("sc")
                pl.wait("ve", "sc", v_w2s)
                pl.emit("ve", lambda e, _w=wsb, _d=dtpb: e.tensor_mul(_w, _w, _d))
                v_w3 = pl.sig("ve")
                pl.wait("sc", "ve", v_w3)
                pl.emit("sc", lambda e, _s=segb: e.activation(
                    ESGF, _s, AF.Exp))
                v_esg = pl.sig("sc")
                bank_wait("te", 6)
                bank_wait("te", 7)
                pl.wait("te", "sc", v_esg)
                pb7b = pb7[:].bitcast(BF16)
                pl.emit("te", lambda e, _s=segb: e.transpose(
                    pb7[:, 64:128], _s, ID_F32[0:64, 0:64]))
                pl.emit("te", lambda e, _d=dtpb: e.transpose(
                    pb7[:, 128:192], _d, ID_F32[0:64, 0:64]))
                pl.emit("te", lambda e, _w=wsb: e.transpose(
                    pb7[:, 192:256], _w, ID_F32[0:64, 0:64]))
                pl.emit("te", lambda e: e.transpose(
                    pb7[:, 256:320], ESGF, ID_F32[0:64, 0:64]))
                pl.emit("te", lambda e: e.transpose(
                    pb7[0:1, 320:384], ESGF[:, P - 1:P], ID_F32[0:64, 0:64]))
                pl.emit("te", lambda e, _t=tok: e.matmul(
                    pb6[:, 0:128], cv(32, _t, P), cv(33, _t, P),
                    start=True, stop=True))
                v_pre1 = pl.sig("te")
                pl.wait("sc", "te", v_pre1)
                pl.emit("sc", lambda e: e.activation(SEGT, pb7[:, 64:128], AF.Identity))
                pl.emit("sc", lambda e: e.activation(DTT, pb7[:, 128:192], AF.Identity))
                pl.emit("sc", lambda e: e.activation(WST, pb7[:, 192:256], AF.Identity))
                pl.emit("sc", lambda e: e.activation(ESGT, pb7[:, 256:320], AF.Identity))
                pl.emit("sc", lambda e: e.activation(
                    ENDR[0:1, 0:64], pb7[0:1, 320:384], AF.Identity))
                v_pre2 = pl.sig("sc")
                pl.wait("ve", "te", v_pre1)
                pl.emit("ve", lambda e: e.tensor_tensor(CBM, pb6[:, 0:128], LT, ALU.mult))
                v_cbm = pl.sig("ve")
                bank_free[6] = ("ve", v_cbm)
                pl.wait("te", "sc", v_pre2)
                pl.emit("te", lambda e: e.matmul(
                    pb7[:, 384:448], ONES_R[0:1, 0:128], ENDR[0:1, 0:64],
                    start=False, stop=True, skip_group_check=True))
                pl.emit("te", lambda e, _t=tok: e.transpose(
                    pb7b[:, 896:1024], cv(32, _t, P), ID_BF))
                v_pre3 = pl.sig("te")
                pl.wait("sc", "te", v_pre3)
                pl.emit("sc", lambda e: e.activation(B64, pb7[:, 384:448], AF.Identity))
                pl.emit("sc", lambda e: e.activation(BT, pb7b[:, 896:1024], AF.Identity))
                v_pre4 = pl.sig("sc")
                bank_free[7] = ("sc", v_pre4)
                v_xt = 0
                for half in range(4):
                    bk = 4 + (half % 2)
                    bank_wait("te", bk)
                    pbk = banks[bk][:].bitcast(BF16)
                    for k in range(8):
                        ft = half * 8 + k
                        pl.emit("te", lambda e, _p=pbk, _k=k, _f=ft, _t=tok:
                                e.transpose(_p[:, _k * P:(_k + 1) * P],
                                            cv(_f, _t, P), ID_BF))
                    v_t = pl.sig("te")
                    pl.wait("sc", "te", v_t)
                    pl.emit("sc", lambda e, _p=pbk, _h=half: e.activation(
                        XT[:, _h * 1024:(_h + 1) * 1024], _p, AF.Identity))
                    v_xt = pl.sig("sc")
                    bank_free[bk] = ("sc", v_xt)

                relu_sig = {}
                m8_sig = {}
                xw_sig = {}
                hbf_sig = {}
                yfin_sig = {}
                hupd_sig = {}
                ywr_sig = {}
                rmm_sig = {}
                ymm_sig = {}
                stm_sig = {}
                yt_sig = {}
                for g in range(NG):
                    pr = g % 2
                    rbA, rbB = (0, 1) if pr == 0 else (2, 3)
                    y1b = 4 + pr
                    y2b = 6 + pr
                    if g >= 2:
                        pl.wait("sc", "te", rmm_sig[g - 2])
                    seg_rows = dtb[g * 8:(g + 1) * 8, TCH + qb * P: TCH + (qb + 1) * P]
                    pl.wait("sc", "sy_d", v_prep_store)
                    v_sgc = pl.dma(
                        "sc", "sc_d",
                        SGC[pr][0:1, :].rearrange("p (a b) -> p a b", a=8),
                        seg_rows)
                    pl.wait("te", "sc_d", v_sgc)
                    if g >= 2:
                        pl.wait("te", "ve", hupd_sig[g - 2])
                        pl.wait("te", "ve", ywr_sig[g - 2])
                    else:
                        bank_wait("te", rbA)
                        bank_wait("te", rbB)
                    pl.emit("te", lambda e, _b=banks[rbA], _p=pr: e.matmul(
                        _b[:, 0:512], ONES_R[0:1, 0:128], SGC[_p][0:1, 0:512],
                        start=True, stop=True))
                    pl.emit("te", lambda e, _b=banks[rbB], _p=pr: e.matmul(
                        _b[:, 0:512], ONES_R[0:1, 0:128], SGC[_p][0:1, 512:1024],
                        start=True, stop=True))
                    rmm_sig[g] = pl.sig("te")
                    pl.wait("sc", "te", rmm_sig[g])
                    if g >= 2:
                        pl.wait("sc", "ve", m8_sig[g - 2])
                    for rl in range(8):
                        bsel = banks[rbA] if rl < 4 else banks[rbB]
                        pl.emit("sc", lambda e, _b=bsel, _r=rl, _g=g:
                                e.activation(
                                    _b[:, (_r % 4) * P:((_r % 4) + 1) * P],
                                    _b[:, (_r % 4) * P:((_r % 4) + 1) * P],
                                    AF.Relu, scale=-1.0,
                                    bias=SEGT[:, _g * 8 + _r:_g * 8 + _r + 1]))
                    pl.emit("sc", lambda e, _p=pr, _bA=banks[rbA]: e.activation(
                        E8[_p][:, 0:512], _bA[:, 0:512], AF.Exp, scale=-1.0))
                    pl.emit("sc", lambda e, _p=pr, _bB=banks[rbB]: e.activation(
                        E8[_p][:, 512:1024], _bB[:, 0:512], AF.Exp, scale=-1.0))
                    relu_sig[g] = pl.sig("sc")
                    pl.wait("ve", "sc", relu_sig[g])
                    pl.emit("ve", lambda e, _p=pr: e.tensor_tensor(
                        E8[_p], E8[_p],
                        CBM.unsqueeze(1).broadcast_to([P, 8, P]), ALU.mult))
                    pl.emit("ve", lambda e, _p=pr, _g=g: e.tensor_tensor(
                        M8[_p], E8[_p],
                        DTT[:, _g * 8:(_g + 1) * 8].unsqueeze(2)
                        .broadcast_to([P, 8, P]), ALU.mult))
                    pl.emit("ve", lambda e, _p=pr, _g=g: e.tensor_tensor(
                        XW[_p], XT[:, _g * 512:(_g + 1) * 512],
                        WST[:, _g * 8:(_g + 1) * 8].unsqueeze(2)
                        .broadcast_to([P, 8, HD]), ALU.mult))
                    m8_sig[g] = xw_sig[g] = pl.sig("ve")
                    if g >= 1:
                        pl.wait("sc", "ve", hupd_sig[g - 1])
                    else:
                        pl.wait("sc", "ve", h_upd_last)
                        pl.wait("sc", "gp_c", v_hmem)
                    pl.emit("sc", lambda e, _p=pr, _g=g: e.activation(
                        HBF[_p], h_st[:, _g * 512:(_g + 1) * 512], AF.Identity))
                    hbf_sig[g] = pl.sig("sc")
                    pl.wait("te", "ve", m8_sig[g])
                    pl.wait("te", "sc", hbf_sig[g])
                    if g >= 2:
                        pl.wait("te", "ve", yfin_sig[g - 2])
                    for rl in range(8):
                        pl.emit("te", lambda e, _b=banks[y1b], _p=pr, _r=rl, _g=g:
                                e.matmul(_b[:, _r * HD:(_r + 1) * HD],
                                         M8[_p][:, _r * P:(_r + 1) * P],
                                         XT[:, (_g * 8 + _r) * HD:(_g * 8 + _r + 1) * HD],
                                         start=(_r == 0), stop=(_r == 7),
                                         skip_group_check=True))
                    for rl in range(8):
                        pl.emit("te", lambda e, _b=banks[y2b], _p=pr, _r=rl, _t=tok:
                                e.matmul(_b[:, _r * HD:(_r + 1) * HD],
                                         cv(33, _t, P),
                                         HBF[_p][:, _r * HD:(_r + 1) * HD],
                                         start=(_r == 0), stop=(_r == 7),
                                         skip_group_check=True))
                    ymm_sig[g] = pl.sig("te")
                    pl.wait("te", "ve", xw_sig[g])
                    for rl in range(8):
                        pl.emit("te", lambda e, _b=banks[rbA], _p=pr, _r=rl:
                                e.matmul(_b[:, _r * HD:(_r + 1) * HD],
                                         BT, XW[_p][:, _r * HD:(_r + 1) * HD],
                                         start=(_r == 0), stop=(_r == 7),
                                         skip_group_check=True))
                    stm_sig[g] = pl.sig("te")
                    pl.wait("ve", "te", ymm_sig[g])
                    pl.emit("ve", lambda e, _p=pr, _g=g, _b=banks[y2b]:
                            e.tensor_tensor(Y2S[_p], _b[:, 0:512],
                                            ESGT[:, _g * 8:(_g + 1) * 8]
                                            .unsqueeze(2).broadcast_to([P, 8, HD]),
                                            ALU.mult))
                    pl.emit("ve", lambda e, _p=pr, _b=banks[y1b]:
                            e.tensor_tensor(YTM[_p], _b[:, 0:512], Y2S[_p], ALU.add))
                    yfin_sig[g] = pl.sig("ve")
                    pl.wait("ve", "te", stm_sig[g])
                    pl.emit("ve", lambda e, _p=pr, _g=g: e.tensor_tensor(
                        Y2S[_p], h_st[:, _g * 512:(_g + 1) * 512],
                        B64[:, _g * 8:(_g + 1) * 8].unsqueeze(2)
                        .broadcast_to([P, 8, HD]), ALU.mult))
                    pl.emit("ve", lambda e, _p=pr, _g=g, _b=banks[rbA]:
                            e.tensor_tensor(h_st[:, _g * 512:(_g + 1) * 512],
                                            Y2S[_p], _b[:, 0:512], ALU.add))
                    pl.wait("te", "ve", yfin_sig[g])
                    rbB_bf = banks[rbB][:].bitcast(BF16)
                    for k in range(4):
                        pl.emit("te", lambda e, _b=rbB_bf, _p=pr, _k=k:
                                e.transpose(_b[:, 512 + _k * P: 512 + (_k + 1) * P],
                                            YTM[_p][:, _k * P:(_k + 1) * P], ID_BF))
                    yt_sig[g] = pl.sig("te")
                    pl.wait("ve", "te", yt_sig[g])
                    for k in range(4):
                        ft = g * 4 + k
                        pl.emit("ve", lambda e, _b=rbB_bf, _k=k, _f=ft, _t=tok:
                                e.scalar_tensor_tensor(
                                    cv(_f, _t, P), cv(_f, _t, P),
                                    DFM[:, _f:_f + 1],
                                    _b[:, 512 + _k * P: 512 + (_k + 1) * P],
                                    ALU.mult, ALU.add))
                    hupd_sig[g] = ywr_sig[g] = pl.sig("ve")
                h_upd_last = hupd_sig[NG - 1]
                blk_te = yt_sig[NG - 1]
                blk_ve = ywr_sig[NG - 1]
                bank_free[0] = ("ve", hupd_sig[NG - 2])
                bank_free[1] = ("ve", ywr_sig[NG - 2])
                bank_free[2] = ("ve", hupd_sig[NG - 1])
                bank_free[3] = ("ve", ywr_sig[NG - 1])
                bank_free[4] = ("ve", yfin_sig[NG - 2])
                bank_free[5] = ("ve", yfin_sig[NG - 1])
                bank_free[6] = ("ve", yfin_sig[NG - 2])
                bank_free[7] = ("ve", yfin_sig[NG - 1])
            if "y" in want and ch == 0:
                dump("y", convin[:, 0:8192], [("ve", blk_ve)])

            SZ = [w2bf(0, 512), w2bf(1024, 512)]
            GY2 = [w2bf(2048, 512), w2bf(3072, 512)]
            SRMS = w2f32(4096, 1024)
            ez_mm = {}
            gy_sig = {}
            g2_sig = {}
            sqs = {}
            ezl = {}
            def e_load(jt):
                buf = jt % 2
                if jt >= 2:
                    pl.wait("gp", "te", ez_mm[jt - 2])
                elif jt == 1:
                    pl.wait("gp", "te", blk_te)
                    pl.wait("gp", "ve", blk_ve)
                else:
                    pl.wait("gp", "te", w_last[buf])
                src = w_in[:, jt * P:(jt + 1) * P].rearrange("(kc p) j -> p kc j", p=P)
                dstw = wbufs[buf][:].rearrange("p (kc j) -> p kc j", kc=NKC)
                ezl[jt] = pl.dma("gp", "gp_d", dstw, src)

            e_load(0)
            e_load(1)
            for jt in range(NJ2):
                wb = wbufs[jt % 2]
                pl.wait("te", "gp_d", ezl[jt])
                pl.wait("te", "ve", blk_ve)
                for tg in range(2):
                    bk = (2 * jt + tg) % 4
                    bank_wait("te", bk)
                    for kc in range(NKC):
                        pl.emit("te", lambda e, _b=banks[bk], _w=wb, _kc=kc, _tg=tg:
                                e.matmul(_b[:, 0:512],
                                         _w[:, _kc * P:(_kc + 1) * P],
                                         xrq[:, _kc * TCH + _tg * 512: _kc * TCH + (_tg + 1) * 512],
                                         start=(_kc == 0), stop=(_kc == NKC - 1)))
                    v_mm = pl.sig("te")
                    if tg == 1:
                        ez_mm[jt] = v_mm
                        w_last[jt % 2] = v_mm
                        if jt + 2 < NJ2:
                            e_load(jt + 2)
                    pl.wait("sc", "te", v_mm)
                    pl.wait("sc", "ve", gy_sig.get((jt - 1, tg), 0))
                    pl.emit("sc", lambda e, _b=banks[bk], _jt=jt, _tg=tg:
                            e.activation(SZ[_tg], _b[:, 0:512], AF.Silu,
                                         scale=MUP[:, _jt:_jt + 1]))
                    v_sz = pl.sig("sc")
                    bank_free[bk] = ("sc", v_sz)
                    pl.wait("ve", "sc", v_sz)
                    pl.emit("ve", lambda e, _jt=jt, _tg=tg: e.tensor_tensor(
                        cv(_jt, 3 + _tg * 512, 512), cv(_jt, 3 + _tg * 512, 512),
                        SZ[_tg], ALU.mult))
                    gy_sig[(jt, tg)] = pl.sig("ve")
                    pl.wait("sc", "ve", gy_sig[(jt, tg)])
                    pl.wait("sc", "te", sqs.get((jt - 1, tg), 0))
                    pl.emit("sc", lambda e, _jt=jt, _tg=tg: e.activation(
                        GY2[_tg], cv(_jt, 3 + _tg * 512, 512), AF.Square))
                    g2_sig[(jt, tg)] = pl.sig("sc")
                    pl.wait("te", "sc", g2_sig[(jt, tg)])
                    sbk = 4 + tg
                    if jt == 0:
                        bank_wait("te", sbk)
                    pl.emit("te", lambda e, _b=banks[sbk], _tg=tg, _jt=jt:
                            e.matmul(_b[0:1, 0:512], ONES_BC, GY2[_tg],
                                     start=(_jt == 0), stop=(_jt == NJ2 - 1),
                                     skip_group_check=True))
                    sqs[(jt, tg)] = pl.sig("te")
                    pl.wait("ve", "sc", g2_sig[(jt, tg)])
                    pl.emit("ve", lambda e, _jt=jt, _tg=tg: e.tensor_scalar_mul(
                        cv(_jt, 3 + _tg * 512, 512), cv(_jt, 3 + _tg * 512, 512),
                        NRMW[:, _jt:_jt + 1]))
            pl.wait("sc", "te", sqs[(NJ2 - 1, 1)])
            pl.emit("sc", lambda e: e.activation(SRMS[0:1, 0:512], pb4[0:1, 0:512], AF.Identity))
            pl.emit("sc", lambda e: e.activation(SRMS[0:1, 512:1024], pb5[0:1, 0:512], AF.Identity))
            v_sr = pl.sig("sc")
            bank_free[4] = ("sc", v_sr)
            bank_free[5] = ("sc", v_sr)
            pl.wait("ve", "sc", v_sr)
            pl.emit("ve", lambda e: e.tensor_scalar(
                SRMS[0:1, 0:1024], SRMS[0:1, 0:1024], 1.0 / DS, EPS,
                ALU.mult, ALU.add))
            v_sr2 = pl.sig("ve")
            pl.wait("sc", "ve", v_sr2)
            pl.emit("sc", lambda e: e.activation(SRMS[0:1, 0:1024], SRMS[0:1, 0:1024], AF.Ln))
            pl.emit("sc", lambda e: e.activation(SRMS[0:1, 0:1024], SRMS[0:1, 0:1024],
                                                 AF.Exp, scale=-0.5))
            v_sr3 = pl.sig("sc")
            pl.wait("te", "sc", v_sr3)
            bank_wait("te", 6)
            for tt in range(NTT):
                pl.emit("te", lambda e, _t=tt: e.matmul(
                    pb6[:, _t:_t + 1], SRMS[0:1, _t * P:(_t + 1) * P],
                    ONES_R[0:1, 0:1], start=(_t == 0), stop=(_t == NTT - 1),
                    skip_group_check=True))
            v_scol = pl.sig("te")
            pl.wait("sc", "te", v_scol)
            for tt in range(NTT):
                pl.emit("sc", lambda e, _t=tt, _c=ch: e.activation(
                    stc(_c, _t, 4), pb6[:, _t:_t + 1], AF.Identity))
            v_scols = pl.sig("sc")
            bank_free[6] = ("sc", v_scols)
            if "gy" in want and ch == 0:
                dump("gy", convin[:, 0:8192], [("sc", v_scols)])
            if "srms" in want and ch == 0:
                dump("srms", stats[:, 0:96], [("sc", v_scols)])

            XNQ = w2f32(0, 4096)
            XQT = wqbf(0, 4096)
            f1_done = 0
            v_f1b_prev = 0
            v_f1c_prev = 0
            v_f1_te_tm = 0
            pl.wait("sy", "sc", v_scols)
            for tt in range(NTT):
                pl.wait("sy", "ve", v_f1c_prev)
                vl = pl.dma("sy", "sy_d", XNQ,
                            x_hbm[t0 + tt * P: t0 + (tt + 1) * P, :])
                pl.wait("ve", "sy_d", vl)
                pl.wait("ve", "sc", v_f1b_prev)
                pl.emit("ve", lambda e, _c=ch, _t=tt: e.tensor_mul(
                    TC[:, 20:21], stc(_c, _t, 1), stc(_c, _t, 2)))
                vsp()
                pl.emit("ve", lambda e, _c=ch, _t=tt: e.tensor_mul(
                    TC[:, 21:22], stc(_c, _t, 0), TC[:, 20:21]))
                vsp()
                pl.emit("ve", lambda e: e.tensor_tensor(
                    TC[:, 21:22], TC[:, 21:22], NEG1C, ALU.mult))
                vsp()
                v_f1a = pl.sig("ve")
                pl.wait("sc", "ve", v_f1a)
                pl.emit("sc", lambda e: e.activation(
                    XNQ, XNQ, AF.Identity, scale=TC[:, 20:21], bias=TC[:, 21:22]))
                v_f1b_prev = pl.sig("sc")
                pl.wait("ve", "sc", v_f1b_prev)
                pl.wait("ve", "te", v_f1_te_tm)
                pl.emit("ve", lambda e: e.tensor_copy(XNQ.bitcast(I32), XNQ))
                pl.emit("ve", lambda e: e.tensor_copy(XNQ, XNQ.bitcast(I32)))
                pl.emit("ve", lambda e: e.tensor_scalar(
                    XQT[:, 0:4096], XNQ, QBHI, -QBHI, ALU.min, ALU.max))
                v_f1c_prev = pl.sig("ve")
                pl.wait("te", "ve", v_f1c_prev)
                f1_done, v_f1_te_tm = tm_to_fm(XQT[:, 0:4096], tt)
            if "xq" in want and ch == 0:
                dump("xq", xrq[:, 0:8192], [("sc", f1_done)])
            v_f1_te = pl.cnt["te"]
            v_f1_ve = pl.cnt["ve"]
            v_f1_sc = f1_done

            WOBUF = w2bf(0, 8192)
            WQB = wqbf(0, 8192)
            GBV = SPARE[:].bitcast(BF16)
            GB = [GBV[:, 0:256], GBV[:, 256:512]]
            ffn_mm = {}
            wo_used = {}
            gb_busy = {0: 0, 1: 0}
            for ob in range(NOB):
                if ob >= 1:
                    pl.wait("sy", "te", ffn_mm[ob - 1])
                else:
                    pl.wait("sy", "sc", v_f1_sc)
                src = wqt_hbm[:, ob * OB:(ob + 1) * OB].rearrange(
                    "(kc p) o -> p kc o", p=P)
                v_wq = pl.dma("sy", "sy_d",
                              WQB[:].rearrange("p (kc o) -> p kc o", kc=NKC), src)
                if ob >= 1:
                    pl.wait("gp", "te", wo_used[ob - 1])
                else:
                    pl.wait("gp", "ve", v_f1_ve)
                    pl.wait("gp", "te", v_f1_te)
                srco = w_out[:, ob * OB:(ob + 1) * OB].rearrange(
                    "(kc p) o -> p kc o", p=P)
                v_wo = pl.dma("gp", "gp_d",
                              WOBUF[:].rearrange("p (kc o) -> p kc o", kc=NKC),
                              srco)
                if "wqb" in want and ch == 0 and ob == 0:
                    dump("wqb", WQB[:, 0:8192], [("sy_d", v_wq)])
                    dump("wqt1", wqt_hbm[P:2 * P, 0:4096], [("sy_d", v_wq)])
                pl.wait("te", "sy_d", v_wq)
                for tt in range(NTT):
                    bk = 2 + (tt % 2)
                    bank_wait("te", bk)
                    for kc in range(NKC):
                        pl.emit("te", lambda e, _b=banks[bk], _kc=kc, _t=tt:
                                e.matmul(_b[:, 0:OB],
                                         xrq[:, _kc * TCH + _t * P: _kc * TCH + (_t + 1) * P],
                                         WQB[:, _kc * OB:(_kc + 1) * OB],
                                         start=(_kc == 0), stop=(_kc == NKC - 1)))
                    v_fmm = pl.sig("te")
                    if tt == NTT - 1:
                        ffn_mm[ob] = v_fmm
                    pl.wait("sc", "te", v_fmm)
                    if "psb00" in want and ch == 0 and ob == 0 and tt == 0:
                        pl.emit("sc", lambda e, _b=banks[bk]: e.activation(
                            w2f32(16384, 256), _b[:, 0:OB], AF.Identity))
                        dump("psb00", w2f32(16384, 256), [("sc", pl.sig("sc"))])
                    pl.wait("sc", "ve", gb_busy[tt % 2])
                    pl.emit("sc", lambda e, _b=banks[bk], _t=tt, _c=ch, _g=GB[tt % 2]:
                            e.activation(_g, _b[:, 0:OB], AF.Silu,
                                         scale=stc(_c, _t, 3)))
                    v_gb = pl.sig("sc")
                    bank_free[bk] = ("sc", v_gb)
                    pl.wait("te", "gp_d", v_wo)
                    bko = tt % 2
                    bank_wait("te", bko)
                    for kc in range(NKC):
                        pl.emit("te", lambda e, _b=banks[bko], _kc=kc, _t=tt:
                                e.matmul(_b[:, 0:OB],
                                         cv(_kc, 3 + _t * P, P),
                                         WOBUF[:, _kc * OB:(_kc + 1) * OB],
                                         start=(_kc == 0), stop=(_kc == NKC - 1)))
                    v_omm = pl.sig("te")
                    if tt == NTT - 1:
                        wo_used[ob] = v_omm
                    pl.wait("ve", "te", v_omm)
                    pl.wait("ve", "sc", v_gb)
                    pl.wait("ve", "gp_d", res_slot[tt % 2])
                    rslot = resb[:, (tt % 2) * OB:(tt % 2 + 1) * OB]
                    if "only_f" in want:
                        pl.emit("ve", lambda e, _b=banks[bk], _t=tt, _c=ch,
                                _r=rslot: e.scalar_tensor_tensor(
                            _r, _b[:, 0:OB], stc(_c, _t, 3),
                            ONES_CF.broadcast_to([P, OB]), ALU.mult, ALU.mult))
                    elif "only_s" in want:
                        pl.emit("ve", lambda e, _b=banks[bko], _t=tt, _c=ch,
                                _r=rslot: e.scalar_tensor_tensor(
                            _r, _b[:, 0:OB], stc(_c, _t, 4),
                            ONES_CF.broadcast_to([P, OB]), ALU.mult, ALU.mult))
                    elif "only_g" in want:
                        pl.emit("ve", lambda e, _b=banks[bko], _t=tt, _c=ch,
                                _g=GB[tt % 2], _r=rslot: e.scalar_tensor_tensor(
                            _r, _b[:, 0:OB], 0.0, _g, ALU.mult, ALU.add))
                    else:
                        pl.emit("ve", lambda e, _b=banks[bko], _t=tt, _c=ch,
                                _g=GB[tt % 2], _r=rslot: e.scalar_tensor_tensor(
                            _r, _b[:, 0:OB], stc(_c, _t, 4), _g, ALU.mult, ALU.mult))
                    v_res = pl.sig("ve")
                    gb_busy[tt % 2] = v_res
                    bank_free[bko] = ("ve", v_res)
                    pl.wait("gp", "ve", v_res)
                    pl.wait("gp", "sc_d", v_prefill)
                    dstout = out_hbm[t0 + tt * P: t0 + (tt + 1) * P,
                                     ob * OB:(ob + 1) * OB]
                    res_slot[tt % 2] = pl.dma("gp", "gp_d", dstout, rslot,
                                              accum_op=ALU.add)
            v_f2_te_last = pl.cnt["te"]
            v_f2_ve_last = pl.cnt["ve"]
            v_f2_gpd_last = pl.cnt["gp_d"]

        pl.emit("gp", lambda e: e.wait_ge(s_gpd, pl.cnt["gp_d"]))
        pl.emit("sy", lambda e: e.wait_ge(s_syd, pl.cnt["sy_d"]))
        pl.emit("sc", lambda e: e.wait_ge(s_scd, pl.cnt["sc_d"]))

        for s in sems.values():
            nc.sync.sem_clear(s)
        nc.all_engine_barrier()
        with nc.Block() as block:
            @block.sync
            def _(e):
                for fn in pl.ops["sy"]:
                    fn(e)

            @block.gpsimd
            def _(e):
                for fn in pl.ops["gp"]:
                    fn(e)

            @block.tensor
            def _(e):
                for fn in pl.ops["te"]:
                    fn(e)

            @block.scalar
            def _(e):
                for fn in pl.ops["sc"]:
                    fn(e)

            @block.vector
            def _(e):
                for fn in pl.ops["ve"]:
                    fn(e)

    return nc, dbg_names


def _consts():
    c = {}
    c["id_bf"] = np.eye(P, dtype=ml_dtypes.bfloat16)
    c["id_f32"] = np.ascontiguousarray(np.eye(P, dtype=np.float32)[:, 0:64])
    c["ones_r"] = np.ones((1, P), np.float32)
    c["ones_cf"] = np.ones((P, 1), np.float32)
    c["ones_bc"] = np.ones((P, 1), ml_dtypes.bfloat16)
    c["epsc"] = np.full((P, 1), 1e-5, np.float32)
    c["neg1c"] = np.full((P, 1), -1.0, np.float32)
    lt = (np.arange(P)[:, None] <= np.arange(P)[None, :])
    c["lt_bf"] = lt.astype(ml_dtypes.bfloat16)
    return c


def layout_params(ins):
    d = {}
    d["w_in"] = np.ascontiguousarray(ins["L0_attn_in_proj_w"], np.float32)
    d["w_out"] = np.ascontiguousarray(ins["L0_attn_out_proj_w"], np.float32)
    d["w_ffn"] = np.ascontiguousarray(ins["L0_ffn_weight"], np.float32)
    d["ln_w_fm"] = np.ascontiguousarray(
        np.asarray(ins["L0_attn_ln_w"], np.float32).reshape(NKC, P).T)
    d["norm_w_fm"] = np.ascontiguousarray(
        np.asarray(ins["L0_attn_norm_w"], np.float32).reshape(NKC, P).T)
    mup = np.zeros(67 * P, np.float32)
    mup[:PROJ] = np.asarray(ins["L0_attn_mup_vector"], np.float32)
    d["mup_fm"] = np.ascontiguousarray(mup.reshape(67, P).T)
    d["conv_w_fm"] = np.ascontiguousarray(
        np.asarray(ins["L0_attn_conv_w"], np.float32)
        .reshape(NCV, P, 4).transpose(1, 0, 2).reshape(P, NCV * 4))
    d["d_fm"] = np.ascontiguousarray(
        np.repeat(np.asarray(ins["L0_attn_D"], np.float32), HD)
        .reshape(DS // P, P).T)
    d["dt_bias_c"] = np.ascontiguousarray(
        np.asarray(ins["L0_attn_dt_bias"], np.float32).reshape(NH, 1))
    d["a_log_c"] = np.ascontiguousarray(
        np.asarray(ins["L0_attn_A_log"], np.float32).reshape(NH, 1))
    d.update(_consts())
    return d


def kernel(**inputs):
    x_full = np.asarray(inputs["L0_attn_hidden_states"], np.float32)
    params = layout_params({k: np.asarray(v) for k, v in inputs.items()})
    nc, _ = build_program()
    in_maps = []
    for c in range(8):
        m = dict(params)
        m["x"] = np.ascontiguousarray(x_full[c])
        in_maps.append(m)
    res = run_bass_kernel_spmd(nc, in_maps, core_ids=list(range(8)))
    return np.stack([res.results[c]["out"] for c in range(8)], axis=0)


# version 05
# speedup vs baseline: 1.3013x; 1.3013x over previous
import numpy as np
import ml_dtypes
import concourse.bass as bass
import concourse.mybir as mybir
from concourse.bass_utils import run_bass_kernel_spmd

F32 = mybir.dt.float32
BF16 = mybir.dt.bfloat16
I32 = mybir.dt.int32
AF = mybir.ActivationFunctionType
ALU = mybir.AluOpType
AX = mybir.AxisListType

P = 128
H = 4096
DS = 4096
NH = 64
HD = 64
CONV = 4352
PROJ = 8512
L = 2048
EPS = 1e-5
QBHI = 128.0 - 1e-5

TCH = 1024
NCH = L // TCH
NTT = TCH // P
NCV = CONV // P
NKC = H // P
CVW = 3 + TCH
NJ1 = 35
NJ2 = 32
OB = 256
NOB = H // OB
NG = 8
NBLK = TCH // P


class Plan:
    ENGS = ("sy", "gp", "te", "sc", "ve")

    def __init__(self, sems):
        self.ops = {e: [] for e in self.ENGS}
        self.sems = sems
        self.cnt = {k: 0 for k in sems}
        self.last_wait = {}

    def emit(self, eng, fn):
        self.ops[eng].append(fn)

    def wait(self, eng, sem, val):
        if val is None or val <= 0:
            return
        key = (eng, sem)
        if self.last_wait.get(key, 0) >= val:
            return
        self.last_wait[key] = val
        h = self.sems[sem]
        self.emit(eng, lambda e, _h=h, _v=val: e.wait_ge(_h, _v))

    DRAIN_ENGS = ("ve",)
    ENG_OF = {"gp_c": "gp"}

    def sig(self, eng):
        self.cnt[eng] += 1
        h = self.sems[eng]
        stream = self.ENG_OF.get(eng, eng)
        if eng in self.DRAIN_ENGS:
            self.emit(stream, lambda e, _h=h: e.drain(fusable=False).then_inc(_h, 1))
        else:
            fn = self.ops[stream].pop()
            self.ops[stream].append(lambda e, _f=fn, _h=h: _f(e).then_inc(_h, 1))
        return self.cnt[eng]

    def dma(self, queue, sem, out_ap, in_ap, **kw):
        self.cnt[sem] += 16
        h = self.sems[sem]
        self.emit(queue, lambda e, _o=out_ap, _i=in_ap, _h=h:
                  e.dma_start(_o, _i, **kw).then_inc(_h, 16))
        return self.cnt[sem]


def build_program(n_dbg=0, dump_spec=()):
    nc = bass.Bass("TRN2", target_bir_lowering=False, debug=False)

    x_hbm = nc.dram_tensor("x", [L, H], F32, kind="ExternalInput").ap()
    w_in = nc.dram_tensor("w_in", [H, PROJ], F32, kind="ExternalInput").ap()
    w_out = nc.dram_tensor("w_out", [DS, H], F32, kind="ExternalInput").ap()
    w_ffn = nc.dram_tensor("w_ffn", [H, H], F32, kind="ExternalInput").ap()
    ln_w_fm = nc.dram_tensor("ln_w_fm", [P, NKC], F32, kind="ExternalInput").ap()
    norm_w_fm = nc.dram_tensor("norm_w_fm", [P, NKC], F32, kind="ExternalInput").ap()
    mup_fm = nc.dram_tensor("mup_fm", [P, 67], F32, kind="ExternalInput").ap()
    conv_w_fm = nc.dram_tensor("conv_w_fm", [P, NCV * 4], F32, kind="ExternalInput").ap()
    d_fm = nc.dram_tensor("d_fm", [P, DS // P], F32, kind="ExternalInput").ap()
    dt_bias_c = nc.dram_tensor("dt_bias_c", [NH, 1], F32, kind="ExternalInput").ap()
    a_log_c = nc.dram_tensor("a_log_c", [NH, 1], F32, kind="ExternalInput").ap()
    id_bf = nc.dram_tensor("id_bf", [P, P], BF16, kind="ExternalInput").ap()
    id_f32 = nc.dram_tensor("id_f32", [P, 64], F32, kind="ExternalInput").ap()
    ones_r = nc.dram_tensor("ones_r", [1, P], F32, kind="ExternalInput").ap()
    ones_cf = nc.dram_tensor("ones_cf", [P, 1], F32, kind="ExternalInput").ap()
    ones_bc = nc.dram_tensor("ones_bc", [P, 1], BF16, kind="ExternalInput").ap()
    lt_bf = nc.dram_tensor("lt_bf", [P, P], BF16, kind="ExternalInput").ap()
    epsc_in = nc.dram_tensor("epsc", [P, 1], F32, kind="ExternalInput").ap()
    neg1c_in = nc.dram_tensor("neg1c", [P, 1], F32, kind="ExternalInput").ap()
    out_hbm = nc.dram_tensor("out", [L, H], F32, kind="ExternalOutput").ap()
    wqt_hbm = nc.dram_tensor("wqt_scr", [H, H], BF16, kind="Internal").ap()
    dbg = [nc.dram_tensor(f"dbg{i}", [P, 8192], F32, kind="ExternalOutput").ap()
           for i in range(n_dbg)]
    dbg_names = []
    want = set(dump_spec)

    import contextlib
    with contextlib.ExitStack() as _st:
        en = _st.enter_context
        convin = en(nc.sbuf_tensor("convin", [P, NCV * CVW], BF16))
        xrq = en(nc.sbuf_tensor("xrq", [P, NKC * TCH], BF16))
        h_st = en(nc.sbuf_tensor("h_st", [P, NH * HD], F32))
        w1a = en(nc.sbuf_tensor("w1a", [P, NKC * P], BF16))
        w2 = en(nc.sbuf_tensor("w2", [P, 10240], BF16))
        wqa = en(nc.sbuf_tensor("wqa", [P, 8192], BF16))
        dtb = en(nc.sbuf_tensor("dtb", [P, 2 * TCH], F32))
        csts = en(nc.sbuf_tensor("csts", [P, 264], BF16))
        cstf = en(nc.sbuf_tensor("cstf", [P, 363], F32))
        stats = en(nc.sbuf_tensor("stats", [P, 96], F32))
        small = en(nc.sbuf_tensor("small", [P, 512], F32))
        resb = dtb[:, 0:512]
        pb0 = en(nc.psum_tensor("pb0", [P, 512], F32))
        pb1 = en(nc.psum_tensor("pb1", [P, 512], F32))
        pb2 = en(nc.psum_tensor("pb2", [P, 512], F32))
        pb3 = en(nc.psum_tensor("pb3", [P, 512], F32))
        pb4 = en(nc.psum_tensor("pb4", [P, 512], F32))
        pb5 = en(nc.psum_tensor("pb5", [P, 512], F32))
        pb6 = en(nc.psum_tensor("pb6", [P, 512], F32))
        pb7 = en(nc.psum_tensor("pb7", [P, 512], F32))
        s_te = en(nc.semaphore("s_te"))
        s_sc = en(nc.semaphore("s_sc"))
        s_ve = en(nc.semaphore("s_ve"))
        s_gpc = en(nc.semaphore("s_gpc"))
        s_syd = en(nc.semaphore("s_syd"))
        s_gpd = en(nc.semaphore("s_gpd"))
        s_scd = en(nc.semaphore("s_scd"))
        sems = {"te": s_te, "sc": s_sc, "ve": s_ve, "gp_c": s_gpc,
                "sy_d": s_syd, "gp_d": s_gpd, "sc_d": s_scd}
        pl = Plan(sems)
        banks = [pb0, pb1, pb2, pb3, pb4, pb5, pb6, pb7]
        bank_free = {i: ("te", 0) for i in range(8)}

        def bank_wait(eng, i):
            sem, val = bank_free[i]
            pl.wait(eng, sem, val)

        ID_BF = csts[:, 0:128]
        LT = csts[:, 128:256]
        ONES_BC = csts[:, 256:257]
        ID_F32 = cstf[:, 0:64]
        ONES_R = cstf[:, 64:192]
        ONES_CF = cstf[:, 192:193]
        LNW = cstf[:, 193:225]
        NRMW = cstf[:, 225:257]
        DFM = cstf[:, 257:289]
        MUP = cstf[:, 289:356]
        DTBI = cstf[:, 356:357]
        ALOG = cstf[:, 357:358]
        AC = cstf[:, 358:359]
        RWC = cstf[:, 359:360]
        SWC = cstf[:, 360:361]
        EPSC = cstf[:, 361:362]
        NEG1C = cstf[:, 362:363]
        CW4 = small[:, 0:136]
        ABSS = small[:, 136:168]
        BNO = small[:, 168:216]
        TC = small[:, 216:248]
        SPARE = small[:, 248:512]

        def stc(ch, tt, k):
            i = (ch * NTT + tt) * 6 + k
            return stats[:, i:i + 1]

        def vsp():
            pl.emit("ve", lambda e: e.tensor_copy(TC[:, 30:31], EPSC))

        def w2f32(off_b, n):
            return w2[:, off_b // 2: off_b // 2 + 2 * n].bitcast(F32)

        def w2bf(off_b, n):
            return w2[:, off_b // 2: off_b // 2 + n]

        def w2i32(off_b, n):
            return w2[:, off_b // 2: off_b // 2 + 2 * n].bitcast(I32)

        def wqf32(off_b, n):
            return wqa[:, off_b // 2: off_b // 2 + 2 * n].bitcast(F32)

        def wqbf(off_b, n):
            return wqa[:, off_b // 2: off_b // 2 + n]

        def cv(ft, c0, n):
            return convin[:, ft * CVW + c0: ft * CVW + c0 + n]

        def dump(tag, ap, waits):
            if len(dbg_names) >= len(dbg):
                return
            i = len(dbg_names)
            dbg_names.append(tag)
            for sem, val in waits:
                pl.wait("gp", sem, val)
            rows = ap.shape[0]
            tot = 1
            for s_ in ap.shape[1:]:
                tot *= s_
            pl.dma("gp", "gp_d", dbg[i][0:rows, 0:tot], ap)

        pl.dma("sy", "sy_d", ID_BF, id_bf)
        pl.dma("sy", "sy_d", LT, lt_bf)
        pl.dma("sy", "sy_d", ONES_BC, ones_bc)
        pl.dma("sy", "sy_d", ID_F32, id_f32)
        pl.dma("sy", "sy_d", ONES_R[0:1, 0:128], ones_r)
        pl.dma("sy", "sy_d", ONES_CF, ones_cf)
        pl.dma("sy", "sy_d", LNW, ln_w_fm)
        pl.dma("sy", "sy_d", NRMW, norm_w_fm)
        pl.dma("sy", "sy_d", DFM, d_fm)
        pl.dma("sy", "sy_d", MUP, mup_fm)
        pl.dma("sy", "sy_d", DTBI[0:NH, :], dt_bias_c)
        pl.dma("sy", "sy_d", CW4, conv_w_fm)
        pl.dma("sy", "sy_d", EPSC, epsc_in)
        pl.dma("sy", "sy_d", NEG1C, neg1c_in)
        v_ld = pl.dma("sy", "sy_d", ALOG[0:NH, :], a_log_c)
        pl.wait("sc", "sy_d", v_ld)
        pl.emit("sc", lambda e: e.activation(AC[0:NH, :], ALOG[0:NH, :], AF.Exp))
        pl.wait("ve", "sc", pl.sig("sc"))
        pl.emit("ve", lambda e: e.tensor_scalar_mul(AC[0:NH, :], AC[0:NH, :], -1.0))
        for ft in range(NCV):
            pl.emit("ve", lambda e, _f=ft: e.memset(cv(_f, 0, 3), 0.0))
        v_init = pl.sig("ve")

        v_prefill = pl.dma("sc", "sc_d", out_hbm, x_hbm)

        wl0_pre = pl.dma("gp", "gp_d",
                         w1a[:, 0:NKC * P].rearrange("p (kc j) -> p kc j", kc=NKC),
                         w_in[:, DS:DS + P].rearrange("(kc p) j -> p kc j", p=P))

        WSTG = h_st[:, 0:4096]
        WQR = dtb[:, 0:2048].bitcast(BF16)
        TSG = wqbf(8192, 4096)
        q_sig = {}
        st_sig = {}
        red_sig = {}
        for rb in range(NKC):
            if rb >= 1:
                pl.wait("sy", "sc", red_sig[rb - 1])
            vl = pl.dma("sy", "sy_d", WSTG, w_ffn[rb * P:(rb + 1) * P, :])
            pl.wait("sc", "sy_d", vl)
            pl.emit("sc", lambda e, _rb=rb: e.activation(
                TSG[:, 0:4096], WSTG, AF.Abs, accum_out=ABSS[:, _rb:_rb + 1]))
            red_sig[rb] = pl.sig("sc")
        pl.wait("ve", "sc", red_sig[NKC - 1])
        pl.emit("ve", lambda e: e.tensor_reduce(TC[:, 0:1], ABSS, AX.X, ALU.add))
        pl.wait("te", "ve", pl.sig("ve"))
        pl.emit("te", lambda e: e.matmul(pb7[0:1, 0:1], ONES_CF, TC[:, 0:1],
                                         start=True, stop=True))
        pl.wait("sc", "te", pl.sig("te"))
        pl.emit("sc", lambda e: e.activation(TC[0:1, 1:2], pb7[0:1, 0:1], AF.Identity,
                                             scale=1.0 / float(H * H)))
        pl.wait("ve", "sc", pl.sig("sc"))
        pl.emit("ve", lambda e: e.tensor_tensor(
            TC[0:1, 1:2], TC[0:1, 1:2], EPSC[0:1, :], ALU.add))
        pl.emit("ve", lambda e: e.tensor_copy(TC[:, 30:31], EPSC))
        pl.emit("ve", lambda e: e.tensor_copy(TC[:, 30:31], EPSC))
        pl.emit("ve", lambda e: e.reciprocal(TC[0:1, 2:3], TC[0:1, 1:2]))
        pl.wait("te", "ve", pl.sig("ve"))
        pl.emit("te", lambda e: e.matmul(pb7[0:P, 2:3], ONES_R[0:1, 0:128],
                                         TC[0:1, 2:3], start=True, stop=True))
        pl.emit("te", lambda e: e.matmul(pb7[0:P, 3:4], ONES_R[0:1, 0:128],
                                         TC[0:1, 1:2], start=False, stop=True,
                                         skip_group_check=True))
        pl.wait("sc", "te", pl.sig("te"))
        pl.emit("sc", lambda e: e.activation(RWC, pb7[0:P, 2:3], AF.Identity))
        pl.emit("sc", lambda e: e.activation(SWC, pb7[0:P, 3:4], AF.Identity))
        v_rwc = pl.sig("sc")
        pl.wait("ve", "sc", v_rwc)
        pl.wait("gp", "sc", v_rwc)
        bank_free[7] = ("sc", v_rwc)

        for rb in range(NKC):
            if rb >= 1:
                pl.wait("sy", "ve", q_sig[rb - 1])
            vl = pl.dma("sy", "sy_d", WSTG, w_ffn[rb * P:(rb + 1) * P, :])
            pl.wait("ve", "sy_d", vl)
            if rb >= 1:
                pl.wait("ve", "te", st_sig[rb - 1][0])
            st = WSTG
            pl.emit("ve", lambda e, _s=st: e.tensor_scalar_mul(_s, _s, RWC))
            pl.emit("ve", lambda e, _s=st: e.tensor_copy(_s.bitcast(I32), _s))
            pl.emit("ve", lambda e, _s=st: e.tensor_scalar(
                WQR, _s.bitcast(I32), 1.0, -1.0, ALU.min, ALU.max))
            q_sig[rb] = pl.sig("ve")
            pl.wait("te", "ve", q_sig[rb])
            if rb >= 1:
                pl.wait("te", "sy_d", st_sig[rb - 1][1])
            last_te = 0
            v_e = 0
            for half in range(4):
                bk = 4 + (half % 2)
                bank_wait("te", bk)
                pbk = banks[bk][:].bitcast(BF16)
                for k in range(8):
                    kc = half * 8 + k
                    pl.emit("te", lambda e, _p=pbk, _k=k, _kc=kc: e.transpose(
                        _p[:, _k * P:(_k + 1) * P],
                        WQR[:, _kc * P:(_kc + 1) * P], ID_BF))
                v_t = pl.sig("te")
                last_te = v_t
                pl.wait("sc", "te", v_t)
                pl.emit("sc", lambda e, _p=pbk, _h=half: e.activation(
                    TSG[:, _h * 1024:(_h + 1) * 1024], _p, AF.Identity))
                v_e = pl.sig("sc")
                bank_free[bk] = ("sc", v_e)
            pl.wait("sy", "sc", v_e)
            dst = wqt_hbm[:, rb * P:(rb + 1) * P].rearrange(
                "(kc p) o -> p kc o", p=P)
            v_st = pl.dma("sy", "sy_d", dst,
                          TSG[:].rearrange("p (kc o) -> p kc o", kc=NKC))
            st_sig[rb] = (last_te, v_st)
        v_prep_te = pl.cnt["te"]
        v_prep_store = pl.cnt["sy_d"]
        if "wqt" in want:
            dump("wqt", wqt_hbm[0:P, 0:4096], [("sy_d", v_prep_store)])
            dump("prepsc", small[:, 0:512], [("sy_d", v_prep_store)])
        pl.wait("gp", "ve", q_sig[NKC - 1])
        pl.emit("gp", lambda e: e.memset(h_st[:], 0.0))
        v_hmem = pl.sig("gp_c")

        def tm_to_fm(src_tm, tt):
            last = 0
            last_te = 0
            for half in range(4):
                bk = 4 + (half % 2)
                bank_wait("te", bk)
                pbk = banks[bk][:].bitcast(BF16)
                for k in range(8):
                    kc = half * 8 + k
                    pl.emit("te", lambda e, _p=pbk, _k=k, _kc=kc, _s=src_tm:
                            e.transpose(_p[:, _k * P:(_k + 1) * P],
                                        _s[:, _kc * P:(_kc + 1) * P], ID_BF))
                v_t = pl.sig("te")
                last_te = v_t
                pl.wait("sc", "te", v_t)
                dstv = xrq[:].rearrange("p (kc t) -> p kc t", kc=NKC)[
                    :, half * 8:(half + 1) * 8, tt * P:(tt + 1) * P]
                pl.emit("sc", lambda e, _p=pbk, _d=dstv: e.activation(
                    _d, _p, AF.Identity))
                v_e = pl.sig("sc")
                bank_free[bk] = ("sc", v_e)
                last = v_e
            return last, last_te

        XA = w2f32(0, 4096)
        XRT = wqbf(0, 4096)
        wbufs = [w1a, w2bf(12288, 4096)]
        w_last = {0: 0, 1: 0}
        h_upd_last = 0
        v_f2_te_last = 0
        v_f2_ve_last = 0
        v_f2_gpd_last = 0
        res_slot = {0: 0, 1: 0}
        a_done_prev = 0
        blk_te = 0
        blk_ve = 0

        for ch in range(NCH):
            t0 = ch * TCH

            a_done = {}
            v_xrt_prev = 0
            v_tm_te_prev = 0
            pl.wait("sc", "te", v_f2_te_last)
            pl.wait("sc", "ve", v_f2_ve_last)
            for tt in range(NTT):
                xa = XA
                xrt = XRT
                vl = pl.dma("sc", "sc_d", xa,
                            x_hbm[t0 + tt * P: t0 + (tt + 1) * P, :])
                pl.wait("ve", "sc_d", vl)
                pl.wait("ve", "sc", v_xrt_prev)
                for sl in range(8):
                    pl.emit("ve", lambda e, _x=xa, _s=sl: e.bn_stats(
                        BNO[:, _s * 6:(_s + 1) * 6], _x[:, _s * 512:(_s + 1) * 512]))
                pl.emit("ve", lambda e: e.bn_aggr(TC[:, 4:6], BNO))
                pl.emit("ve", lambda e, _x=xa: e.tensor_reduce(
                    TC[:, 7:8], _x, AX.X, ALU.max))
                pl.emit("ve", lambda e: e.tensor_tensor(
                    TC[:, 6:7], TC[:, 5:6], EPSC, ALU.add))
                pl.emit("ve", lambda e, _x=xa: e.tensor_reduce(
                    TC[:, 8:9], _x, AX.X, ALU.min))
                pl.emit("ve", lambda e: e.reciprocal(TC[:, 14:15], TC[:, 6:7]))
                vsp()
                pl.emit("ve", lambda e: e.tensor_sub(TC[:, 10:11], TC[:, 7:8], TC[:, 4:5]))
                vsp()
                pl.emit("ve", lambda e: e.tensor_sub(TC[:, 11:12], TC[:, 4:5], TC[:, 8:9]))
                vsp()
                pl.emit("ve", lambda e: e.tensor_max(TC[:, 10:11], TC[:, 10:11], TC[:, 11:12]))
                vsp()
                pl.emit("ve", lambda e: e.tensor_mul(TC[:, 15:16], TC[:, 5:6], TC[:, 14:15]))
                vsp()
                pl.emit("ve", lambda e: e.tensor_tensor(
                    TC[:, 15:16], TC[:, 15:16], EPSC, ALU.add))
                v_st1 = pl.sig("ve")
                pl.wait("sc", "ve", v_st1)
                pl.emit("sc", lambda e: e.activation(TC[:, 9:10], TC[:, 6:7], AF.Ln))
                pl.emit("sc", lambda e, _c=ch, _t=tt: e.activation(
                    stc(_c, _t, 1), TC[:, 9:10], AF.Exp, scale=-0.5))
                pl.emit("sc", lambda e: e.activation(TC[:, 16:17], TC[:, 15:16], AF.Ln))
                pl.emit("sc", lambda e: e.activation(TC[:, 16:17], TC[:, 16:17],
                                                     AF.Exp, scale=-0.5))
                v_sc1 = pl.sig("sc")
                pl.wait("ve", "sc", v_sc1)
                pl.emit("ve", lambda e, _c=ch, _t=tt: e.tensor_copy(
                    stc(_c, _t, 0), TC[:, 4:5]))
                vsp()
                pl.emit("ve", lambda e, _c=ch, _t=tt: e.scalar_tensor_tensor(
                    TC[:, 12:13], TC[:, 10:11], 1.0, stc(_c, _t, 1),
                    ALU.mult, ALU.mult))
                vsp()
                pl.emit("ve", lambda e: e.tensor_tensor(
                    TC[:, 13:14], TC[:, 12:13], EPSC, ALU.add))
                vsp()
                pl.emit("ve", lambda e, _c=ch, _t=tt: e.tensor_mul(
                    TC[:, 17:18], stc(_c, _t, 1), TC[:, 16:17]))
                vsp()
                pl.emit("ve", lambda e: e.reciprocal(TC[:, 19:20], TC[:, 13:14]))
                vsp()
                pl.emit("ve", lambda e, _c=ch, _t=tt: e.tensor_scalar(
                    stc(_c, _t, 3), TC[:, 13:14], SWC, 1.0 / 128.0,
                    ALU.mult, ALU.mult))
                vsp()
                pl.emit("ve", lambda e, _c=ch, _t=tt: e.tensor_scalar_mul(
                    stc(_c, _t, 2), TC[:, 19:20], 128.0))
                vsp()
                pl.emit("ve", lambda e: e.tensor_mul(TC[:, 18:19], TC[:, 4:5], TC[:, 17:18]))
                vsp()
                pl.emit("ve", lambda e: e.tensor_tensor(
                    TC[:, 18:19], TC[:, 18:19], NEG1C, ALU.mult))
                vsp()
                v_ve3 = pl.sig("ve")
                if "a0" in want and ch == 0 and tt == 0:
                    dump("a0", small[:, 0:512], [("ve", v_ve3)])
                    dump("a0s", stats[:, 0:96], [("ve", v_ve3)])
                pl.wait("sc", "ve", v_ve3)
                pl.wait("sc", "te", v_tm_te_prev)
                pl.emit("sc", lambda e, _x=xa, _r=xrt: e.activation(
                    _r, _x, AF.Identity, scale=TC[:, 17:18], bias=TC[:, 18:19]))
                v_xrt_prev = pl.sig("sc")
                if "xrt0" in want and ch == 0 and tt == 0:
                    dump("xrt0", xrt[:, 0:4096], [("sc", v_xrt_prev)])
                pl.wait("te", "sc", v_xrt_prev)
                a_done[tt], v_tm_te_prev = tm_to_fm(xrt, tt)
            a_done_prev = a_done[NTT - 1]
            w_last[1] = v_tm_te_prev
            pl.wait("ve", "sc", a_done_prev)
            for kc in range(NKC):
                pl.emit("ve", lambda e, _k=kc: e.tensor_scalar_mul(
                    xrq[:, _k * TCH:(_k + 1) * TCH],
                    xrq[:, _k * TCH:(_k + 1) * TCH], LNW[:, _k:_k + 1]))
            v_xr_ready = pl.sig("ve")
            if "xr" in want and ch == 0:
                dump("xr", xrq[:, 0:8192], [("ve", v_xr_ready)])

            CACC = [w2f32(0, 1024), w2f32(4096, 1024)]
            mm_done = {}
            ev_done = {}
            cv_done = {}
            cv_ve = {}
            wl = {}
            pl.wait("te", "ve", v_xr_ready)

            def b_load(jt):
                if ch == 0 and jt == 0:
                    wl[0] = wl0_pre
                    return
                width = P if jt < 34 else 64
                buf = jt % 2
                if jt >= 2:
                    pl.wait("gp", "te", mm_done[jt - 2])
                else:
                    pl.wait("gp", "te", w_last[buf])
                src = w_in[:, DS + jt * P: DS + jt * P + width].rearrange(
                    "(kc p) j -> p kc j", p=P)
                dstw = wbufs[buf][:, 0:NKC * width].rearrange(
                    "p (kc j) -> p kc j", kc=NKC)
                wl[jt] = pl.dma("gp", "gp_d", dstw, src)

            b_load(0)
            b_load(1)
            for jt in range(NJ1):
                width = P if jt < 34 else 64
                wb = wbufs[jt % 2]
                pl.wait("te", "gp_d", wl[jt])
                for tg in range(2):
                    bk = (2 * jt + tg) % 6
                    bank_wait("te", bk)
                    outp = banks[bk][0:width, 0:512]
                    for kc in range(NKC):
                        pl.emit("te", lambda e, _o=outp, _w=wb, _kc=kc, _tg=tg,
                                _wd=width: e.matmul(
                            _o, _w[:, _kc * _wd:(_kc + 1) * _wd],
                            xrq[:, _kc * TCH + _tg * 512: _kc * TCH + (_tg + 1) * 512],
                            start=(_kc == 0), stop=(_kc == NKC - 1)))
                    v_mm = pl.sig("te")
                    if tg == 1:
                        mm_done[jt] = v_mm
                        w_last[jt % 2] = v_mm
                    pl.wait("sc", "te", v_mm)
                    if jt < 34:
                        pl.emit("sc", lambda e, _b=banks[bk], _jt=jt, _tg=tg:
                                e.activation(cv(_jt, 3 + _tg * 512, 512),
                                             _b[:, 0:512], AF.Identity,
                                             scale=MUP[:, 32 + _jt:33 + _jt]))
                    else:
                        pl.wait("sc", "te", v_prep_te)
                        dtr_t = w2f32(8192, 1024)
                        pl.emit("sc", lambda e, _b=banks[bk], _tg=tg, _d=dtr_t:
                                e.activation(
                                    _d[0:NH, _tg * 512:(_tg + 1) * 512],
                                    _b[0:NH, 0:512], AF.Identity,
                                    scale=MUP[0:NH, 66:67]))
                    v_ev = pl.sig("sc")
                    bank_free[bk] = ("sc", v_ev)
                    ev_done[(jt, tg)] = v_ev
                if jt + 2 < NJ1:
                    b_load(jt + 2)
                if jt < 34:
                    pl.wait("ve", "sc", ev_done[(jt, 1)])
                    if jt >= 2:
                        pl.wait("ve", "sc", cv_done.get(jt - 2, 0))
                    acc = CACC[jt % 2]
                    pl.emit("ve", lambda e, _j=jt, _a=acc: e.tensor_scalar_mul(
                        _a, cv(_j, 0, TCH), CW4[:, _j * 4:_j * 4 + 1]))
                    for k in range(1, 4):
                        pl.emit("ve", lambda e, _j=jt, _a=acc, _k=k:
                                e.scalar_tensor_tensor(
                                    _a, cv(_j, _k, TCH),
                                    CW4[:, _j * 4 + _k:_j * 4 + _k + 1], _a,
                                    ALU.mult, ALU.add))
                    pl.emit("ve", lambda e, _j=jt: e.tensor_copy(
                        cv(_j, 0, 3), cv(_j, TCH, 3)))
                    cv_ve[jt] = pl.sig("ve")
                lag = jt - 1
                if 0 <= lag < 34:
                    pl.wait("sc", "ve", cv_ve[lag])
                    pl.emit("sc", lambda e, _j=lag, _a=CACC[lag % 2]: e.activation(
                        cv(_j, 3, TCH), _a, AF.Silu))
                    cv_done[lag] = pl.sig("sc")
                if jt == NJ1 - 1:
                    pl.wait("sc", "ve", cv_ve[33])
                    pl.emit("sc", lambda e, _a=CACC[33 % 2]: e.activation(
                        cv(33, 3, TCH), _a, AF.Silu))
                    cv_done[33] = pl.sig("sc")
            DTP = dtb[0:NH, 0:TCH]
            SEG = dtb[0:NH, TCH:2 * TCH]
            DTR = w2f32(8192, 1024)[0:NH, :]
            pl.wait("sc", "gp_d", v_f2_gpd_last)
            pl.wait("ve", "gp_d", v_f2_gpd_last)
            pl.emit("sc", lambda e: e.activation(DTR, DTR, AF.Identity, bias=DTBI[0:NH, :]))
            pl.emit("sc", lambda e: e.activation(SEG, DTR, AF.Abs))
            pl.emit("sc", lambda e: e.activation(SEG, SEG, AF.Exp, scale=-1.0))
            pl.emit("sc", lambda e: e.activation(SEG, SEG, AF.Ln, bias=1.0))
            v_sp = pl.sig("sc")
            pl.wait("ve", "sc", v_sp)
            pl.emit("ve", lambda e: e.tensor_scalar_max(DTP, DTR, 0.0))
            pl.emit("ve", lambda e: e.tensor_add(DTP, DTP, SEG))
            pl.emit("ve", lambda e: e.tensor_mul(
                DTR, DTP, AC[0:NH, :].broadcast_to([NH, TCH])))
            for qb in range(NBLK):
                pl.emit("ve", lambda e, _q=qb: e.tensor_tensor_scan(
                    SEG[:, _q * P:(_q + 1) * P], DTR[:, _q * P:(_q + 1) * P],
                    DTR[:, _q * P:(_q + 1) * P], 0.0, ALU.add, ALU.bypass))
            v_seg = pl.sig("ve")
            if "convout" in want and ch == 0:
                dump("convout", convin[:, 0:8192], [("sc", cv_done[33])])
            if "dt" in want and ch == 0:
                dump("dt", dtb[0:NH, 0:2048], [("ve", v_seg)])

            XT = w2bf(0, 4096)
            E8 = [w2bf(8192, 1024), w2bf(10240, 1024)]
            M8 = E8
            Y2S = [w2f32(12288, 512), w2f32(14336, 512)]
            XW = [w2bf(16384, 512), w2bf(17408, 512)]
            YTM = [w2bf(18432, 512), w2bf(19456, 512)]
            HBF = [wqbf(0, 512), wqbf(1024, 512)]
            CBM = wqbf(2048, 128)
            BT = wqbf(2304, 128)
            SEGT = wqf32(2560, 64)
            DTT = wqf32(2816, 64)
            WST = wqf32(3072, 64)
            ESGT = wqf32(3328, 64)
            ENDR = wqf32(3584, 64)
            B64 = wqf32(3840, 64)
            SGC = [wqf32(4096, 1024), wqf32(8192, 1024)]
            WSB = SPARE[0:NH, 0:128]
            ESGF = SPARE[0:NH, 128:256]

            for qb in range(NBLK):
                tok = 3 + qb * P
                segb = SEG[:, qb * P:(qb + 1) * P]
                dtpb = DTP[:, qb * P:(qb + 1) * P]
                wsb = WSB
                pl.wait("ve", "te", blk_te)
                pl.emit("ve", lambda e, _s=segb, _w=wsb, _q=qb: e.tensor_scalar(
                    _w, _s, SEG[0:NH, (_q + 1) * P - 1:(_q + 1) * P], None,
                    ALU.subtract))
                v_w1 = pl.sig("ve")
                pl.wait("sc", "ve", v_w1)
                pl.wait("sc", "te", blk_te)
                pl.emit("sc", lambda e, _w=wsb: e.activation(_w, _w, AF.Exp, scale=-1.0))
                v_w2s = pl.sig("sc")
                pl.wait("ve", "sc", v_w2s)
                pl.emit("ve", lambda e, _w=wsb, _d=dtpb: e.tensor_mul(_w, _w, _d))
                v_w3 = pl.sig("ve")
                pl.wait("sc", "ve", v_w3)
                pl.emit("sc", lambda e, _s=segb: e.activation(
                    ESGF, _s, AF.Exp))
                v_esg = pl.sig("sc")
                bank_wait("te", 6)
                bank_wait("te", 7)
                pl.wait("te", "sc", v_esg)
                pb7b = pb7[:].bitcast(BF16)
                pl.emit("te", lambda e, _s=segb: e.transpose(
                    pb7[:, 64:128], _s, ID_F32[0:64, 0:64]))
                pl.emit("te", lambda e, _d=dtpb: e.transpose(
                    pb7[:, 128:192], _d, ID_F32[0:64, 0:64]))
                pl.emit("te", lambda e, _w=wsb: e.transpose(
                    pb7[:, 192:256], _w, ID_F32[0:64, 0:64]))
                pl.emit("te", lambda e: e.transpose(
                    pb7[:, 256:320], ESGF, ID_F32[0:64, 0:64]))
                pl.emit("te", lambda e: e.transpose(
                    pb7[0:1, 320:384], ESGF[:, P - 1:P], ID_F32[0:64, 0:64]))
                pl.emit("te", lambda e, _t=tok: e.matmul(
                    pb6[:, 0:128], cv(32, _t, P), cv(33, _t, P),
                    start=True, stop=True))
                v_pre1 = pl.sig("te")
                pl.wait("sc", "te", v_pre1)
                pl.emit("sc", lambda e: e.activation(SEGT, pb7[:, 64:128], AF.Identity))
                pl.emit("sc", lambda e: e.activation(DTT, pb7[:, 128:192], AF.Identity))
                pl.emit("sc", lambda e: e.activation(WST, pb7[:, 192:256], AF.Identity))
                pl.emit("sc", lambda e: e.activation(ESGT, pb7[:, 256:320], AF.Identity))
                pl.emit("sc", lambda e: e.activation(
                    ENDR[0:1, 0:64], pb7[0:1, 320:384], AF.Identity))
                v_pre2 = pl.sig("sc")
                pl.wait("ve", "te", v_pre1)
                pl.emit("ve", lambda e: e.tensor_tensor(CBM, pb6[:, 0:128], LT, ALU.mult))
                v_cbm = pl.sig("ve")
                bank_free[6] = ("ve", v_cbm)
                pl.wait("te", "sc", v_pre2)
                pl.emit("te", lambda e: e.matmul(
                    pb7[:, 384:448], ONES_R[0:1, 0:128], ENDR[0:1, 0:64],
                    start=False, stop=True, skip_group_check=True))
                pl.emit("te", lambda e, _t=tok: e.transpose(
                    pb7b[:, 896:1024], cv(32, _t, P), ID_BF))
                v_pre3 = pl.sig("te")
                pl.wait("sc", "te", v_pre3)
                pl.emit("sc", lambda e: e.activation(B64, pb7[:, 384:448], AF.Identity))
                pl.emit("sc", lambda e: e.activation(BT, pb7b[:, 896:1024], AF.Identity))
                v_pre4 = pl.sig("sc")
                bank_free[7] = ("sc", v_pre4)
                v_xt = 0
                for half in range(4):
                    bk = 4 + (half % 2)
                    bank_wait("te", bk)
                    pbk = banks[bk][:].bitcast(BF16)
                    for k in range(8):
                        ft = half * 8 + k
                        pl.emit("te", lambda e, _p=pbk, _k=k, _f=ft, _t=tok:
                                e.transpose(_p[:, _k * P:(_k + 1) * P],
                                            cv(_f, _t, P), ID_BF))
                    v_t = pl.sig("te")
                    pl.wait("sc", "te", v_t)
                    pl.emit("sc", lambda e, _p=pbk, _h=half: e.activation(
                        XT[:, _h * 1024:(_h + 1) * 1024], _p, AF.Identity))
                    v_xt = pl.sig("sc")
                    bank_free[bk] = ("sc", v_xt)

                relu_sig = {}
                m8_sig = {}
                xw_sig = {}
                hbf_sig = {}
                yfin_sig = {}
                hupd_sig = {}
                ywr_sig = {}
                rmm_sig = {}
                ymm_sig = {}
                stm_sig = {}
                yt_sig = {}
                for g in range(NG):
                    pr = g % 2
                    rbA, rbB = (0, 1) if pr == 0 else (2, 3)
                    y1b = 4 + pr
                    y2b = 6 + pr
                    if g >= 2:
                        pl.wait("sc", "te", rmm_sig[g - 2])
                    seg_rows = dtb[g * 8:(g + 1) * 8, TCH + qb * P: TCH + (qb + 1) * P]
                    pl.wait("sc", "sy_d", v_prep_store)
                    v_sgc = pl.dma(
                        "sc", "sc_d",
                        SGC[pr][0:1, :].rearrange("p (a b) -> p a b", a=8),
                        seg_rows)
                    pl.wait("te", "sc_d", v_sgc)
                    if g >= 2:
                        pl.wait("te", "ve", hupd_sig[g - 2])
                        pl.wait("te", "ve", ywr_sig[g - 2])
                    else:
                        bank_wait("te", rbA)
                        bank_wait("te", rbB)
                    pl.emit("te", lambda e, _b=banks[rbA], _p=pr: e.matmul(
                        _b[:, 0:512], ONES_R[0:1, 0:128], SGC[_p][0:1, 0:512],
                        start=True, stop=True))
                    pl.emit("te", lambda e, _b=banks[rbB], _p=pr: e.matmul(
                        _b[:, 0:512], ONES_R[0:1, 0:128], SGC[_p][0:1, 512:1024],
                        start=True, stop=True))
                    rmm_sig[g] = pl.sig("te")
                    pl.wait("sc", "te", rmm_sig[g])
                    if g >= 2:
                        pl.wait("sc", "ve", m8_sig[g - 2])
                    for rl in range(8):
                        bsel = banks[rbA] if rl < 4 else banks[rbB]
                        pl.emit("sc", lambda e, _b=bsel, _r=rl, _g=g:
                                e.activation(
                                    _b[:, (_r % 4) * P:((_r % 4) + 1) * P],
                                    _b[:, (_r % 4) * P:((_r % 4) + 1) * P],
                                    AF.Relu, scale=-1.0,
                                    bias=SEGT[:, _g * 8 + _r:_g * 8 + _r + 1]))
                    pl.emit("sc", lambda e, _p=pr, _bA=banks[rbA]: e.activation(
                        E8[_p][:, 0:512], _bA[:, 0:512], AF.Exp, scale=-1.0))
                    pl.emit("sc", lambda e, _p=pr, _bB=banks[rbB]: e.activation(
                        E8[_p][:, 512:1024], _bB[:, 0:512], AF.Exp, scale=-1.0))
                    relu_sig[g] = pl.sig("sc")
                    pl.wait("ve", "sc", relu_sig[g])
                    pl.emit("ve", lambda e, _p=pr: e.tensor_tensor(
                        E8[_p], E8[_p],
                        CBM.unsqueeze(1).broadcast_to([P, 8, P]), ALU.mult))
                    pl.emit("ve", lambda e, _p=pr, _g=g: e.tensor_tensor(
                        M8[_p], E8[_p],
                        DTT[:, _g * 8:(_g + 1) * 8].unsqueeze(2)
                        .broadcast_to([P, 8, P]), ALU.mult))
                    pl.emit("ve", lambda e, _p=pr, _g=g: e.tensor_tensor(
                        XW[_p], XT[:, _g * 512:(_g + 1) * 512],
                        WST[:, _g * 8:(_g + 1) * 8].unsqueeze(2)
                        .broadcast_to([P, 8, HD]), ALU.mult))
                    m8_sig[g] = xw_sig[g] = pl.sig("ve")
                    if g >= 1:
                        pl.wait("sc", "ve", hupd_sig[g - 1])
                    else:
                        pl.wait("sc", "ve", h_upd_last)
                        pl.wait("sc", "gp_c", v_hmem)
                    pl.emit("sc", lambda e, _p=pr, _g=g: e.activation(
                        HBF[_p], h_st[:, _g * 512:(_g + 1) * 512], AF.Identity))
                    hbf_sig[g] = pl.sig("sc")
                    pl.wait("te", "ve", m8_sig[g])
                    pl.wait("te", "sc", hbf_sig[g])
                    if g >= 2:
                        pl.wait("te", "ve", yfin_sig[g - 2])
                    for rl in range(8):
                        pl.emit("te", lambda e, _b=banks[y1b], _p=pr, _r=rl, _g=g:
                                e.matmul(_b[:, _r * HD:(_r + 1) * HD],
                                         M8[_p][:, _r * P:(_r + 1) * P],
                                         XT[:, (_g * 8 + _r) * HD:(_g * 8 + _r + 1) * HD],
                                         start=(_r == 0), stop=(_r == 7),
                                         skip_group_check=True))
                    for rl in range(8):
                        pl.emit("te", lambda e, _b=banks[y2b], _p=pr, _r=rl, _t=tok:
                                e.matmul(_b[:, _r * HD:(_r + 1) * HD],
                                         cv(33, _t, P),
                                         HBF[_p][:, _r * HD:(_r + 1) * HD],
                                         start=(_r == 0), stop=(_r == 7),
                                         skip_group_check=True))
                    ymm_sig[g] = pl.sig("te")
                    pl.wait("te", "ve", xw_sig[g])
                    for rl in range(8):
                        pl.emit("te", lambda e, _b=banks[rbA], _p=pr, _r=rl:
                                e.matmul(_b[:, _r * HD:(_r + 1) * HD],
                                         BT, XW[_p][:, _r * HD:(_r + 1) * HD],
                                         start=(_r == 0), stop=(_r == 7),
                                         skip_group_check=True))
                    stm_sig[g] = pl.sig("te")
                    pl.wait("ve", "te", ymm_sig[g])
                    pl.emit("ve", lambda e, _p=pr, _g=g, _b=banks[y2b]:
                            e.tensor_tensor(Y2S[_p], _b[:, 0:512],
                                            ESGT[:, _g * 8:(_g + 1) * 8]
                                            .unsqueeze(2).broadcast_to([P, 8, HD]),
                                            ALU.mult))
                    pl.emit("ve", lambda e, _p=pr, _b=banks[y1b]:
                            e.tensor_tensor(YTM[_p], _b[:, 0:512], Y2S[_p], ALU.add))
                    yfin_sig[g] = pl.sig("ve")
                    pl.wait("ve", "te", stm_sig[g])
                    pl.emit("ve", lambda e, _p=pr, _g=g: e.tensor_tensor(
                        Y2S[_p], h_st[:, _g * 512:(_g + 1) * 512],
                        B64[:, _g * 8:(_g + 1) * 8].unsqueeze(2)
                        .broadcast_to([P, 8, HD]), ALU.mult))
                    pl.emit("ve", lambda e, _p=pr, _g=g, _b=banks[rbA]:
                            e.tensor_tensor(h_st[:, _g * 512:(_g + 1) * 512],
                                            Y2S[_p], _b[:, 0:512], ALU.add))
                    pl.wait("te", "ve", yfin_sig[g])
                    rbB_bf = banks[rbB][:].bitcast(BF16)
                    for k in range(4):
                        pl.emit("te", lambda e, _b=rbB_bf, _p=pr, _k=k:
                                e.transpose(_b[:, 512 + _k * P: 512 + (_k + 1) * P],
                                            YTM[_p][:, _k * P:(_k + 1) * P], ID_BF))
                    yt_sig[g] = pl.sig("te")
                    pl.wait("ve", "te", yt_sig[g])
                    for k in range(4):
                        ft = g * 4 + k
                        pl.emit("ve", lambda e, _b=rbB_bf, _k=k, _f=ft, _t=tok:
                                e.scalar_tensor_tensor(
                                    cv(_f, _t, P), cv(_f, _t, P),
                                    DFM[:, _f:_f + 1],
                                    _b[:, 512 + _k * P: 512 + (_k + 1) * P],
                                    ALU.mult, ALU.add))
                    hupd_sig[g] = ywr_sig[g] = pl.sig("ve")
                h_upd_last = hupd_sig[NG - 1]
                blk_te = yt_sig[NG - 1]
                blk_ve = ywr_sig[NG - 1]
                bank_free[0] = ("ve", hupd_sig[NG - 2])
                bank_free[1] = ("ve", ywr_sig[NG - 2])
                bank_free[2] = ("ve", hupd_sig[NG - 1])
                bank_free[3] = ("ve", ywr_sig[NG - 1])
                bank_free[4] = ("ve", yfin_sig[NG - 2])
                bank_free[5] = ("ve", yfin_sig[NG - 1])
                bank_free[6] = ("ve", yfin_sig[NG - 2])
                bank_free[7] = ("ve", yfin_sig[NG - 1])
            if "y" in want and ch == 0:
                dump("y", convin[:, 0:8192], [("ve", blk_ve)])

            SZ = [w2bf(0, 512), w2bf(1024, 512)]
            GY2 = [w2bf(2048, 512), w2bf(3072, 512)]
            SRMS = w2f32(4096, 1024)
            ez_mm = {}
            gy_sig = {}
            g2_sig = {}
            sqs = {}
            ezl = {}
            def e_load(jt):
                buf = jt % 2
                if jt >= 2:
                    pl.wait("gp", "te", ez_mm[jt - 2])
                elif jt == 1:
                    pl.wait("gp", "te", blk_te)
                    pl.wait("gp", "ve", blk_ve)
                else:
                    pl.wait("gp", "te", w_last[buf])
                src = w_in[:, jt * P:(jt + 1) * P].rearrange("(kc p) j -> p kc j", p=P)
                dstw = wbufs[buf][:].rearrange("p (kc j) -> p kc j", kc=NKC)
                ezl[jt] = pl.dma("gp", "gp_d", dstw, src)

            e_load(0)
            e_load(1)
            for jt in range(NJ2):
                wb = wbufs[jt % 2]
                pl.wait("te", "gp_d", ezl[jt])
                pl.wait("te", "ve", blk_ve)
                for tg in range(2):
                    bk = (2 * jt + tg) % 4
                    bank_wait("te", bk)
                    for kc in range(NKC):
                        pl.emit("te", lambda e, _b=banks[bk], _w=wb, _kc=kc, _tg=tg:
                                e.matmul(_b[:, 0:512],
                                         _w[:, _kc * P:(_kc + 1) * P],
                                         xrq[:, _kc * TCH + _tg * 512: _kc * TCH + (_tg + 1) * 512],
                                         start=(_kc == 0), stop=(_kc == NKC - 1)))
                    v_mm = pl.sig("te")
                    if tg == 1:
                        ez_mm[jt] = v_mm
                        w_last[jt % 2] = v_mm
                        if jt + 2 < NJ2:
                            e_load(jt + 2)
                    pl.wait("sc", "te", v_mm)
                    pl.wait("sc", "ve", gy_sig.get((jt - 1, tg), 0))
                    pl.emit("sc", lambda e, _b=banks[bk], _jt=jt, _tg=tg:
                            e.activation(SZ[_tg], _b[:, 0:512], AF.Silu,
                                         scale=MUP[:, _jt:_jt + 1]))
                    v_sz = pl.sig("sc")
                    bank_free[bk] = ("sc", v_sz)
                    pl.wait("ve", "sc", v_sz)
                    pl.emit("ve", lambda e, _jt=jt, _tg=tg: e.tensor_tensor(
                        cv(_jt, 3 + _tg * 512, 512), cv(_jt, 3 + _tg * 512, 512),
                        SZ[_tg], ALU.mult))
                    gy_sig[(jt, tg)] = pl.sig("ve")
                    pl.wait("sc", "ve", gy_sig[(jt, tg)])
                    pl.wait("sc", "te", sqs.get((jt - 1, tg), 0))
                    pl.emit("sc", lambda e, _jt=jt, _tg=tg: e.activation(
                        GY2[_tg], cv(_jt, 3 + _tg * 512, 512), AF.Square))
                    g2_sig[(jt, tg)] = pl.sig("sc")
                    pl.wait("te", "sc", g2_sig[(jt, tg)])
                    sbk = 4 + tg
                    if jt == 0:
                        bank_wait("te", sbk)
                    pl.emit("te", lambda e, _b=banks[sbk], _tg=tg, _jt=jt:
                            e.matmul(_b[0:1, 0:512], ONES_BC, GY2[_tg],
                                     start=(_jt == 0), stop=(_jt == NJ2 - 1),
                                     skip_group_check=True))
                    sqs[(jt, tg)] = pl.sig("te")
                    pl.wait("ve", "sc", g2_sig[(jt, tg)])
                    pl.emit("ve", lambda e, _jt=jt, _tg=tg: e.tensor_scalar_mul(
                        cv(_jt, 3 + _tg * 512, 512), cv(_jt, 3 + _tg * 512, 512),
                        NRMW[:, _jt:_jt + 1]))
            pl.wait("sc", "te", sqs[(NJ2 - 1, 1)])
            pl.emit("sc", lambda e: e.activation(SRMS[0:1, 0:512], pb4[0:1, 0:512], AF.Identity))
            pl.emit("sc", lambda e: e.activation(SRMS[0:1, 512:1024], pb5[0:1, 0:512], AF.Identity))
            v_sr = pl.sig("sc")
            bank_free[4] = ("sc", v_sr)
            bank_free[5] = ("sc", v_sr)
            pl.wait("ve", "sc", v_sr)
            pl.emit("ve", lambda e: e.tensor_scalar(
                SRMS[0:1, 0:1024], SRMS[0:1, 0:1024], 1.0 / DS, EPS,
                ALU.mult, ALU.add))
            v_sr2 = pl.sig("ve")
            pl.wait("sc", "ve", v_sr2)
            pl.emit("sc", lambda e: e.activation(SRMS[0:1, 0:1024], SRMS[0:1, 0:1024], AF.Ln))
            pl.emit("sc", lambda e: e.activation(SRMS[0:1, 0:1024], SRMS[0:1, 0:1024],
                                                 AF.Exp, scale=-0.5))
            v_sr3 = pl.sig("sc")
            pl.wait("te", "sc", v_sr3)
            bank_wait("te", 6)
            for tt in range(NTT):
                pl.emit("te", lambda e, _t=tt: e.matmul(
                    pb6[:, _t:_t + 1], SRMS[0:1, _t * P:(_t + 1) * P],
                    ONES_R[0:1, 0:1], start=(_t == 0), stop=(_t == NTT - 1),
                    skip_group_check=True))
            v_scol = pl.sig("te")
            pl.wait("sc", "te", v_scol)
            for tt in range(NTT):
                pl.emit("sc", lambda e, _t=tt, _c=ch: e.activation(
                    stc(_c, _t, 4), pb6[:, _t:_t + 1], AF.Identity))
            v_scols = pl.sig("sc")
            bank_free[6] = ("sc", v_scols)
            if "gy" in want and ch == 0:
                dump("gy", convin[:, 0:8192], [("sc", v_scols)])
            if "srms" in want and ch == 0:
                dump("srms", stats[:, 0:96], [("sc", v_scols)])

            XNQ = w2f32(0, 4096)
            XQT = wqbf(0, 4096)
            f1_done = 0
            v_f1b_prev = 0
            v_f1c_prev = 0
            v_f1_te_tm = 0
            pl.wait("sy", "sc", v_scols)
            for tt in range(NTT):
                pl.wait("sy", "ve", v_f1c_prev)
                vl = pl.dma("sy", "sy_d", XNQ,
                            x_hbm[t0 + tt * P: t0 + (tt + 1) * P, :])
                pl.wait("ve", "sy_d", vl)
                pl.wait("ve", "sc", v_f1b_prev)
                pl.emit("ve", lambda e, _c=ch, _t=tt: e.tensor_mul(
                    TC[:, 20:21], stc(_c, _t, 1), stc(_c, _t, 2)))
                vsp()
                pl.emit("ve", lambda e, _c=ch, _t=tt: e.tensor_mul(
                    TC[:, 21:22], stc(_c, _t, 0), TC[:, 20:21]))
                vsp()
                pl.emit("ve", lambda e: e.tensor_tensor(
                    TC[:, 21:22], TC[:, 21:22], NEG1C, ALU.mult))
                vsp()
                v_f1a = pl.sig("ve")
                pl.wait("sc", "ve", v_f1a)
                pl.emit("sc", lambda e: e.activation(
                    XNQ, XNQ, AF.Identity, scale=TC[:, 20:21], bias=TC[:, 21:22]))
                v_f1b_prev = pl.sig("sc")
                pl.wait("ve", "sc", v_f1b_prev)
                pl.wait("ve", "te", v_f1_te_tm)
                pl.emit("ve", lambda e: e.tensor_copy(XNQ.bitcast(I32), XNQ))
                pl.emit("ve", lambda e: e.tensor_copy(XNQ, XNQ.bitcast(I32)))
                pl.emit("ve", lambda e: e.tensor_scalar(
                    XQT[:, 0:4096], XNQ, QBHI, -QBHI, ALU.min, ALU.max))
                v_f1c_prev = pl.sig("ve")
                pl.wait("te", "ve", v_f1c_prev)
                f1_done, v_f1_te_tm = tm_to_fm(XQT[:, 0:4096], tt)
            if "xq" in want and ch == 0:
                dump("xq", xrq[:, 0:8192], [("sc", f1_done)])
            v_f1_te = pl.cnt["te"]
            v_f1_ve = pl.cnt["ve"]
            v_f1_sc = f1_done

            WOBUF = w2bf(0, 8192)
            WQB = wqbf(0, 8192)
            GBV = SPARE[:].bitcast(BF16)
            GB = [GBV[:, 0:256], GBV[:, 256:512]]
            ffn_mm = {}
            wo_used = {}
            gb_busy = {0: 0, 1: 0}
            for ob in range(NOB):
                if ob >= 1:
                    pl.wait("sy", "te", ffn_mm[ob - 1])
                else:
                    pl.wait("sy", "sc", v_f1_sc)
                src = wqt_hbm[:, ob * OB:(ob + 1) * OB].rearrange(
                    "(kc p) o -> p kc o", p=P)
                v_wq = pl.dma("sy", "sy_d",
                              WQB[:].rearrange("p (kc o) -> p kc o", kc=NKC), src)
                if ob >= 1:
                    pl.wait("gp", "te", wo_used[ob - 1])
                else:
                    pl.wait("gp", "ve", v_f1_ve)
                    pl.wait("gp", "te", v_f1_te)
                srco = w_out[:, ob * OB:(ob + 1) * OB].rearrange(
                    "(kc p) o -> p kc o", p=P)
                v_wo = pl.dma("gp", "gp_d",
                              WOBUF[:].rearrange("p (kc o) -> p kc o", kc=NKC),
                              srco)
                if "wqb" in want and ch == 0 and ob == 0:
                    dump("wqb", WQB[:, 0:8192], [("sy_d", v_wq)])
                    dump("wqt1", wqt_hbm[P:2 * P, 0:4096], [("sy_d", v_wq)])
                pl.wait("te", "sy_d", v_wq)
                for tt in range(NTT):
                    bk = 2 + (tt % 2)
                    bank_wait("te", bk)
                    for kc in range(NKC):
                        pl.emit("te", lambda e, _b=banks[bk], _kc=kc, _t=tt:
                                e.matmul(_b[:, 0:OB],
                                         xrq[:, _kc * TCH + _t * P: _kc * TCH + (_t + 1) * P],
                                         WQB[:, _kc * OB:(_kc + 1) * OB],
                                         start=(_kc == 0), stop=(_kc == NKC - 1)))
                    v_fmm = pl.sig("te")
                    if tt == NTT - 1:
                        ffn_mm[ob] = v_fmm
                    pl.wait("sc", "te", v_fmm)
                    if "psb00" in want and ch == 0 and ob == 0 and tt == 0:
                        pl.emit("sc", lambda e, _b=banks[bk]: e.activation(
                            w2f32(16384, 256), _b[:, 0:OB], AF.Identity))
                        dump("psb00", w2f32(16384, 256), [("sc", pl.sig("sc"))])
                    pl.wait("sc", "ve", gb_busy[tt % 2])
                    pl.emit("sc", lambda e, _b=banks[bk], _t=tt, _c=ch, _g=GB[tt % 2]:
                            e.activation(_g, _b[:, 0:OB], AF.Silu,
                                         scale=stc(_c, _t, 3)))
                    v_gb = pl.sig("sc")
                    bank_free[bk] = ("sc", v_gb)
                    pl.wait("te", "gp_d", v_wo)
                    bko = tt % 2
                    bank_wait("te", bko)
                    for kc in range(NKC):
                        pl.emit("te", lambda e, _b=banks[bko], _kc=kc, _t=tt:
                                e.matmul(_b[:, 0:OB],
                                         cv(_kc, 3 + _t * P, P),
                                         WOBUF[:, _kc * OB:(_kc + 1) * OB],
                                         start=(_kc == 0), stop=(_kc == NKC - 1)))
                    v_omm = pl.sig("te")
                    if tt == NTT - 1:
                        wo_used[ob] = v_omm
                    pl.wait("ve", "te", v_omm)
                    pl.wait("ve", "sc", v_gb)
                    pl.wait("ve", "gp_d", res_slot[tt % 2])
                    rslot = resb[:, (tt % 2) * OB:(tt % 2 + 1) * OB]
                    if "only_f" in want:
                        pl.emit("ve", lambda e, _b=banks[bk], _t=tt, _c=ch,
                                _r=rslot: e.scalar_tensor_tensor(
                            _r, _b[:, 0:OB], stc(_c, _t, 3),
                            ONES_CF.broadcast_to([P, OB]), ALU.mult, ALU.mult))
                    elif "only_s" in want:
                        pl.emit("ve", lambda e, _b=banks[bko], _t=tt, _c=ch,
                                _r=rslot: e.scalar_tensor_tensor(
                            _r, _b[:, 0:OB], stc(_c, _t, 4),
                            ONES_CF.broadcast_to([P, OB]), ALU.mult, ALU.mult))
                    elif "only_g" in want:
                        pl.emit("ve", lambda e, _b=banks[bko], _t=tt, _c=ch,
                                _g=GB[tt % 2], _r=rslot: e.scalar_tensor_tensor(
                            _r, _b[:, 0:OB], 0.0, _g, ALU.mult, ALU.add))
                    else:
                        pl.emit("ve", lambda e, _b=banks[bko], _t=tt, _c=ch,
                                _g=GB[tt % 2], _r=rslot: e.scalar_tensor_tensor(
                            _r, _b[:, 0:OB], stc(_c, _t, 4), _g, ALU.mult, ALU.mult))
                    v_res = pl.sig("ve")
                    gb_busy[tt % 2] = v_res
                    bank_free[bko] = ("ve", v_res)
                    pl.wait("gp", "ve", v_res)
                    pl.wait("gp", "sc_d", v_prefill)
                    dstout = out_hbm[t0 + tt * P: t0 + (tt + 1) * P,
                                     ob * OB:(ob + 1) * OB]
                    res_slot[tt % 2] = pl.dma("gp", "gp_d", dstout, rslot,
                                              accum_op=ALU.add)
            v_f2_te_last = pl.cnt["te"]
            v_f2_ve_last = pl.cnt["ve"]
            v_f2_gpd_last = pl.cnt["gp_d"]

        pl.emit("gp", lambda e: e.wait_ge(s_gpd, pl.cnt["gp_d"]))
        pl.emit("sy", lambda e: e.wait_ge(s_syd, pl.cnt["sy_d"]))
        pl.emit("sc", lambda e: e.wait_ge(s_scd, pl.cnt["sc_d"]))

        for s in sems.values():
            nc.sync.sem_clear(s)
        nc.all_engine_barrier()
        with nc.Block() as block:
            @block.sync
            def _(e):
                for fn in pl.ops["sy"]:
                    fn(e)

            @block.gpsimd
            def _(e):
                for fn in pl.ops["gp"]:
                    fn(e)

            @block.tensor
            def _(e):
                for fn in pl.ops["te"]:
                    fn(e)

            @block.scalar
            def _(e):
                for fn in pl.ops["sc"]:
                    fn(e)

            @block.vector
            def _(e):
                for fn in pl.ops["ve"]:
                    fn(e)

    return nc, dbg_names


def _consts():
    c = {}
    c["id_bf"] = np.eye(P, dtype=ml_dtypes.bfloat16)
    c["id_f32"] = np.ascontiguousarray(np.eye(P, dtype=np.float32)[:, 0:64])
    c["ones_r"] = np.ones((1, P), np.float32)
    c["ones_cf"] = np.ones((P, 1), np.float32)
    c["ones_bc"] = np.ones((P, 1), ml_dtypes.bfloat16)
    c["epsc"] = np.full((P, 1), 1e-5, np.float32)
    c["neg1c"] = np.full((P, 1), -1.0, np.float32)
    lt = (np.arange(P)[:, None] <= np.arange(P)[None, :])
    c["lt_bf"] = lt.astype(ml_dtypes.bfloat16)
    return c


def layout_params(ins):
    d = {}
    d["w_in"] = np.ascontiguousarray(ins["L0_attn_in_proj_w"], np.float32)
    d["w_out"] = np.ascontiguousarray(ins["L0_attn_out_proj_w"], np.float32)
    d["w_ffn"] = np.ascontiguousarray(ins["L0_ffn_weight"], np.float32)
    d["ln_w_fm"] = np.ascontiguousarray(
        np.asarray(ins["L0_attn_ln_w"], np.float32).reshape(NKC, P).T)
    d["norm_w_fm"] = np.ascontiguousarray(
        np.asarray(ins["L0_attn_norm_w"], np.float32).reshape(NKC, P).T)
    mup = np.zeros(67 * P, np.float32)
    mup[:PROJ] = np.asarray(ins["L0_attn_mup_vector"], np.float32)
    d["mup_fm"] = np.ascontiguousarray(mup.reshape(67, P).T)
    d["conv_w_fm"] = np.ascontiguousarray(
        np.asarray(ins["L0_attn_conv_w"], np.float32)
        .reshape(NCV, P, 4).transpose(1, 0, 2).reshape(P, NCV * 4))
    d["d_fm"] = np.ascontiguousarray(
        np.repeat(np.asarray(ins["L0_attn_D"], np.float32), HD)
        .reshape(DS // P, P).T)
    d["dt_bias_c"] = np.ascontiguousarray(
        np.asarray(ins["L0_attn_dt_bias"], np.float32).reshape(NH, 1))
    d["a_log_c"] = np.ascontiguousarray(
        np.asarray(ins["L0_attn_A_log"], np.float32).reshape(NH, 1))
    d.update(_consts())
    return d


def kernel(**inputs):
    x_full = np.asarray(inputs["L0_attn_hidden_states"], np.float32)
    params = layout_params({k: np.asarray(v) for k, v in inputs.items()})
    nc, _ = build_program()
    in_maps = []
    for c in range(8):
        m = dict(params)
        m["x"] = np.ascontiguousarray(x_full[c])
        in_maps.append(m)
    res = run_bass_kernel_spmd(nc, in_maps, core_ids=list(range(8)))
    return np.stack([res.results[c]["out"] for c in range(8)], axis=0)


# version 06
# speedup vs baseline: 1.3288x; 1.0211x over previous
import numpy as np
import ml_dtypes
import concourse.bass as bass
import concourse.mybir as mybir
from concourse.bass_utils import run_bass_kernel_spmd

F32 = mybir.dt.float32
BF16 = mybir.dt.bfloat16
I32 = mybir.dt.int32
AF = mybir.ActivationFunctionType
ALU = mybir.AluOpType
AX = mybir.AxisListType

P = 128
H = 4096
DS = 4096
NH = 64
HD = 64
CONV = 4352
PROJ = 8512
L = 2048
EPS = 1e-5
QBHI = 128.0 - 1e-5

TCH = 1024
NCH = L // TCH
NTT = TCH // P
NCV = CONV // P
NKC = H // P
CVW = 3 + TCH
NJ1 = 35
NJ2 = 32
OB = 256
NOB = H // OB
NG = 8
NBLK = TCH // P


class Plan:
    ENGS = ("sy", "gp", "te", "sc", "ve")

    def __init__(self, sems):
        self.ops = {e: [] for e in self.ENGS}
        self.sems = sems
        self.cnt = {k: 0 for k in sems}
        self.last_wait = {}

    def emit(self, eng, fn):
        self.ops[eng].append(fn)

    def wait(self, eng, sem, val):
        if val is None or val <= 0:
            return
        key = (eng, sem)
        if self.last_wait.get(key, 0) >= val:
            return
        self.last_wait[key] = val
        h = self.sems[sem]
        self.emit(eng, lambda e, _h=h, _v=val: e.wait_ge(_h, _v))

    DRAIN_ENGS = ("ve",)
    ENG_OF = {"gp_c": "gp"}

    def sig(self, eng):
        self.cnt[eng] += 1
        h = self.sems[eng]
        stream = self.ENG_OF.get(eng, eng)
        if eng in self.DRAIN_ENGS:
            self.emit(stream, lambda e, _h=h: e.drain(fusable=False).then_inc(_h, 1))
        else:
            fn = self.ops[stream].pop()
            self.ops[stream].append(lambda e, _f=fn, _h=h: _f(e).then_inc(_h, 1))
        return self.cnt[eng]

    def dma(self, queue, sem, out_ap, in_ap, **kw):
        self.cnt[sem] += 16
        h = self.sems[sem]
        self.emit(queue, lambda e, _o=out_ap, _i=in_ap, _h=h:
                  e.dma_start(_o, _i, **kw).then_inc(_h, 16))
        return self.cnt[sem]


def build_program(n_dbg=0, dump_spec=()):
    nc = bass.Bass("TRN2", target_bir_lowering=False, debug=False)

    x_hbm = nc.dram_tensor("x", [L, H], F32, kind="ExternalInput").ap()
    w_in = nc.dram_tensor("w_in", [P, 67 * NKC * P], F32, kind="ExternalInput").ap()
    w_out = nc.dram_tensor("w_out", [P, 16 * NKC * OB], F32, kind="ExternalInput").ap()
    w_ffn = nc.dram_tensor("w_ffn", [H, H], F32, kind="ExternalInput").ap()
    ln_w_fm = nc.dram_tensor("ln_w_fm", [P, NKC], F32, kind="ExternalInput").ap()
    norm_w_fm = nc.dram_tensor("norm_w_fm", [P, NKC], F32, kind="ExternalInput").ap()
    mup_fm = nc.dram_tensor("mup_fm", [P, 67], F32, kind="ExternalInput").ap()
    conv_w_fm = nc.dram_tensor("conv_w_fm", [P, NCV * 4], F32, kind="ExternalInput").ap()
    d_fm = nc.dram_tensor("d_fm", [P, DS // P], F32, kind="ExternalInput").ap()
    dt_bias_c = nc.dram_tensor("dt_bias_c", [NH, 1], F32, kind="ExternalInput").ap()
    a_log_c = nc.dram_tensor("a_log_c", [NH, 1], F32, kind="ExternalInput").ap()
    id_bf = nc.dram_tensor("id_bf", [P, P], BF16, kind="ExternalInput").ap()
    id_f32 = nc.dram_tensor("id_f32", [P, 64], F32, kind="ExternalInput").ap()
    ones_r = nc.dram_tensor("ones_r", [1, P], F32, kind="ExternalInput").ap()
    ones_cf = nc.dram_tensor("ones_cf", [P, 1], F32, kind="ExternalInput").ap()
    ones_bc = nc.dram_tensor("ones_bc", [P, 1], BF16, kind="ExternalInput").ap()
    lt_bf = nc.dram_tensor("lt_bf", [P, P], BF16, kind="ExternalInput").ap()
    epsc_in = nc.dram_tensor("epsc", [P, 1], F32, kind="ExternalInput").ap()
    neg1c_in = nc.dram_tensor("neg1c", [P, 1], F32, kind="ExternalInput").ap()
    out_hbm = nc.dram_tensor("out", [L, H], F32, kind="ExternalOutput").ap()
    wqt_hbm = nc.dram_tensor("wqt_scr", [P, 16 * NKC * OB], BF16, kind="Internal").ap()
    dbg = [nc.dram_tensor(f"dbg{i}", [P, 8192], F32, kind="ExternalOutput").ap()
           for i in range(n_dbg)]
    dbg_names = []
    want = set(dump_spec)

    import contextlib
    with contextlib.ExitStack() as _st:
        en = _st.enter_context
        convin = en(nc.sbuf_tensor("convin", [P, NCV * CVW], BF16))
        xrq = en(nc.sbuf_tensor("xrq", [P, NKC * TCH], BF16))
        h_st = en(nc.sbuf_tensor("h_st", [P, NH * HD], F32))
        w1a = en(nc.sbuf_tensor("w1a", [P, NKC * P], BF16))
        w2 = en(nc.sbuf_tensor("w2", [P, 10240], BF16))
        wqa = en(nc.sbuf_tensor("wqa", [P, 8192], BF16))
        dtb = en(nc.sbuf_tensor("dtb", [P, 2 * TCH], F32))
        csts = en(nc.sbuf_tensor("csts", [P, 264], BF16))
        cstf = en(nc.sbuf_tensor("cstf", [P, 363], F32))
        stats = en(nc.sbuf_tensor("stats", [P, 96], F32))
        small = en(nc.sbuf_tensor("small", [P, 512], F32))
        resb = dtb[:, 0:512]
        pb0 = en(nc.psum_tensor("pb0", [P, 512], F32))
        pb1 = en(nc.psum_tensor("pb1", [P, 512], F32))
        pb2 = en(nc.psum_tensor("pb2", [P, 512], F32))
        pb3 = en(nc.psum_tensor("pb3", [P, 512], F32))
        pb4 = en(nc.psum_tensor("pb4", [P, 512], F32))
        pb5 = en(nc.psum_tensor("pb5", [P, 512], F32))
        pb6 = en(nc.psum_tensor("pb6", [P, 512], F32))
        pb7 = en(nc.psum_tensor("pb7", [P, 512], F32))
        s_te = en(nc.semaphore("s_te"))
        s_sc = en(nc.semaphore("s_sc"))
        s_ve = en(nc.semaphore("s_ve"))
        s_gpc = en(nc.semaphore("s_gpc"))
        s_syd = en(nc.semaphore("s_syd"))
        s_gpd = en(nc.semaphore("s_gpd"))
        s_scd = en(nc.semaphore("s_scd"))
        sems = {"te": s_te, "sc": s_sc, "ve": s_ve, "gp_c": s_gpc,
                "sy_d": s_syd, "gp_d": s_gpd, "sc_d": s_scd}
        pl = Plan(sems)
        banks = [pb0, pb1, pb2, pb3, pb4, pb5, pb6, pb7]
        bank_free = {i: ("te", 0) for i in range(8)}

        def bank_wait(eng, i):
            sem, val = bank_free[i]
            pl.wait(eng, sem, val)

        ID_BF = csts[:, 0:128]
        LT = csts[:, 128:256]
        ONES_BC = csts[:, 256:257]
        ID_F32 = cstf[:, 0:64]
        ONES_R = cstf[:, 64:192]
        ONES_CF = cstf[:, 192:193]
        LNW = cstf[:, 193:225]
        NRMW = cstf[:, 225:257]
        DFM = cstf[:, 257:289]
        MUP = cstf[:, 289:356]
        DTBI = cstf[:, 356:357]
        ALOG = cstf[:, 357:358]
        AC = cstf[:, 358:359]
        RWC = cstf[:, 359:360]
        SWC = cstf[:, 360:361]
        EPSC = cstf[:, 361:362]
        NEG1C = cstf[:, 362:363]
        CW4 = small[:, 0:136]
        ABSS = small[:, 136:168]
        BNO = small[:, 168:216]
        TC = small[:, 216:248]
        SPARE = small[:, 248:512]

        def stc(ch, tt, k):
            i = (ch * NTT + tt) * 6 + k
            return stats[:, i:i + 1]

        def vsp():
            pl.emit("ve", lambda e: e.tensor_copy(TC[:, 30:31], EPSC))

        def w2f32(off_b, n):
            return w2[:, off_b // 2: off_b // 2 + 2 * n].bitcast(F32)

        def w2bf(off_b, n):
            return w2[:, off_b // 2: off_b // 2 + n]

        def w2i32(off_b, n):
            return w2[:, off_b // 2: off_b // 2 + 2 * n].bitcast(I32)

        def wqf32(off_b, n):
            return wqa[:, off_b // 2: off_b // 2 + 2 * n].bitcast(F32)

        def wqbf(off_b, n):
            return wqa[:, off_b // 2: off_b // 2 + n]

        def cv(ft, c0, n):
            return convin[:, ft * CVW + c0: ft * CVW + c0 + n]

        def dump(tag, ap, waits):
            if len(dbg_names) >= len(dbg):
                return
            i = len(dbg_names)
            dbg_names.append(tag)
            for sem, val in waits:
                pl.wait("gp", sem, val)
            rows = ap.shape[0]
            tot = 1
            for s_ in ap.shape[1:]:
                tot *= s_
            pl.dma("gp", "gp_d", dbg[i][0:rows, 0:tot], ap)

        pl.dma("sy", "sy_d", ID_BF, id_bf)
        pl.dma("sy", "sy_d", LT, lt_bf)
        pl.dma("sy", "sy_d", ONES_BC, ones_bc)
        pl.dma("sy", "sy_d", ID_F32, id_f32)
        pl.dma("sy", "sy_d", ONES_R[0:1, 0:128], ones_r)
        pl.dma("sy", "sy_d", ONES_CF, ones_cf)
        pl.dma("sy", "sy_d", LNW, ln_w_fm)
        pl.dma("sy", "sy_d", NRMW, norm_w_fm)
        pl.dma("sy", "sy_d", DFM, d_fm)
        pl.dma("sy", "sy_d", MUP, mup_fm)
        pl.dma("sy", "sy_d", DTBI[0:NH, :], dt_bias_c)
        pl.dma("sy", "sy_d", CW4, conv_w_fm)
        pl.dma("sy", "sy_d", EPSC, epsc_in)
        pl.dma("sy", "sy_d", NEG1C, neg1c_in)
        v_ld = pl.dma("sy", "sy_d", ALOG[0:NH, :], a_log_c)
        pl.wait("sc", "sy_d", v_ld)
        pl.emit("sc", lambda e: e.activation(AC[0:NH, :], ALOG[0:NH, :], AF.Exp))
        pl.wait("ve", "sc", pl.sig("sc"))
        pl.emit("ve", lambda e: e.tensor_scalar_mul(AC[0:NH, :], AC[0:NH, :], -1.0))
        for ft in range(NCV):
            pl.emit("ve", lambda e, _f=ft: e.memset(cv(_f, 0, 3), 0.0))
        v_init = pl.sig("ve")

        v_prefill = pl.dma("sc", "sc_d", out_hbm, x_hbm)

        wl0_pre = pl.dma("gp", "gp_d",
                         w1a[:, 0:NKC * P].rearrange("p (kc j) -> p kc j", kc=NKC),
                         w_in[:, 32 * 4096:33 * 4096].rearrange(
                             "p (kc j) -> p kc j", kc=NKC))

        STG1 = [xrq[:, 0:8192].bitcast(F32), xrq[:, 8192:16384].bitcast(F32)]
        WQRs = [dtb[:, 0:2048].bitcast(BF16), wqa[:, 0:4096]]
        TSG = wqbf(8192, 4096)
        q_sig = {}
        st_sig = {}
        red_sig = {}
        for rb in range(NKC):
            buf = rb % 2
            q, qd = ("sy", "sy_d") if buf == 0 else ("gp", "gp_d")
            if rb >= 2:
                pl.wait(q, "sc", red_sig[rb - 2])
            vl = pl.dma(q, qd, STG1[buf], w_ffn[rb * P:(rb + 1) * P, :])
            pl.wait("sc", qd, vl)
            pl.emit("sc", lambda e, _b=buf, _rb=rb: e.activation(
                TSG[:, 0:4096], STG1[_b], AF.Abs, accum_out=ABSS[:, _rb:_rb + 1]))
            red_sig[rb] = pl.sig("sc")
        pl.wait("ve", "sc", red_sig[NKC - 1])
        pl.emit("ve", lambda e: e.tensor_reduce(TC[:, 0:1], ABSS, AX.X, ALU.add))
        pl.wait("te", "ve", pl.sig("ve"))
        pl.emit("te", lambda e: e.matmul(pb7[0:1, 0:1], ONES_CF, TC[:, 0:1],
                                         start=True, stop=True))
        pl.wait("sc", "te", pl.sig("te"))
        pl.emit("sc", lambda e: e.activation(TC[0:1, 1:2], pb7[0:1, 0:1], AF.Identity,
                                             scale=1.0 / float(H * H)))
        pl.wait("ve", "sc", pl.sig("sc"))
        pl.emit("ve", lambda e: e.tensor_tensor(
            TC[0:1, 1:2], TC[0:1, 1:2], EPSC[0:1, :], ALU.add))
        pl.emit("ve", lambda e: e.tensor_copy(TC[:, 30:31], EPSC))
        pl.emit("ve", lambda e: e.tensor_copy(TC[:, 30:31], EPSC))
        pl.emit("ve", lambda e: e.reciprocal(TC[0:1, 2:3], TC[0:1, 1:2]))
        pl.wait("te", "ve", pl.sig("ve"))
        pl.emit("te", lambda e: e.matmul(pb7[0:P, 2:3], ONES_R[0:1, 0:128],
                                         TC[0:1, 2:3], start=True, stop=True))
        pl.emit("te", lambda e: e.matmul(pb7[0:P, 3:4], ONES_R[0:1, 0:128],
                                         TC[0:1, 1:2], start=False, stop=True,
                                         skip_group_check=True))
        pl.wait("sc", "te", pl.sig("te"))
        pl.emit("sc", lambda e: e.activation(RWC, pb7[0:P, 2:3], AF.Identity))
        pl.emit("sc", lambda e: e.activation(SWC, pb7[0:P, 3:4], AF.Identity))
        v_rwc = pl.sig("sc")
        pl.wait("ve", "sc", v_rwc)
        pl.wait("gp", "sc", v_rwc)
        bank_free[7] = ("sc", v_rwc)

        for rb in range(NKC):
            buf = rb % 2
            q, qd = ("sy", "sy_d") if buf == 0 else ("gp", "gp_d")
            if rb >= 2:
                pl.wait(q, "ve", q_sig[rb - 2])
            else:
                pl.wait(q, "sc", red_sig[NKC - 2 + rb])
            vl = pl.dma(q, qd, STG1[buf], w_ffn[rb * P:(rb + 1) * P, :])
            pl.wait("ve", qd, vl)
            if rb >= 2:
                pl.wait("ve", "te", st_sig[rb - 2][0])
            wqr = WQRs[buf]
            st = STG1[buf]
            pl.emit("ve", lambda e, _s=st: e.tensor_scalar_mul(_s, _s, RWC))
            pl.emit("ve", lambda e, _s=st: e.tensor_copy(_s.bitcast(I32), _s))
            pl.emit("ve", lambda e, _s=st, _w=wqr: e.tensor_scalar(
                _w, _s.bitcast(I32), 1.0, -1.0, ALU.min, ALU.max))
            q_sig[rb] = pl.sig("ve")
            pl.wait("te", "ve", q_sig[rb])
            if rb >= 1:
                pl.wait("te", "sy_d", st_sig[rb - 1][1])
            last_te = 0
            v_e = 0
            for half in range(4):
                bk = 4 + (half % 2)
                bank_wait("te", bk)
                pbk = banks[bk][:].bitcast(BF16)
                for k in range(8):
                    kc = half * 8 + k
                    pl.emit("te", lambda e, _p=pbk, _k=k, _kc=kc, _w=wqr:
                            e.transpose(_p[:, _k * P:(_k + 1) * P],
                                        _w[:, _kc * P:(_kc + 1) * P], ID_BF))
                v_t = pl.sig("te")
                last_te = v_t
                pl.wait("sc", "te", v_t)
                pl.emit("sc", lambda e, _p=pbk, _h=half: e.activation(
                    TSG[:, _h * 1024:(_h + 1) * 1024], _p, AF.Identity))
                v_e = pl.sig("sc")
                bank_free[bk] = ("sc", v_e)
            pl.wait("sy", "sc", v_e)
            dst = wqt_hbm[:, (rb // 2) * 8192:(rb // 2 + 1) * 8192].rearrange(
                "p (kc o) -> p kc o", kc=NKC)[:, :, (rb % 2) * P:(rb % 2) * P + P]
            v_st = pl.dma("sy", "sy_d", dst,
                          TSG[:].rearrange("p (kc o) -> p kc o", kc=NKC))
            st_sig[rb] = (last_te, v_st)
        v_prep_te = pl.cnt["te"]
        v_prep_store = pl.cnt["sy_d"]
        if "wqt" in want:
            dump("wqt", wqt_hbm[0:P, 0:4096], [("sy_d", v_prep_store)])
            dump("prepsc", small[:, 0:512], [("sy_d", v_prep_store)])
        pl.wait("gp", "ve", q_sig[NKC - 1])
        pl.emit("gp", lambda e: e.memset(h_st[:], 0.0))
        v_hmem = pl.sig("gp_c")

        def tm_to_fm(src_tm, tt):
            last = 0
            last_te = 0
            for half in range(4):
                bk = 4 + (half % 2)
                bank_wait("te", bk)
                pbk = banks[bk][:].bitcast(BF16)
                for k in range(8):
                    kc = half * 8 + k
                    pl.emit("te", lambda e, _p=pbk, _k=k, _kc=kc, _s=src_tm:
                            e.transpose(_p[:, _k * P:(_k + 1) * P],
                                        _s[:, _kc * P:(_kc + 1) * P], ID_BF))
                v_t = pl.sig("te")
                last_te = v_t
                pl.wait("sc", "te", v_t)
                dstv = xrq[:].rearrange("p (kc t) -> p kc t", kc=NKC)[
                    :, half * 8:(half + 1) * 8, tt * P:(tt + 1) * P]
                pl.emit("sc", lambda e, _p=pbk, _d=dstv: e.activation(
                    _d, _p, AF.Identity))
                v_e = pl.sig("sc")
                bank_free[bk] = ("sc", v_e)
                last = v_e
            return last, last_te

        XA = w2f32(0, 4096)
        XRT = wqbf(0, 4096)
        wbufs = [w1a, w2bf(12288, 4096)]
        w_last = {0: 0, 1: 0}
        h_upd_last = 0
        v_f2_te_last = 0
        v_f2_ve_last = 0
        v_f2_gpd_last = 0
        res_slot = {0: 0, 1: 0}
        a_done_prev = 0
        blk_te = 0
        blk_ve = 0

        for ch in range(NCH):
            t0 = ch * TCH

            a_done = {}
            v_xrt_prev = 0
            v_tm_te_prev = 0
            pl.wait("sc", "te", v_f2_te_last)
            pl.wait("sc", "ve", v_f2_ve_last)
            if ch == 0:
                pl.wait("sc", "ve", q_sig[NKC - 1])
                pl.wait("sc", "te", st_sig[NKC - 1][0])
            for tt in range(NTT):
                xa = XA
                xrt = XRT
                vl = pl.dma("sc", "sc_d", xa,
                            x_hbm[t0 + tt * P: t0 + (tt + 1) * P, :])
                pl.wait("ve", "sc_d", vl)
                pl.wait("ve", "sc", v_xrt_prev)
                for sl in range(8):
                    pl.emit("ve", lambda e, _x=xa, _s=sl: e.bn_stats(
                        BNO[:, _s * 6:(_s + 1) * 6], _x[:, _s * 512:(_s + 1) * 512]))
                pl.emit("ve", lambda e: e.bn_aggr(TC[:, 4:6], BNO))
                pl.emit("ve", lambda e, _x=xa: e.tensor_reduce(
                    TC[:, 7:8], _x, AX.X, ALU.max))
                pl.emit("ve", lambda e: e.tensor_tensor(
                    TC[:, 6:7], TC[:, 5:6], EPSC, ALU.add))
                pl.emit("ve", lambda e, _x=xa: e.tensor_reduce(
                    TC[:, 8:9], _x, AX.X, ALU.min))
                pl.emit("ve", lambda e: e.reciprocal(TC[:, 14:15], TC[:, 6:7]))
                vsp()
                pl.emit("ve", lambda e: e.tensor_sub(TC[:, 10:11], TC[:, 7:8], TC[:, 4:5]))
                vsp()
                pl.emit("ve", lambda e: e.tensor_sub(TC[:, 11:12], TC[:, 4:5], TC[:, 8:9]))
                vsp()
                pl.emit("ve", lambda e: e.tensor_max(TC[:, 10:11], TC[:, 10:11], TC[:, 11:12]))
                vsp()
                pl.emit("ve", lambda e: e.tensor_mul(TC[:, 15:16], TC[:, 5:6], TC[:, 14:15]))
                vsp()
                pl.emit("ve", lambda e: e.tensor_tensor(
                    TC[:, 15:16], TC[:, 15:16], EPSC, ALU.add))
                v_st1 = pl.sig("ve")
                pl.wait("sc", "ve", v_st1)
                pl.emit("sc", lambda e: e.activation(TC[:, 9:10], TC[:, 6:7], AF.Ln))
                pl.emit("sc", lambda e, _c=ch, _t=tt: e.activation(
                    stc(_c, _t, 1), TC[:, 9:10], AF.Exp, scale=-0.5))
                pl.emit("sc", lambda e: e.activation(TC[:, 16:17], TC[:, 15:16], AF.Ln))
                pl.emit("sc", lambda e: e.activation(TC[:, 16:17], TC[:, 16:17],
                                                     AF.Exp, scale=-0.5))
                v_sc1 = pl.sig("sc")
                pl.wait("ve", "sc", v_sc1)
                pl.emit("ve", lambda e, _c=ch, _t=tt: e.tensor_copy(
                    stc(_c, _t, 0), TC[:, 4:5]))
                vsp()
                pl.emit("ve", lambda e, _c=ch, _t=tt: e.scalar_tensor_tensor(
                    TC[:, 12:13], TC[:, 10:11], 1.0, stc(_c, _t, 1),
                    ALU.mult, ALU.mult))
                vsp()
                pl.emit("ve", lambda e: e.tensor_tensor(
                    TC[:, 13:14], TC[:, 12:13], EPSC, ALU.add))
                vsp()
                pl.emit("ve", lambda e, _c=ch, _t=tt: e.tensor_mul(
                    TC[:, 17:18], stc(_c, _t, 1), TC[:, 16:17]))
                vsp()
                pl.emit("ve", lambda e: e.reciprocal(TC[:, 19:20], TC[:, 13:14]))
                vsp()
                pl.emit("ve", lambda e, _c=ch, _t=tt: e.tensor_scalar(
                    stc(_c, _t, 3), TC[:, 13:14], SWC, 1.0 / 128.0,
                    ALU.mult, ALU.mult))
                vsp()
                pl.emit("ve", lambda e, _c=ch, _t=tt: e.tensor_scalar_mul(
                    stc(_c, _t, 2), TC[:, 19:20], 128.0))
                vsp()
                pl.emit("ve", lambda e: e.tensor_mul(TC[:, 18:19], TC[:, 4:5], TC[:, 17:18]))
                vsp()
                pl.emit("ve", lambda e: e.tensor_tensor(
                    TC[:, 18:19], TC[:, 18:19], NEG1C, ALU.mult))
                vsp()
                v_ve3 = pl.sig("ve")
                if "a0" in want and ch == 0 and tt == 0:
                    dump("a0", small[:, 0:512], [("ve", v_ve3)])
                    dump("a0s", stats[:, 0:96], [("ve", v_ve3)])
                pl.wait("sc", "ve", v_ve3)
                pl.wait("sc", "te", v_tm_te_prev)
                pl.emit("sc", lambda e, _x=xa, _r=xrt: e.activation(
                    _r, _x, AF.Identity, scale=TC[:, 17:18], bias=TC[:, 18:19]))
                v_xrt_prev = pl.sig("sc")
                if "xrt0" in want and ch == 0 and tt == 0:
                    dump("xrt0", xrt[:, 0:4096], [("sc", v_xrt_prev)])
                pl.wait("te", "sc", v_xrt_prev)
                a_done[tt], v_tm_te_prev = tm_to_fm(xrt, tt)
            a_done_prev = a_done[NTT - 1]
            w_last[1] = v_tm_te_prev
            pl.wait("ve", "sc", a_done_prev)
            for kc in range(NKC):
                pl.emit("ve", lambda e, _k=kc: e.tensor_scalar_mul(
                    xrq[:, _k * TCH:(_k + 1) * TCH],
                    xrq[:, _k * TCH:(_k + 1) * TCH], LNW[:, _k:_k + 1]))
            v_xr_ready = pl.sig("ve")
            if "xr" in want and ch == 0:
                dump("xr", xrq[:, 0:8192], [("ve", v_xr_ready)])

            CACC = [w2f32(0, 1024), w2f32(4096, 1024)]
            mm_done = {}
            ev_done = {}
            cv_done = {}
            cv_ve = {}
            wl = {}
            pl.wait("te", "ve", v_xr_ready)

            def b_load(jt):
                if ch == 0 and jt == 0:
                    wl[0] = wl0_pre
                    return
                buf = jt % 2
                if jt >= 2:
                    pl.wait("gp", "te", mm_done[jt - 2])
                else:
                    pl.wait("gp", "te", w_last[buf])
                g = 32 + jt
                src = w_in[:, g * 4096:(g + 1) * 4096].rearrange(
                    "p (kc j) -> p kc j", kc=NKC)
                dstw = wbufs[buf][:, 0:NKC * P].rearrange(
                    "p (kc j) -> p kc j", kc=NKC)
                wl[jt] = pl.dma("gp", "gp_d", dstw, src)

            b_load(0)
            b_load(1)
            for jt in range(NJ1):
                width = P if jt < 34 else 64
                wb = wbufs[jt % 2]
                pl.wait("te", "gp_d", wl[jt])
                for tg in range(2):
                    bk = (2 * jt + tg) % 6
                    bank_wait("te", bk)
                    outp = banks[bk][0:width, 0:512]
                    for kc in range(NKC):
                        pl.emit("te", lambda e, _o=outp, _w=wb, _kc=kc, _tg=tg,
                                _wd=width: e.matmul(
                            _o, _w[:, _kc * P:_kc * P + _wd],
                            xrq[:, _kc * TCH + _tg * 512: _kc * TCH + (_tg + 1) * 512],
                            start=(_kc == 0), stop=(_kc == NKC - 1)))
                    v_mm = pl.sig("te")
                    if tg == 1:
                        mm_done[jt] = v_mm
                        w_last[jt % 2] = v_mm
                    pl.wait("sc", "te", v_mm)
                    if jt < 34:
                        pl.emit("sc", lambda e, _b=banks[bk], _jt=jt, _tg=tg:
                                e.activation(cv(_jt, 3 + _tg * 512, 512),
                                             _b[:, 0:512], AF.Identity,
                                             scale=MUP[:, 32 + _jt:33 + _jt]))
                    else:
                        pl.wait("sc", "te", v_prep_te)
                        dtr_t = w2f32(8192, 1024)
                        pl.emit("sc", lambda e, _b=banks[bk], _tg=tg, _d=dtr_t:
                                e.activation(
                                    _d[0:NH, _tg * 512:(_tg + 1) * 512],
                                    _b[0:NH, 0:512], AF.Identity,
                                    scale=MUP[0:NH, 66:67]))
                    v_ev = pl.sig("sc")
                    bank_free[bk] = ("sc", v_ev)
                    ev_done[(jt, tg)] = v_ev
                if jt + 2 < NJ1:
                    b_load(jt + 2)
                if jt < 34:
                    pl.wait("ve", "sc", ev_done[(jt, 1)])
                    if jt >= 2:
                        pl.wait("ve", "sc", cv_done.get(jt - 2, 0))
                    acc = CACC[jt % 2]
                    pl.emit("ve", lambda e, _j=jt, _a=acc: e.tensor_scalar_mul(
                        _a, cv(_j, 0, TCH), CW4[:, _j * 4:_j * 4 + 1]))
                    for k in range(1, 4):
                        pl.emit("ve", lambda e, _j=jt, _a=acc, _k=k:
                                e.scalar_tensor_tensor(
                                    _a, cv(_j, _k, TCH),
                                    CW4[:, _j * 4 + _k:_j * 4 + _k + 1], _a,
                                    ALU.mult, ALU.add))
                    pl.emit("ve", lambda e, _j=jt: e.tensor_copy(
                        cv(_j, 0, 3), cv(_j, TCH, 3)))
                    cv_ve[jt] = pl.sig("ve")
                lag = jt - 1
                if 0 <= lag < 34:
                    pl.wait("sc", "ve", cv_ve[lag])
                    pl.emit("sc", lambda e, _j=lag, _a=CACC[lag % 2]: e.activation(
                        cv(_j, 3, TCH), _a, AF.Silu))
                    cv_done[lag] = pl.sig("sc")
                if jt == NJ1 - 1:
                    pl.wait("sc", "ve", cv_ve[33])
                    pl.emit("sc", lambda e, _a=CACC[33 % 2]: e.activation(
                        cv(33, 3, TCH), _a, AF.Silu))
                    cv_done[33] = pl.sig("sc")
            DTP = dtb[0:NH, 0:TCH]
            SEG = dtb[0:NH, TCH:2 * TCH]
            DTR = w2f32(8192, 1024)[0:NH, :]
            pl.wait("sc", "gp_d", v_f2_gpd_last)
            pl.wait("ve", "gp_d", v_f2_gpd_last)
            pl.emit("sc", lambda e: e.activation(DTR, DTR, AF.Identity, bias=DTBI[0:NH, :]))
            pl.emit("sc", lambda e: e.activation(SEG, DTR, AF.Abs))
            pl.emit("sc", lambda e: e.activation(SEG, SEG, AF.Exp, scale=-1.0))
            pl.emit("sc", lambda e: e.activation(SEG, SEG, AF.Ln, bias=1.0))
            v_sp = pl.sig("sc")
            pl.wait("ve", "sc", v_sp)
            pl.emit("ve", lambda e: e.tensor_scalar_max(DTP, DTR, 0.0))
            pl.emit("ve", lambda e: e.tensor_add(DTP, DTP, SEG))
            pl.emit("ve", lambda e: e.tensor_mul(
                DTR, DTP, AC[0:NH, :].broadcast_to([NH, TCH])))
            for qb in range(NBLK):
                pl.emit("ve", lambda e, _q=qb: e.tensor_tensor_scan(
                    SEG[:, _q * P:(_q + 1) * P], DTR[:, _q * P:(_q + 1) * P],
                    DTR[:, _q * P:(_q + 1) * P], 0.0, ALU.add, ALU.bypass))
            v_seg = pl.sig("ve")
            if "convout" in want and ch == 0:
                dump("convout", convin[:, 0:8192], [("sc", cv_done[33])])
            if "dt" in want and ch == 0:
                dump("dt", dtb[0:NH, 0:2048], [("ve", v_seg)])

            XT = w2bf(0, 4096)
            E8 = [w2bf(8192, 1024), w2bf(10240, 1024)]
            M8 = E8
            Y2S = [w2f32(12288, 512), w2f32(14336, 512)]
            XW = [w2bf(16384, 512), w2bf(17408, 512)]
            YTM = [w2bf(18432, 512), w2bf(19456, 512)]
            HBF = [wqbf(0, 512), wqbf(1024, 512)]
            CBM = wqbf(2048, 128)
            BT = wqbf(2304, 128)
            SEGT = wqf32(2560, 64)
            DTT = wqf32(2816, 64)
            WST = wqf32(3072, 64)
            ESGT = wqf32(3328, 64)
            ENDR = wqf32(3584, 64)
            B64 = wqf32(3840, 64)
            SGC = [wqf32(4096, 1024), wqf32(8192, 1024)]
            WSB = SPARE[0:NH, 0:128]
            ESGF = SPARE[0:NH, 128:256]

            for qb in range(NBLK):
                tok = 3 + qb * P
                segb = SEG[:, qb * P:(qb + 1) * P]
                dtpb = DTP[:, qb * P:(qb + 1) * P]
                wsb = WSB
                pl.wait("ve", "te", blk_te)
                pl.emit("ve", lambda e, _s=segb, _w=wsb, _q=qb: e.tensor_scalar(
                    _w, _s, SEG[0:NH, (_q + 1) * P - 1:(_q + 1) * P], None,
                    ALU.subtract))
                v_w1 = pl.sig("ve")
                pl.wait("sc", "ve", v_w1)
                pl.wait("sc", "te", blk_te)
                pl.emit("sc", lambda e, _w=wsb: e.activation(_w, _w, AF.Exp, scale=-1.0))
                v_w2s = pl.sig("sc")
                pl.wait("ve", "sc", v_w2s)
                pl.emit("ve", lambda e, _w=wsb, _d=dtpb: e.tensor_mul(_w, _w, _d))
                v_w3 = pl.sig("ve")
                pl.wait("sc", "ve", v_w3)
                pl.emit("sc", lambda e, _s=segb: e.activation(
                    ESGF, _s, AF.Exp))
                v_esg = pl.sig("sc")
                bank_wait("te", 6)
                bank_wait("te", 7)
                pl.wait("te", "sc", v_esg)
                pb7b = pb7[:].bitcast(BF16)
                pl.emit("te", lambda e, _s=segb: e.transpose(
                    pb7[:, 64:128], _s, ID_F32[0:64, 0:64]))
                pl.emit("te", lambda e, _d=dtpb: e.transpose(
                    pb7[:, 128:192], _d, ID_F32[0:64, 0:64]))
                pl.emit("te", lambda e, _w=wsb: e.transpose(
                    pb7[:, 192:256], _w, ID_F32[0:64, 0:64]))
                pl.emit("te", lambda e: e.transpose(
                    pb7[:, 256:320], ESGF, ID_F32[0:64, 0:64]))
                pl.emit("te", lambda e: e.transpose(
                    pb7[0:1, 320:384], ESGF[:, P - 1:P], ID_F32[0:64, 0:64]))
                pl.emit("te", lambda e, _t=tok: e.matmul(
                    pb6[:, 0:128], cv(32, _t, P), cv(33, _t, P),
                    start=True, stop=True))
                v_pre1 = pl.sig("te")
                pl.wait("sc", "te", v_pre1)
                pl.emit("sc", lambda e: e.activation(SEGT, pb7[:, 64:128], AF.Identity))
                pl.emit("sc", lambda e: e.activation(DTT, pb7[:, 128:192], AF.Identity))
                pl.emit("sc", lambda e: e.activation(WST, pb7[:, 192:256], AF.Identity))
                pl.emit("sc", lambda e: e.activation(ESGT, pb7[:, 256:320], AF.Identity))
                pl.emit("sc", lambda e: e.activation(
                    ENDR[0:1, 0:64], pb7[0:1, 320:384], AF.Identity))
                v_pre2 = pl.sig("sc")
                pl.wait("ve", "te", v_pre1)
                pl.emit("ve", lambda e: e.tensor_tensor(CBM, pb6[:, 0:128], LT, ALU.mult))
                v_cbm = pl.sig("ve")
                bank_free[6] = ("ve", v_cbm)
                pl.wait("te", "sc", v_pre2)
                pl.emit("te", lambda e: e.matmul(
                    pb7[:, 384:448], ONES_R[0:1, 0:128], ENDR[0:1, 0:64],
                    start=False, stop=True, skip_group_check=True))
                pl.emit("te", lambda e, _t=tok: e.transpose(
                    pb7b[:, 896:1024], cv(32, _t, P), ID_BF))
                v_pre3 = pl.sig("te")
                pl.wait("sc", "te", v_pre3)
                pl.emit("sc", lambda e: e.activation(B64, pb7[:, 384:448], AF.Identity))
                pl.emit("sc", lambda e: e.activation(BT, pb7b[:, 896:1024], AF.Identity))
                v_pre4 = pl.sig("sc")
                bank_free[7] = ("sc", v_pre4)
                v_xt = 0
                for half in range(4):
                    bk = 4 + (half % 2)
                    bank_wait("te", bk)
                    pbk = banks[bk][:].bitcast(BF16)
                    for k in range(8):
                        ft = half * 8 + k
                        pl.emit("te", lambda e, _p=pbk, _k=k, _f=ft, _t=tok:
                                e.transpose(_p[:, _k * P:(_k + 1) * P],
                                            cv(_f, _t, P), ID_BF))
                    v_t = pl.sig("te")
                    pl.wait("sc", "te", v_t)
                    pl.emit("sc", lambda e, _p=pbk, _h=half: e.activation(
                        XT[:, _h * 1024:(_h + 1) * 1024], _p, AF.Identity))
                    v_xt = pl.sig("sc")
                    bank_free[bk] = ("sc", v_xt)

                relu_sig = {}
                m8_sig = {}
                xw_sig = {}
                hbf_sig = {}
                yfin_sig = {}
                hupd_sig = {}
                ywr_sig = {}
                rmm_sig = {}
                ymm_sig = {}
                stm_sig = {}
                yt_sig = {}
                for g in range(NG):
                    pr = g % 2
                    rbA, rbB = (0, 1) if pr == 0 else (2, 3)
                    y1b = 4 + pr
                    y2b = 6 + pr
                    if g >= 2:
                        pl.wait("sc", "te", rmm_sig[g - 2])
                    seg_rows = dtb[g * 8:(g + 1) * 8, TCH + qb * P: TCH + (qb + 1) * P]
                    pl.wait("sc", "sy_d", v_prep_store)
                    v_sgc = pl.dma(
                        "sc", "sc_d",
                        SGC[pr][0:1, :].rearrange("p (a b) -> p a b", a=8),
                        seg_rows)
                    pl.wait("te", "sc_d", v_sgc)
                    if g >= 2:
                        pl.wait("te", "ve", hupd_sig[g - 2])
                        pl.wait("te", "ve", ywr_sig[g - 2])
                    else:
                        bank_wait("te", rbA)
                        bank_wait("te", rbB)
                    pl.emit("te", lambda e, _b=banks[rbA], _p=pr: e.matmul(
                        _b[:, 0:512], ONES_R[0:1, 0:128], SGC[_p][0:1, 0:512],
                        start=True, stop=True))
                    pl.emit("te", lambda e, _b=banks[rbB], _p=pr: e.matmul(
                        _b[:, 0:512], ONES_R[0:1, 0:128], SGC[_p][0:1, 512:1024],
                        start=True, stop=True))
                    rmm_sig[g] = pl.sig("te")
                    pl.wait("sc", "te", rmm_sig[g])
                    if g >= 2:
                        pl.wait("sc", "ve", m8_sig[g - 2])
                    for rl in range(8):
                        bsel = banks[rbA] if rl < 4 else banks[rbB]
                        pl.emit("sc", lambda e, _b=bsel, _r=rl, _g=g:
                                e.activation(
                                    _b[:, (_r % 4) * P:((_r % 4) + 1) * P],
                                    _b[:, (_r % 4) * P:((_r % 4) + 1) * P],
                                    AF.Relu, scale=-1.0,
                                    bias=SEGT[:, _g * 8 + _r:_g * 8 + _r + 1]))
                    pl.emit("sc", lambda e, _p=pr, _bA=banks[rbA]: e.activation(
                        E8[_p][:, 0:512], _bA[:, 0:512], AF.Exp, scale=-1.0))
                    pl.emit("sc", lambda e, _p=pr, _bB=banks[rbB]: e.activation(
                        E8[_p][:, 512:1024], _bB[:, 0:512], AF.Exp, scale=-1.0))
                    relu_sig[g] = pl.sig("sc")
                    pl.wait("ve", "sc", relu_sig[g])
                    pl.emit("ve", lambda e, _p=pr: e.tensor_tensor(
                        E8[_p], E8[_p],
                        CBM.unsqueeze(1).broadcast_to([P, 8, P]), ALU.mult))
                    pl.emit("ve", lambda e, _p=pr, _g=g: e.tensor_tensor(
                        M8[_p], E8[_p],
                        DTT[:, _g * 8:(_g + 1) * 8].unsqueeze(2)
                        .broadcast_to([P, 8, P]), ALU.mult))
                    pl.emit("ve", lambda e, _p=pr, _g=g: e.tensor_tensor(
                        XW[_p], XT[:, _g * 512:(_g + 1) * 512],
                        WST[:, _g * 8:(_g + 1) * 8].unsqueeze(2)
                        .broadcast_to([P, 8, HD]), ALU.mult))
                    m8_sig[g] = xw_sig[g] = pl.sig("ve")
                    if g >= 1:
                        pl.wait("sc", "ve", hupd_sig[g - 1])
                    else:
                        pl.wait("sc", "ve", h_upd_last)
                        pl.wait("sc", "gp_c", v_hmem)
                    pl.emit("sc", lambda e, _p=pr, _g=g: e.activation(
                        HBF[_p], h_st[:, _g * 512:(_g + 1) * 512], AF.Identity))
                    hbf_sig[g] = pl.sig("sc")
                    pl.wait("te", "ve", m8_sig[g])
                    pl.wait("te", "sc", hbf_sig[g])
                    if g >= 2:
                        pl.wait("te", "ve", yfin_sig[g - 2])
                    for rl in range(8):
                        pl.emit("te", lambda e, _b=banks[y1b], _p=pr, _r=rl, _g=g:
                                e.matmul(_b[:, _r * HD:(_r + 1) * HD],
                                         M8[_p][:, _r * P:(_r + 1) * P],
                                         XT[:, (_g * 8 + _r) * HD:(_g * 8 + _r + 1) * HD],
                                         start=(_r == 0), stop=(_r == 7),
                                         skip_group_check=True))
                    for rl in range(8):
                        pl.emit("te", lambda e, _b=banks[y2b], _p=pr, _r=rl, _t=tok:
                                e.matmul(_b[:, _r * HD:(_r + 1) * HD],
                                         cv(33, _t, P),
                                         HBF[_p][:, _r * HD:(_r + 1) * HD],
                                         start=(_r == 0), stop=(_r == 7),
                                         skip_group_check=True))
                    ymm_sig[g] = pl.sig("te")
                    pl.wait("te", "ve", xw_sig[g])
                    for rl in range(8):
                        pl.emit("te", lambda e, _b=banks[rbA], _p=pr, _r=rl:
                                e.matmul(_b[:, _r * HD:(_r + 1) * HD],
                                         BT, XW[_p][:, _r * HD:(_r + 1) * HD],
                                         start=(_r == 0), stop=(_r == 7),
                                         skip_group_check=True))
                    stm_sig[g] = pl.sig("te")
                    pl.wait("ve", "te", ymm_sig[g])
                    pl.emit("ve", lambda e, _p=pr, _g=g, _b=banks[y2b]:
                            e.tensor_tensor(Y2S[_p], _b[:, 0:512],
                                            ESGT[:, _g * 8:(_g + 1) * 8]
                                            .unsqueeze(2).broadcast_to([P, 8, HD]),
                                            ALU.mult))
                    pl.emit("ve", lambda e, _p=pr, _b=banks[y1b]:
                            e.tensor_tensor(YTM[_p], _b[:, 0:512], Y2S[_p], ALU.add))
                    yfin_sig[g] = pl.sig("ve")
                    pl.wait("ve", "te", stm_sig[g])
                    pl.emit("ve", lambda e, _p=pr, _g=g: e.tensor_tensor(
                        Y2S[_p], h_st[:, _g * 512:(_g + 1) * 512],
                        B64[:, _g * 8:(_g + 1) * 8].unsqueeze(2)
                        .broadcast_to([P, 8, HD]), ALU.mult))
                    pl.emit("ve", lambda e, _p=pr, _g=g, _b=banks[rbA]:
                            e.tensor_tensor(h_st[:, _g * 512:(_g + 1) * 512],
                                            Y2S[_p], _b[:, 0:512], ALU.add))
                    pl.wait("te", "ve", yfin_sig[g])
                    rbB_bf = banks[rbB][:].bitcast(BF16)
                    for k in range(4):
                        pl.emit("te", lambda e, _b=rbB_bf, _p=pr, _k=k:
                                e.transpose(_b[:, 512 + _k * P: 512 + (_k + 1) * P],
                                            YTM[_p][:, _k * P:(_k + 1) * P], ID_BF))
                    yt_sig[g] = pl.sig("te")
                    pl.wait("ve", "te", yt_sig[g])
                    for k in range(4):
                        ft = g * 4 + k
                        pl.emit("ve", lambda e, _b=rbB_bf, _k=k, _f=ft, _t=tok:
                                e.scalar_tensor_tensor(
                                    cv(_f, _t, P), cv(_f, _t, P),
                                    DFM[:, _f:_f + 1],
                                    _b[:, 512 + _k * P: 512 + (_k + 1) * P],
                                    ALU.mult, ALU.add))
                    hupd_sig[g] = ywr_sig[g] = pl.sig("ve")
                h_upd_last = hupd_sig[NG - 1]
                blk_te = yt_sig[NG - 1]
                blk_ve = ywr_sig[NG - 1]
                bank_free[0] = ("ve", hupd_sig[NG - 2])
                bank_free[1] = ("ve", ywr_sig[NG - 2])
                bank_free[2] = ("ve", hupd_sig[NG - 1])
                bank_free[3] = ("ve", ywr_sig[NG - 1])
                bank_free[4] = ("ve", yfin_sig[NG - 2])
                bank_free[5] = ("ve", yfin_sig[NG - 1])
                bank_free[6] = ("ve", yfin_sig[NG - 2])
                bank_free[7] = ("ve", yfin_sig[NG - 1])
            if "y" in want and ch == 0:
                dump("y", convin[:, 0:8192], [("ve", blk_ve)])

            SZ = [w2bf(0, 512), w2bf(1024, 512)]
            GY2 = [w2bf(2048, 512), w2bf(3072, 512)]
            SRMS = w2f32(4096, 1024)
            ez_mm = {}
            gy_sig = {}
            g2_sig = {}
            sqs = {}
            ezl = {}
            def e_load(jt):
                buf = jt % 2
                if jt >= 2:
                    pl.wait("gp", "te", ez_mm[jt - 2])
                elif jt == 1:
                    pl.wait("gp", "te", blk_te)
                    pl.wait("gp", "ve", blk_ve)
                else:
                    pl.wait("gp", "te", w_last[buf])
                src = w_in[:, jt * 4096:(jt + 1) * 4096].rearrange(
                    "p (kc j) -> p kc j", kc=NKC)
                dstw = wbufs[buf][:].rearrange("p (kc j) -> p kc j", kc=NKC)
                ezl[jt] = pl.dma("gp", "gp_d", dstw, src)

            e_load(0)
            e_load(1)
            for jt in range(NJ2):
                wb = wbufs[jt % 2]
                pl.wait("te", "gp_d", ezl[jt])
                pl.wait("te", "ve", blk_ve)
                for tg in range(2):
                    bk = (2 * jt + tg) % 4
                    bank_wait("te", bk)
                    for kc in range(NKC):
                        pl.emit("te", lambda e, _b=banks[bk], _w=wb, _kc=kc, _tg=tg:
                                e.matmul(_b[:, 0:512],
                                         _w[:, _kc * P:(_kc + 1) * P],
                                         xrq[:, _kc * TCH + _tg * 512: _kc * TCH + (_tg + 1) * 512],
                                         start=(_kc == 0), stop=(_kc == NKC - 1)))
                    v_mm = pl.sig("te")
                    if tg == 1:
                        ez_mm[jt] = v_mm
                        w_last[jt % 2] = v_mm
                        if jt + 2 < NJ2:
                            e_load(jt + 2)
                    pl.wait("sc", "te", v_mm)
                    pl.wait("sc", "ve", gy_sig.get((jt - 1, tg), 0))
                    pl.emit("sc", lambda e, _b=banks[bk], _jt=jt, _tg=tg:
                            e.activation(SZ[_tg], _b[:, 0:512], AF.Silu,
                                         scale=MUP[:, _jt:_jt + 1]))
                    v_sz = pl.sig("sc")
                    bank_free[bk] = ("sc", v_sz)
                    pl.wait("ve", "sc", v_sz)
                    pl.emit("ve", lambda e, _jt=jt, _tg=tg: e.tensor_tensor(
                        cv(_jt, 3 + _tg * 512, 512), cv(_jt, 3 + _tg * 512, 512),
                        SZ[_tg], ALU.mult))
                    gy_sig[(jt, tg)] = pl.sig("ve")
                    pl.wait("sc", "ve", gy_sig[(jt, tg)])
                    pl.wait("sc", "te", sqs.get((jt - 1, tg), 0))
                    pl.emit("sc", lambda e, _jt=jt, _tg=tg: e.activation(
                        GY2[_tg], cv(_jt, 3 + _tg * 512, 512), AF.Square))
                    g2_sig[(jt, tg)] = pl.sig("sc")
                    pl.wait("te", "sc", g2_sig[(jt, tg)])
                    sbk = 4 + tg
                    if jt == 0:
                        bank_wait("te", sbk)
                    pl.emit("te", lambda e, _b=banks[sbk], _tg=tg, _jt=jt:
                            e.matmul(_b[0:1, 0:512], ONES_BC, GY2[_tg],
                                     start=(_jt == 0), stop=(_jt == NJ2 - 1),
                                     skip_group_check=True))
                    sqs[(jt, tg)] = pl.sig("te")
                    pl.wait("ve", "sc", g2_sig[(jt, tg)])
                    pl.emit("ve", lambda e, _jt=jt, _tg=tg: e.tensor_scalar_mul(
                        cv(_jt, 3 + _tg * 512, 512), cv(_jt, 3 + _tg * 512, 512),
                        NRMW[:, _jt:_jt + 1]))
            pl.wait("sc", "te", sqs[(NJ2 - 1, 1)])
            pl.emit("sc", lambda e: e.activation(SRMS[0:1, 0:512], pb4[0:1, 0:512], AF.Identity))
            pl.emit("sc", lambda e: e.activation(SRMS[0:1, 512:1024], pb5[0:1, 0:512], AF.Identity))
            v_sr = pl.sig("sc")
            bank_free[4] = ("sc", v_sr)
            bank_free[5] = ("sc", v_sr)
            pl.wait("ve", "sc", v_sr)
            pl.emit("ve", lambda e: e.tensor_scalar(
                SRMS[0:1, 0:1024], SRMS[0:1, 0:1024], 1.0 / DS, EPS,
                ALU.mult, ALU.add))
            v_sr2 = pl.sig("ve")
            pl.wait("sc", "ve", v_sr2)
            pl.emit("sc", lambda e: e.activation(SRMS[0:1, 0:1024], SRMS[0:1, 0:1024], AF.Ln))
            pl.emit("sc", lambda e: e.activation(SRMS[0:1, 0:1024], SRMS[0:1, 0:1024],
                                                 AF.Exp, scale=-0.5))
            v_sr3 = pl.sig("sc")
            pl.wait("te", "sc", v_sr3)
            bank_wait("te", 6)
            for tt in range(NTT):
                pl.emit("te", lambda e, _t=tt: e.matmul(
                    pb6[:, _t:_t + 1], SRMS[0:1, _t * P:(_t + 1) * P],
                    ONES_R[0:1, 0:1], start=(_t == 0), stop=(_t == NTT - 1),
                    skip_group_check=True))
            v_scol = pl.sig("te")
            pl.wait("sc", "te", v_scol)
            for tt in range(NTT):
                pl.emit("sc", lambda e, _t=tt, _c=ch: e.activation(
                    stc(_c, _t, 4), pb6[:, _t:_t + 1], AF.Identity))
            v_scols = pl.sig("sc")
            bank_free[6] = ("sc", v_scols)
            if "gy" in want and ch == 0:
                dump("gy", convin[:, 0:8192], [("sc", v_scols)])
            if "srms" in want and ch == 0:
                dump("srms", stats[:, 0:96], [("sc", v_scols)])

            XNQ = w2f32(0, 4096)
            XQT = wqbf(0, 4096)
            f1_done = 0
            v_f1b_prev = 0
            v_f1c_prev = 0
            v_f1_te_tm = 0
            pl.wait("sy", "sc", v_scols)
            for tt in range(NTT):
                pl.wait("sy", "ve", v_f1c_prev)
                vl = pl.dma("sy", "sy_d", XNQ,
                            x_hbm[t0 + tt * P: t0 + (tt + 1) * P, :])
                pl.wait("ve", "sy_d", vl)
                pl.wait("ve", "sc", v_f1b_prev)
                pl.emit("ve", lambda e, _c=ch, _t=tt: e.tensor_mul(
                    TC[:, 20:21], stc(_c, _t, 1), stc(_c, _t, 2)))
                vsp()
                pl.emit("ve", lambda e, _c=ch, _t=tt: e.tensor_mul(
                    TC[:, 21:22], stc(_c, _t, 0), TC[:, 20:21]))
                vsp()
                pl.emit("ve", lambda e: e.tensor_tensor(
                    TC[:, 21:22], TC[:, 21:22], NEG1C, ALU.mult))
                vsp()
                v_f1a = pl.sig("ve")
                pl.wait("sc", "ve", v_f1a)
                pl.emit("sc", lambda e: e.activation(
                    XNQ, XNQ, AF.Identity, scale=TC[:, 20:21], bias=TC[:, 21:22]))
                v_f1b_prev = pl.sig("sc")
                pl.wait("ve", "sc", v_f1b_prev)
                pl.wait("ve", "te", v_f1_te_tm)
                pl.emit("ve", lambda e: e.tensor_copy(XNQ.bitcast(I32), XNQ))
                pl.emit("ve", lambda e: e.tensor_copy(XNQ, XNQ.bitcast(I32)))
                pl.emit("ve", lambda e: e.tensor_scalar(
                    XQT[:, 0:4096], XNQ, QBHI, -QBHI, ALU.min, ALU.max))
                v_f1c_prev = pl.sig("ve")
                pl.wait("te", "ve", v_f1c_prev)
                f1_done, v_f1_te_tm = tm_to_fm(XQT[:, 0:4096], tt)
            if "xq" in want and ch == 0:
                dump("xq", xrq[:, 0:8192], [("sc", f1_done)])
            v_f1_te = pl.cnt["te"]
            v_f1_ve = pl.cnt["ve"]
            v_f1_sc = f1_done

            WOBUF = w2bf(0, 8192)
            WQB = wqbf(0, 8192)
            GBV = SPARE[:].bitcast(BF16)
            GB = [GBV[:, 0:256], GBV[:, 256:512]]
            ffn_mm = {}
            wo_used = {}
            gb_busy = {0: 0, 1: 0}
            for ob in range(NOB):
                if ob >= 1:
                    pl.wait("sy", "te", ffn_mm[ob - 1])
                else:
                    pl.wait("sy", "sc", v_f1_sc)
                src = wqt_hbm[:, ob * 8192:(ob + 1) * 8192].rearrange(
                    "p (kc o) -> p kc o", kc=NKC)
                v_wq = pl.dma("sy", "sy_d",
                              WQB[:].rearrange("p (kc o) -> p kc o", kc=NKC), src)
                if ob >= 1:
                    pl.wait("gp", "te", wo_used[ob - 1])
                else:
                    pl.wait("gp", "ve", v_f1_ve)
                    pl.wait("gp", "te", v_f1_te)
                srco = w_out[:, ob * 8192:(ob + 1) * 8192].rearrange(
                    "p (kc o) -> p kc o", kc=NKC)
                v_wo = pl.dma("gp", "gp_d",
                              WOBUF[:].rearrange("p (kc o) -> p kc o", kc=NKC),
                              srco)
                if "wqb" in want and ch == 0 and ob == 0:
                    dump("wqb", WQB[:, 0:8192], [("sy_d", v_wq)])
                    dump("wqt1", wqt_hbm[P:2 * P, 0:4096], [("sy_d", v_wq)])
                pl.wait("te", "sy_d", v_wq)
                for tt in range(NTT):
                    bk = 2 + (tt % 2)
                    bank_wait("te", bk)
                    for kc in range(NKC):
                        pl.emit("te", lambda e, _b=banks[bk], _kc=kc, _t=tt:
                                e.matmul(_b[:, 0:OB],
                                         xrq[:, _kc * TCH + _t * P: _kc * TCH + (_t + 1) * P],
                                         WQB[:, _kc * OB:(_kc + 1) * OB],
                                         start=(_kc == 0), stop=(_kc == NKC - 1)))
                    v_fmm = pl.sig("te")
                    if tt == NTT - 1:
                        ffn_mm[ob] = v_fmm
                    pl.wait("sc", "te", v_fmm)
                    if "psb00" in want and ch == 0 and ob == 0 and tt == 0:
                        pl.emit("sc", lambda e, _b=banks[bk]: e.activation(
                            w2f32(16384, 256), _b[:, 0:OB], AF.Identity))
                        dump("psb00", w2f32(16384, 256), [("sc", pl.sig("sc"))])
                    pl.wait("sc", "ve", gb_busy[tt % 2])
                    pl.emit("sc", lambda e, _b=banks[bk], _t=tt, _c=ch, _g=GB[tt % 2]:
                            e.activation(_g, _b[:, 0:OB], AF.Silu,
                                         scale=stc(_c, _t, 3)))
                    v_gb = pl.sig("sc")
                    bank_free[bk] = ("sc", v_gb)
                    pl.wait("te", "gp_d", v_wo)
                    bko = tt % 2
                    bank_wait("te", bko)
                    for kc in range(NKC):
                        pl.emit("te", lambda e, _b=banks[bko], _kc=kc, _t=tt:
                                e.matmul(_b[:, 0:OB],
                                         cv(_kc, 3 + _t * P, P),
                                         WOBUF[:, _kc * OB:(_kc + 1) * OB],
                                         start=(_kc == 0), stop=(_kc == NKC - 1)))
                    v_omm = pl.sig("te")
                    if tt == NTT - 1:
                        wo_used[ob] = v_omm
                    pl.wait("ve", "te", v_omm)
                    pl.wait("ve", "sc", v_gb)
                    pl.wait("ve", "gp_d", res_slot[tt % 2])
                    rslot = resb[:, (tt % 2) * OB:(tt % 2 + 1) * OB]
                    if "only_f" in want:
                        pl.emit("ve", lambda e, _b=banks[bk], _t=tt, _c=ch,
                                _r=rslot: e.scalar_tensor_tensor(
                            _r, _b[:, 0:OB], stc(_c, _t, 3),
                            ONES_CF.broadcast_to([P, OB]), ALU.mult, ALU.mult))
                    elif "only_s" in want:
                        pl.emit("ve", lambda e, _b=banks[bko], _t=tt, _c=ch,
                                _r=rslot: e.scalar_tensor_tensor(
                            _r, _b[:, 0:OB], stc(_c, _t, 4),
                            ONES_CF.broadcast_to([P, OB]), ALU.mult, ALU.mult))
                    elif "only_g" in want:
                        pl.emit("ve", lambda e, _b=banks[bko], _t=tt, _c=ch,
                                _g=GB[tt % 2], _r=rslot: e.scalar_tensor_tensor(
                            _r, _b[:, 0:OB], 0.0, _g, ALU.mult, ALU.add))
                    else:
                        pl.emit("ve", lambda e, _b=banks[bko], _t=tt, _c=ch,
                                _g=GB[tt % 2], _r=rslot: e.scalar_tensor_tensor(
                            _r, _b[:, 0:OB], stc(_c, _t, 4), _g, ALU.mult, ALU.mult))
                    v_res = pl.sig("ve")
                    gb_busy[tt % 2] = v_res
                    bank_free[bko] = ("ve", v_res)
                    pl.wait("gp", "ve", v_res)
                    pl.wait("gp", "sc_d", v_prefill)
                    dstout = out_hbm[t0 + tt * P: t0 + (tt + 1) * P,
                                     ob * OB:(ob + 1) * OB]
                    res_slot[tt % 2] = pl.dma("gp", "gp_d", dstout, rslot,
                                              accum_op=ALU.add)
            v_f2_te_last = pl.cnt["te"]
            v_f2_ve_last = pl.cnt["ve"]
            v_f2_gpd_last = pl.cnt["gp_d"]

        pl.emit("gp", lambda e: e.wait_ge(s_gpd, pl.cnt["gp_d"]))
        pl.emit("sy", lambda e: e.wait_ge(s_syd, pl.cnt["sy_d"]))
        pl.emit("sc", lambda e: e.wait_ge(s_scd, pl.cnt["sc_d"]))

        for s in sems.values():
            nc.sync.sem_clear(s)
        nc.all_engine_barrier()
        with nc.Block() as block:
            @block.sync
            def _(e):
                for fn in pl.ops["sy"]:
                    fn(e)

            @block.gpsimd
            def _(e):
                for fn in pl.ops["gp"]:
                    fn(e)

            @block.tensor
            def _(e):
                for fn in pl.ops["te"]:
                    fn(e)

            @block.scalar
            def _(e):
                for fn in pl.ops["sc"]:
                    fn(e)

            @block.vector
            def _(e):
                for fn in pl.ops["ve"]:
                    fn(e)

    return nc, dbg_names


def _consts():
    c = {}
    c["id_bf"] = np.eye(P, dtype=ml_dtypes.bfloat16)
    c["id_f32"] = np.ascontiguousarray(np.eye(P, dtype=np.float32)[:, 0:64])
    c["ones_r"] = np.ones((1, P), np.float32)
    c["ones_cf"] = np.ones((P, 1), np.float32)
    c["ones_bc"] = np.ones((P, 1), ml_dtypes.bfloat16)
    c["epsc"] = np.full((P, 1), 1e-5, np.float32)
    c["neg1c"] = np.full((P, 1), -1.0, np.float32)
    lt = (np.arange(P)[:, None] <= np.arange(P)[None, :])
    c["lt_bf"] = lt.astype(ml_dtypes.bfloat16)
    return c


def layout_params(ins):
    d = {}
    wi = np.zeros((H, 67 * P), np.float32)
    wi[:, :PROJ] = np.asarray(ins["L0_attn_in_proj_w"], np.float32)
    d["w_in"] = np.ascontiguousarray(
        wi.reshape(NKC, P, 67, P).transpose(1, 2, 0, 3)).reshape(P, 67 * NKC * P)
    wo = np.asarray(ins["L0_attn_out_proj_w"], np.float32)
    d["w_out"] = np.ascontiguousarray(
        wo.reshape(NKC, P, 16, OB).transpose(1, 2, 0, 3)).reshape(P, 16 * NKC * OB)
    d["w_ffn"] = np.ascontiguousarray(ins["L0_ffn_weight"], np.float32)
    d["ln_w_fm"] = np.ascontiguousarray(
        np.asarray(ins["L0_attn_ln_w"], np.float32).reshape(NKC, P).T)
    d["norm_w_fm"] = np.ascontiguousarray(
        np.asarray(ins["L0_attn_norm_w"], np.float32).reshape(NKC, P).T)
    mup = np.zeros(67 * P, np.float32)
    mup[:PROJ] = np.asarray(ins["L0_attn_mup_vector"], np.float32)
    d["mup_fm"] = np.ascontiguousarray(mup.reshape(67, P).T)
    d["conv_w_fm"] = np.ascontiguousarray(
        np.asarray(ins["L0_attn_conv_w"], np.float32)
        .reshape(NCV, P, 4).transpose(1, 0, 2).reshape(P, NCV * 4))
    d["d_fm"] = np.ascontiguousarray(
        np.repeat(np.asarray(ins["L0_attn_D"], np.float32), HD)
        .reshape(DS // P, P).T)
    d["dt_bias_c"] = np.ascontiguousarray(
        np.asarray(ins["L0_attn_dt_bias"], np.float32).reshape(NH, 1))
    d["a_log_c"] = np.ascontiguousarray(
        np.asarray(ins["L0_attn_A_log"], np.float32).reshape(NH, 1))
    d.update(_consts())
    return d


def kernel(**inputs):
    x_full = np.asarray(inputs["L0_attn_hidden_states"], np.float32)
    params = layout_params({k: np.asarray(v) for k, v in inputs.items()})
    nc, _ = build_program()
    in_maps = []
    for c in range(8):
        m = dict(params)
        m["x"] = np.ascontiguousarray(x_full[c])
        in_maps.append(m)
    res = run_bass_kernel_spmd(nc, in_maps, core_ids=list(range(8)))
    return np.stack([res.results[c]["out"] for c in range(8)], axis=0)


# version 09 (final)
# speedup vs baseline: 1.4613x; 1.0997x over previous
import numpy as np
import ml_dtypes
import concourse.bass as bass
import concourse.mybir as mybir
from concourse.bass_utils import run_bass_kernel_spmd

F32 = mybir.dt.float32
BF16 = mybir.dt.bfloat16
I32 = mybir.dt.int32
AF = mybir.ActivationFunctionType
ALU = mybir.AluOpType
AX = mybir.AxisListType

P = 128
H = 4096
DS = 4096
NH = 64
HD = 64
CONV = 4352
PROJ = 8512
L = 2048
EPS = 1e-5
QBHI = 128.0 - 1e-5

TCH = 1024
NCH = L // TCH
NTT = TCH // P
NCV = CONV // P
NKC = H // P
CVW = 3 + TCH
NJ1 = 35
NJ2 = 32
OB = 256
NOB = H // OB
NG = 8
NBLK = TCH // P


class Plan:
    ENGS = ("sy", "gp", "te", "sc", "ve")

    def __init__(self, sems):
        self.ops = {e: [] for e in self.ENGS}
        self.sems = sems
        self.cnt = {k: 0 for k in sems}
        self.last_wait = {}

    def emit(self, eng, fn):
        self.ops[eng].append(fn)

    def wait(self, eng, sem, val):
        if val is None or val <= 0:
            return
        key = (eng, sem)
        if self.last_wait.get(key, 0) >= val:
            return
        self.last_wait[key] = val
        h = self.sems[sem]
        self.emit(eng, lambda e, _h=h, _v=val: e.wait_ge(_h, _v))

    DRAIN_ENGS = ("ve",)
    ENG_OF = {"gp_c": "gp"}

    def sig(self, eng):
        self.cnt[eng] += 1
        h = self.sems[eng]
        stream = self.ENG_OF.get(eng, eng)
        if eng in self.DRAIN_ENGS:
            self.emit(stream, lambda e, _h=h: e.drain(fusable=False).then_inc(_h, 1))
        else:
            fn = self.ops[stream].pop()
            self.ops[stream].append(lambda e, _f=fn, _h=h: _f(e).then_inc(_h, 1))
        return self.cnt[eng]

    def dma(self, queue, sem, out_ap, in_ap, **kw):
        self.cnt[sem] += 16
        h = self.sems[sem]
        self.emit(queue, lambda e, _o=out_ap, _i=in_ap, _h=h:
                  e.dma_start(_o, _i, **kw).then_inc(_h, 16))
        return self.cnt[sem]


def build_program(n_dbg=0, dump_spec=()):
    nc = bass.Bass("TRN2", target_bir_lowering=False, debug=False)

    x_hbm = nc.dram_tensor("x", [L, H], F32, kind="ExternalInput").ap()
    w_in = nc.dram_tensor("w_in", [P, 67 * NKC * P], F32, kind="ExternalInput").ap()
    w_out = nc.dram_tensor("w_out", [P, 16 * NKC * OB], F32, kind="ExternalInput").ap()
    w_ffn = nc.dram_tensor("w_ffn", [H, H], F32, kind="ExternalInput").ap()
    ln_w_fm = nc.dram_tensor("ln_w_fm", [P, NKC], F32, kind="ExternalInput").ap()
    norm_w_fm = nc.dram_tensor("norm_w_fm", [P, NKC], F32, kind="ExternalInput").ap()
    mup_fm = nc.dram_tensor("mup_fm", [P, 67], F32, kind="ExternalInput").ap()
    conv_w_fm = nc.dram_tensor("conv_w_fm", [P, NCV * 4], F32, kind="ExternalInput").ap()
    d_fm = nc.dram_tensor("d_fm", [P, DS // P], F32, kind="ExternalInput").ap()
    dt_bias_c = nc.dram_tensor("dt_bias_c", [NH, 1], F32, kind="ExternalInput").ap()
    a_log_c = nc.dram_tensor("a_log_c", [NH, 1], F32, kind="ExternalInput").ap()
    id_bf = nc.dram_tensor("id_bf", [P, P], BF16, kind="ExternalInput").ap()
    id_f32 = nc.dram_tensor("id_f32", [P, 64], F32, kind="ExternalInput").ap()
    ones_r = nc.dram_tensor("ones_r", [1, P], F32, kind="ExternalInput").ap()
    ones_cf = nc.dram_tensor("ones_cf", [P, 1], F32, kind="ExternalInput").ap()
    ones_bc = nc.dram_tensor("ones_bc", [P, 1], BF16, kind="ExternalInput").ap()
    lt_bf = nc.dram_tensor("lt_bf", [P, P], BF16, kind="ExternalInput").ap()
    epsc_in = nc.dram_tensor("epsc", [P, 1], F32, kind="ExternalInput").ap()
    neg1c_in = nc.dram_tensor("neg1c", [P, 1], F32, kind="ExternalInput").ap()
    out_hbm = nc.dram_tensor("out", [L, H], F32, kind="ExternalOutput").ap()
    wqt_hbm = nc.dram_tensor("wqt_scr", [P, 16 * NKC * OB], BF16, kind="Internal").ap()
    dbg = [nc.dram_tensor(f"dbg{i}", [P, 8192], F32, kind="ExternalOutput").ap()
           for i in range(n_dbg)]
    dbg_names = []
    want = set(dump_spec)

    import contextlib
    with contextlib.ExitStack() as _st:
        en = _st.enter_context
        convin = en(nc.sbuf_tensor("convin", [P, NCV * CVW], BF16))
        xrq = en(nc.sbuf_tensor("xrq", [P, NKC * TCH], BF16))
        h_st = en(nc.sbuf_tensor("h_st", [P, NH * HD], F32))
        w1a = en(nc.sbuf_tensor("w1a", [P, NKC * P], BF16))
        w2 = en(nc.sbuf_tensor("w2", [P, 10240], BF16))
        wqa = en(nc.sbuf_tensor("wqa", [P, 8192], BF16))
        dtb = en(nc.sbuf_tensor("dtb", [P, 2 * TCH], F32))
        csts = en(nc.sbuf_tensor("csts", [P, 264], BF16))
        cstf = en(nc.sbuf_tensor("cstf", [P, 363], F32))
        stats = en(nc.sbuf_tensor("stats", [P, 96], F32))
        small = en(nc.sbuf_tensor("small", [P, 512], F32))
        resb = dtb[:, 0:512]
        pb0 = en(nc.psum_tensor("pb0", [P, 512], F32))
        pb1 = en(nc.psum_tensor("pb1", [P, 512], F32))
        pb2 = en(nc.psum_tensor("pb2", [P, 512], F32))
        pb3 = en(nc.psum_tensor("pb3", [P, 512], F32))
        pb4 = en(nc.psum_tensor("pb4", [P, 512], F32))
        pb5 = en(nc.psum_tensor("pb5", [P, 512], F32))
        pb6 = en(nc.psum_tensor("pb6", [P, 512], F32))
        pb7 = en(nc.psum_tensor("pb7", [P, 512], F32))
        s_te = en(nc.semaphore("s_te"))
        s_sc = en(nc.semaphore("s_sc"))
        s_ve = en(nc.semaphore("s_ve"))
        s_gpc = en(nc.semaphore("s_gpc"))
        s_syd = en(nc.semaphore("s_syd"))
        s_gpd = en(nc.semaphore("s_gpd"))
        s_scd = en(nc.semaphore("s_scd"))
        sems = {"te": s_te, "sc": s_sc, "ve": s_ve, "gp_c": s_gpc,
                "sy_d": s_syd, "gp_d": s_gpd, "sc_d": s_scd}
        pl = Plan(sems)
        banks = [pb0, pb1, pb2, pb3, pb4, pb5, pb6, pb7]
        bank_free = {i: ("te", 0) for i in range(8)}

        def bank_wait(eng, i):
            sem, val = bank_free[i]
            pl.wait(eng, sem, val)

        ID_BF = csts[:, 0:128]
        LT = csts[:, 128:256]
        ONES_BC = csts[:, 256:257]
        ID_F32 = cstf[:, 0:64]
        ONES_R = cstf[:, 64:192]
        ONES_CF = cstf[:, 192:193]
        LNW = cstf[:, 193:225]
        NRMW = cstf[:, 225:257]
        DFM = cstf[:, 257:289]
        MUP = cstf[:, 289:356]
        DTBI = cstf[:, 356:357]
        ALOG = cstf[:, 357:358]
        AC = cstf[:, 358:359]
        RWC = cstf[:, 359:360]
        SWC = cstf[:, 360:361]
        EPSC = cstf[:, 361:362]
        NEG1C = cstf[:, 362:363]
        CW4 = small[:, 0:136]
        ABSS = small[:, 136:168]
        BNO = small[:, 168:216]
        TC = small[:, 216:248]
        SPARE = small[:, 248:512]

        def stc(ch, tt, k):
            i = (ch * NTT + tt) * 6 + k
            return stats[:, i:i + 1]

        def vsp():
            pl.emit("ve", lambda e: e.tensor_copy(TC[:, 30:31], EPSC))

        def w2f32(off_b, n):
            return w2[:, off_b // 2: off_b // 2 + 2 * n].bitcast(F32)

        def w2bf(off_b, n):
            return w2[:, off_b // 2: off_b // 2 + n]

        def w2i32(off_b, n):
            return w2[:, off_b // 2: off_b // 2 + 2 * n].bitcast(I32)

        def wqf32(off_b, n):
            return wqa[:, off_b // 2: off_b // 2 + 2 * n].bitcast(F32)

        def wqbf(off_b, n):
            return wqa[:, off_b // 2: off_b // 2 + n]

        def cv(ft, c0, n):
            return convin[:, ft * CVW + c0: ft * CVW + c0 + n]

        def dump(tag, ap, waits):
            if len(dbg_names) >= len(dbg):
                return
            i = len(dbg_names)
            dbg_names.append(tag)
            for sem, val in waits:
                pl.wait("gp", sem, val)
            rows = ap.shape[0]
            tot = 1
            for s_ in ap.shape[1:]:
                tot *= s_
            pl.dma("gp", "gp_d", dbg[i][0:rows, 0:tot], ap)

        pl.dma("sy", "sy_d", ID_BF, id_bf)
        pl.dma("sy", "sy_d", LT, lt_bf)
        pl.dma("sy", "sy_d", ONES_BC, ones_bc)
        pl.dma("sy", "sy_d", ID_F32, id_f32)
        pl.dma("sy", "sy_d", ONES_R[0:1, 0:128], ones_r)
        pl.dma("sy", "sy_d", ONES_CF, ones_cf)
        pl.dma("sy", "sy_d", LNW, ln_w_fm)
        pl.dma("sy", "sy_d", NRMW, norm_w_fm)
        pl.dma("sy", "sy_d", DFM, d_fm)
        pl.dma("sy", "sy_d", MUP, mup_fm)
        pl.dma("sy", "sy_d", DTBI[0:NH, :], dt_bias_c)
        pl.dma("sy", "sy_d", CW4, conv_w_fm)
        pl.dma("sy", "sy_d", EPSC, epsc_in)
        pl.dma("sy", "sy_d", NEG1C, neg1c_in)
        v_ld = pl.dma("sy", "sy_d", ALOG[0:NH, :], a_log_c)
        pl.wait("sc", "sy_d", v_ld)
        pl.emit("sc", lambda e: e.activation(AC[0:NH, :], ALOG[0:NH, :], AF.Exp))
        pl.wait("ve", "sc", pl.sig("sc"))
        pl.emit("ve", lambda e: e.tensor_scalar_mul(AC[0:NH, :], AC[0:NH, :], -1.0))
        for ft in range(NCV):
            pl.emit("ve", lambda e, _f=ft: e.memset(cv(_f, 0, 3), 0.0))
        v_init = pl.sig("ve")

        v_prefill = pl.dma("sc", "sc_d", out_hbm, x_hbm)

        wl0_pre = pl.dma("gp", "gp_d",
                         w1a[:, 0:NKC * P].rearrange("p (kc j) -> p kc j", kc=NKC),
                         w_in[:, 32 * 4096:33 * 4096].rearrange(
                             "p (kc j) -> p kc j", kc=NKC))

        STG1 = [xrq[:, 0:8192].bitcast(F32), xrq[:, 8192:16384].bitcast(F32)]
        WQRs = [dtb[:, 0:2048].bitcast(BF16), wqa[:, 0:4096]]
        TSG = wqbf(8192, 4096)
        q_sig = {}
        st_sig = {}
        red_sig = {}
        for rb in range(NKC):
            buf = rb % 2
            q, qd = ("sy", "sy_d") if buf == 0 else ("gp", "gp_d")
            if rb >= 2:
                pl.wait(q, "sc", red_sig[rb - 2])
            vl = pl.dma(q, qd, STG1[buf], w_ffn[rb * P:(rb + 1) * P, :])
            pl.wait("sc", qd, vl)
            pl.emit("sc", lambda e, _b=buf, _rb=rb: e.activation(
                TSG[:, 0:4096], STG1[_b], AF.Abs, accum_out=ABSS[:, _rb:_rb + 1]))
            red_sig[rb] = pl.sig("sc")
        pl.wait("ve", "sc", red_sig[NKC - 1])
        pl.emit("ve", lambda e: e.tensor_reduce(TC[:, 0:1], ABSS, AX.X, ALU.add))
        pl.wait("te", "ve", pl.sig("ve"))
        pl.emit("te", lambda e: e.matmul(pb7[0:1, 0:1], ONES_CF, TC[:, 0:1],
                                         start=True, stop=True))
        pl.wait("sc", "te", pl.sig("te"))
        pl.emit("sc", lambda e: e.activation(TC[0:1, 1:2], pb7[0:1, 0:1], AF.Identity,
                                             scale=1.0 / float(H * H)))
        pl.wait("ve", "sc", pl.sig("sc"))
        pl.emit("ve", lambda e: e.tensor_tensor(
            TC[0:1, 1:2], TC[0:1, 1:2], EPSC[0:1, :], ALU.add))
        pl.emit("ve", lambda e: e.tensor_copy(TC[:, 30:31], EPSC))
        pl.emit("ve", lambda e: e.tensor_copy(TC[:, 30:31], EPSC))
        pl.emit("ve", lambda e: e.reciprocal(TC[0:1, 2:3], TC[0:1, 1:2]))
        pl.wait("te", "ve", pl.sig("ve"))
        pl.emit("te", lambda e: e.matmul(pb7[0:P, 2:3], ONES_R[0:1, 0:128],
                                         TC[0:1, 2:3], start=True, stop=True))
        pl.emit("te", lambda e: e.matmul(pb7[0:P, 3:4], ONES_R[0:1, 0:128],
                                         TC[0:1, 1:2], start=False, stop=True,
                                         skip_group_check=True))
        pl.wait("sc", "te", pl.sig("te"))
        pl.emit("sc", lambda e: e.activation(RWC, pb7[0:P, 2:3], AF.Identity))
        pl.emit("sc", lambda e: e.activation(SWC, pb7[0:P, 3:4], AF.Identity))
        v_rwc = pl.sig("sc")
        pl.wait("ve", "sc", v_rwc)
        pl.wait("gp", "sc", v_rwc)
        bank_free[7] = ("sc", v_rwc)

        for rb in range(NKC):
            buf = rb % 2
            q, qd = ("sy", "sy_d") if buf == 0 else ("gp", "gp_d")
            if rb >= 2:
                pl.wait(q, "ve", q_sig[rb - 2])
            else:
                pl.wait(q, "sc", red_sig[NKC - 2 + rb])
            vl = pl.dma(q, qd, STG1[buf], w_ffn[rb * P:(rb + 1) * P, :])
            pl.wait("ve", qd, vl)
            if rb >= 2:
                pl.wait("ve", "te", st_sig[rb - 2][0])
            wqr = WQRs[buf]
            st = STG1[buf]
            pl.emit("ve", lambda e, _s=st: e.tensor_scalar_mul(_s, _s, RWC))
            pl.emit("ve", lambda e, _s=st: e.tensor_copy(_s.bitcast(I32), _s))
            pl.emit("ve", lambda e, _s=st, _w=wqr: e.tensor_scalar(
                _w, _s.bitcast(I32), 1.0, -1.0, ALU.min, ALU.max))
            q_sig[rb] = pl.sig("ve")
            pl.wait("te", "ve", q_sig[rb])
            if rb >= 1:
                pl.wait("te", "sy_d", st_sig[rb - 1][1])
            last_te = 0
            v_e = 0
            for half in range(4):
                bk = 4 + (half % 2)
                bank_wait("te", bk)
                pbk = banks[bk][:].bitcast(BF16)
                for k in range(8):
                    kc = half * 8 + k
                    pl.emit("te", lambda e, _p=pbk, _k=k, _kc=kc, _w=wqr:
                            e.transpose(_p[:, _k * P:(_k + 1) * P],
                                        _w[:, _kc * P:(_kc + 1) * P], ID_BF))
                v_t = pl.sig("te")
                last_te = v_t
                pl.wait("sc", "te", v_t)
                pl.emit("sc", lambda e, _p=pbk, _h=half: e.activation(
                    TSG[:, _h * 1024:(_h + 1) * 1024], _p, AF.Identity))
                v_e = pl.sig("sc")
                bank_free[bk] = ("sc", v_e)
            pl.wait("sy", "sc", v_e)
            dst = wqt_hbm[:, (rb // 2) * 8192:(rb // 2 + 1) * 8192].rearrange(
                "p (kc o) -> p kc o", kc=NKC)[:, :, (rb % 2) * P:(rb % 2) * P + P]
            v_st = pl.dma("sy", "sy_d", dst,
                          TSG[:].rearrange("p (kc o) -> p kc o", kc=NKC))
            st_sig[rb] = (last_te, v_st)
        v_prep_te = pl.cnt["te"]
        v_prep_store = pl.cnt["sy_d"]
        if "wqt" in want:
            dump("wqt", wqt_hbm[0:P, 0:4096], [("sy_d", v_prep_store)])
            dump("prepsc", small[:, 0:512], [("sy_d", v_prep_store)])
        pl.wait("gp", "ve", q_sig[NKC - 1])
        pl.emit("gp", lambda e: e.memset(h_st[:], 0.0))
        v_hmem = pl.sig("gp_c")

        def tm_to_fm(src_tm, tt):
            last = 0
            last_te = 0
            for half in range(4):
                bk = 4 + (half % 2)
                bank_wait("te", bk)
                pbk = banks[bk][:].bitcast(BF16)
                for k in range(8):
                    kc = half * 8 + k
                    pl.emit("te", lambda e, _p=pbk, _k=k, _kc=kc, _s=src_tm:
                            e.transpose(_p[:, _k * P:(_k + 1) * P],
                                        _s[:, _kc * P:(_kc + 1) * P], ID_BF))
                v_t = pl.sig("te")
                last_te = v_t
                pl.wait("sc", "te", v_t)
                dstv = xrq[:].rearrange("p (kc t) -> p kc t", kc=NKC)[
                    :, half * 8:(half + 1) * 8, tt * P:(tt + 1) * P]
                pl.emit("sc", lambda e, _p=pbk, _d=dstv: e.activation(
                    _d, _p, AF.Identity))
                v_e = pl.sig("sc")
                bank_free[bk] = ("sc", v_e)
                last = v_e
            return last, last_te

        XA = w2f32(0, 4096)
        XRT = wqbf(0, 4096)
        wbufs = [w1a, w2bf(12288, 4096)]
        w_last = {0: 0, 1: 0}
        h_upd_last = 0
        v_f2_te_last = 0
        v_f2_ve_last = 0
        v_f2_gpd_last = 0
        res_slot = {0: 0, 1: 0}
        a_done_prev = 0
        blk_te = 0
        blk_ve = 0

        for ch in range(NCH):
            t0 = ch * TCH

            a_done = {}
            v_xrt_prev = 0
            v_tm_te_prev = 0
            pl.wait("sc", "te", v_f2_te_last)
            pl.wait("sc", "ve", v_f2_ve_last)
            if ch == 0:
                pl.wait("sc", "ve", q_sig[NKC - 1])
                pl.wait("sc", "te", st_sig[NKC - 1][0])
            for tt in range(NTT):
                xa = XA
                xrt = XRT
                vl = pl.dma("sc", "sc_d", xa,
                            x_hbm[t0 + tt * P: t0 + (tt + 1) * P, :])
                pl.wait("ve", "sc_d", vl)
                pl.wait("ve", "sc", v_xrt_prev)
                for sl in range(8):
                    pl.emit("ve", lambda e, _x=xa, _s=sl: e.bn_stats(
                        BNO[:, _s * 6:(_s + 1) * 6], _x[:, _s * 512:(_s + 1) * 512]))
                pl.emit("ve", lambda e: e.bn_aggr(TC[:, 4:6], BNO))
                pl.emit("ve", lambda e, _x=xa: e.tensor_reduce(
                    TC[:, 7:8], _x, AX.X, ALU.max))
                pl.emit("ve", lambda e: e.tensor_tensor(
                    TC[:, 6:7], TC[:, 5:6], EPSC, ALU.add))
                pl.emit("ve", lambda e, _x=xa: e.tensor_reduce(
                    TC[:, 8:9], _x, AX.X, ALU.min))
                pl.emit("ve", lambda e: e.reciprocal(TC[:, 14:15], TC[:, 6:7]))
                vsp()
                pl.emit("ve", lambda e: e.tensor_sub(TC[:, 10:11], TC[:, 7:8], TC[:, 4:5]))
                vsp()
                pl.emit("ve", lambda e: e.tensor_sub(TC[:, 11:12], TC[:, 4:5], TC[:, 8:9]))
                vsp()
                pl.emit("ve", lambda e: e.tensor_max(TC[:, 10:11], TC[:, 10:11], TC[:, 11:12]))
                vsp()
                pl.emit("ve", lambda e: e.tensor_mul(TC[:, 15:16], TC[:, 5:6], TC[:, 14:15]))
                vsp()
                pl.emit("ve", lambda e: e.tensor_tensor(
                    TC[:, 15:16], TC[:, 15:16], EPSC, ALU.add))
                v_st1 = pl.sig("ve")
                pl.wait("sc", "ve", v_st1)
                pl.emit("sc", lambda e: e.activation(TC[:, 9:10], TC[:, 6:7], AF.Ln))
                pl.emit("sc", lambda e, _c=ch, _t=tt: e.activation(
                    stc(_c, _t, 1), TC[:, 9:10], AF.Exp, scale=-0.5))
                pl.emit("sc", lambda e: e.activation(TC[:, 16:17], TC[:, 15:16], AF.Ln))
                pl.emit("sc", lambda e: e.activation(TC[:, 16:17], TC[:, 16:17],
                                                     AF.Exp, scale=-0.5))
                v_sc1 = pl.sig("sc")
                pl.wait("ve", "sc", v_sc1)
                pl.emit("ve", lambda e, _c=ch, _t=tt: e.tensor_copy(
                    stc(_c, _t, 0), TC[:, 4:5]))
                vsp()
                pl.emit("ve", lambda e, _c=ch, _t=tt: e.scalar_tensor_tensor(
                    TC[:, 12:13], TC[:, 10:11], 1.0, stc(_c, _t, 1),
                    ALU.mult, ALU.mult))
                vsp()
                pl.emit("ve", lambda e: e.tensor_tensor(
                    TC[:, 13:14], TC[:, 12:13], EPSC, ALU.add))
                vsp()
                pl.emit("ve", lambda e, _c=ch, _t=tt: e.tensor_mul(
                    TC[:, 17:18], stc(_c, _t, 1), TC[:, 16:17]))
                vsp()
                pl.emit("ve", lambda e: e.reciprocal(TC[:, 19:20], TC[:, 13:14]))
                vsp()
                pl.emit("ve", lambda e, _c=ch, _t=tt: e.tensor_scalar(
                    stc(_c, _t, 3), TC[:, 13:14], SWC, 1.0 / 128.0,
                    ALU.mult, ALU.mult))
                vsp()
                pl.emit("ve", lambda e, _c=ch, _t=tt: e.tensor_scalar_mul(
                    stc(_c, _t, 2), TC[:, 19:20], 128.0))
                vsp()
                pl.emit("ve", lambda e: e.tensor_mul(TC[:, 18:19], TC[:, 4:5], TC[:, 17:18]))
                vsp()
                pl.emit("ve", lambda e: e.tensor_tensor(
                    TC[:, 18:19], TC[:, 18:19], NEG1C, ALU.mult))
                vsp()
                v_ve3 = pl.sig("ve")
                if "a0" in want and ch == 0 and tt == 0:
                    dump("a0", small[:, 0:512], [("ve", v_ve3)])
                    dump("a0s", stats[:, 0:96], [("ve", v_ve3)])
                pl.wait("sc", "ve", v_ve3)
                pl.wait("sc", "te", v_tm_te_prev)
                pl.emit("sc", lambda e, _x=xa, _r=xrt: e.activation(
                    _r, _x, AF.Identity, scale=TC[:, 17:18], bias=TC[:, 18:19]))
                v_xrt_prev = pl.sig("sc")
                if "xrt0" in want and ch == 0 and tt == 0:
                    dump("xrt0", xrt[:, 0:4096], [("sc", v_xrt_prev)])
                pl.wait("te", "sc", v_xrt_prev)
                a_done[tt], v_tm_te_prev = tm_to_fm(xrt, tt)
            a_done_prev = a_done[NTT - 1]
            w_last[1] = v_tm_te_prev
            pl.wait("ve", "sc", a_done_prev)
            for kc in range(NKC):
                pl.emit("ve", lambda e, _k=kc: e.tensor_scalar_mul(
                    xrq[:, _k * TCH:(_k + 1) * TCH],
                    xrq[:, _k * TCH:(_k + 1) * TCH], LNW[:, _k:_k + 1]))
            v_xr_ready = pl.sig("ve")
            if "xr" in want and ch == 0:
                dump("xr", xrq[:, 0:8192], [("ve", v_xr_ready)])

            CACC = [w2f32(0, 1024), w2f32(4096, 1024)]
            mm_done = {}
            ev_done = {}
            cv_done = {}
            cv_ve = {}
            wl = {}
            pl.wait("te", "ve", v_xr_ready)

            def b_load(jt):
                if ch == 0 and jt == 0:
                    wl[0] = wl0_pre
                    return
                buf = jt % 2
                if jt >= 2:
                    pl.wait("gp", "te", mm_done[jt - 2])
                else:
                    pl.wait("gp", "te", w_last[buf])
                g = 32 + jt
                src = w_in[:, g * 4096:(g + 1) * 4096].rearrange(
                    "p (kc j) -> p kc j", kc=NKC)
                dstw = wbufs[buf][:, 0:NKC * P].rearrange(
                    "p (kc j) -> p kc j", kc=NKC)
                wl[jt] = pl.dma("gp", "gp_d", dstw, src)

            b_load(0)
            b_load(1)
            for jt in range(NJ1):
                width = P if jt < 34 else 64
                wb = wbufs[jt % 2]
                pl.wait("te", "gp_d", wl[jt])
                for tg in range(2):
                    bk = (2 * jt + tg) % 6
                    bank_wait("te", bk)
                    outp = banks[bk][0:width, 0:512]
                    for kc in range(NKC):
                        pl.emit("te", lambda e, _o=outp, _w=wb, _kc=kc, _tg=tg,
                                _wd=width: e.matmul(
                            _o, _w[:, _kc * P:_kc * P + _wd],
                            xrq[:, _kc * TCH + _tg * 512: _kc * TCH + (_tg + 1) * 512],
                            start=(_kc == 0), stop=(_kc == NKC - 1)))
                    v_mm = pl.sig("te")
                    if tg == 1:
                        mm_done[jt] = v_mm
                        w_last[jt % 2] = v_mm
                    pl.wait("sc", "te", v_mm)
                    if jt < 34:
                        pl.emit("sc", lambda e, _b=banks[bk], _jt=jt, _tg=tg:
                                e.activation(cv(_jt, 3 + _tg * 512, 512),
                                             _b[:, 0:512], AF.Identity,
                                             scale=MUP[:, 32 + _jt:33 + _jt]))
                    else:
                        pl.wait("sc", "te", v_prep_te)
                        dtr_t = w2f32(8192, 1024)
                        pl.emit("sc", lambda e, _b=banks[bk], _tg=tg, _d=dtr_t:
                                e.activation(
                                    _d[0:NH, _tg * 512:(_tg + 1) * 512],
                                    _b[0:NH, 0:512], AF.Identity,
                                    scale=MUP[0:NH, 66:67]))
                    v_ev = pl.sig("sc")
                    bank_free[bk] = ("sc", v_ev)
                    ev_done[(jt, tg)] = v_ev
                if jt + 2 < NJ1:
                    b_load(jt + 2)
                if jt < 34:
                    pl.wait("ve", "sc", ev_done[(jt, 1)])
                    if jt >= 2:
                        pl.wait("ve", "sc", cv_done.get(jt - 2, 0))
                    acc = CACC[jt % 2]
                    pl.emit("ve", lambda e, _j=jt, _a=acc: e.tensor_scalar_mul(
                        _a, cv(_j, 0, TCH), CW4[:, _j * 4:_j * 4 + 1]))
                    for k in range(1, 4):
                        pl.emit("ve", lambda e, _j=jt, _a=acc, _k=k:
                                e.scalar_tensor_tensor(
                                    _a, cv(_j, _k, TCH),
                                    CW4[:, _j * 4 + _k:_j * 4 + _k + 1], _a,
                                    ALU.mult, ALU.add))
                    pl.emit("ve", lambda e, _j=jt: e.tensor_copy(
                        cv(_j, 0, 3), cv(_j, TCH, 3)))
                    cv_ve[jt] = pl.sig("ve")
                lag = jt - 1
                if 0 <= lag < 34:
                    pl.wait("sc", "ve", cv_ve[lag])
                    pl.emit("sc", lambda e, _j=lag, _a=CACC[lag % 2]: e.activation(
                        cv(_j, 3, TCH), _a, AF.Silu))
                    cv_done[lag] = pl.sig("sc")
                if jt == NJ1 - 1:
                    pl.wait("sc", "ve", cv_ve[33])
                    pl.emit("sc", lambda e, _a=CACC[33 % 2]: e.activation(
                        cv(33, 3, TCH), _a, AF.Silu))
                    cv_done[33] = pl.sig("sc")
            DTP = dtb[0:NH, 0:TCH]
            SEG = dtb[0:NH, TCH:2 * TCH]
            DTR = w2f32(8192, 1024)[0:NH, :]
            pl.wait("sc", "gp_d", v_f2_gpd_last)
            pl.wait("ve", "gp_d", v_f2_gpd_last)
            pl.emit("sc", lambda e: e.activation(DTR, DTR, AF.Identity, bias=DTBI[0:NH, :]))
            pl.emit("sc", lambda e: e.activation(SEG, DTR, AF.Abs))
            pl.emit("sc", lambda e: e.activation(SEG, SEG, AF.Exp, scale=-1.0))
            pl.emit("sc", lambda e: e.activation(SEG, SEG, AF.Ln, bias=1.0))
            v_sp = pl.sig("sc")
            pl.wait("ve", "sc", v_sp)
            pl.emit("ve", lambda e: e.tensor_scalar_max(DTP, DTR, 0.0))
            pl.emit("ve", lambda e: e.tensor_add(DTP, DTP, SEG))
            pl.emit("ve", lambda e: e.tensor_mul(
                DTR, DTP, AC[0:NH, :].broadcast_to([NH, TCH])))
            for qb in range(NBLK):
                pl.emit("ve", lambda e, _q=qb: e.tensor_tensor_scan(
                    SEG[:, _q * P:(_q + 1) * P], DTR[:, _q * P:(_q + 1) * P],
                    DTR[:, _q * P:(_q + 1) * P], 0.0, ALU.add, ALU.bypass))
            v_seg = pl.sig("ve")
            if "convout" in want and ch == 0:
                dump("convout", convin[:, 0:8192], [("sc", cv_done[33])])
            if "dt" in want and ch == 0:
                dump("dt", dtb[0:NH, 0:2048], [("ve", v_seg)])

            XT = w2bf(0, 4096)
            E8 = [w2bf(8192, 1024), w2bf(10240, 1024)]
            M8 = E8
            Y2S = [w2f32(12288, 512), w2f32(14336, 512)]
            XW = [w2bf(16384, 512), w2bf(17408, 512)]
            YTM = [w2bf(18432, 512), w2bf(19456, 512)]
            HBF = [wqbf(0, 512), wqbf(1024, 512)]
            CBM = wqbf(2048, 128)
            BT = wqbf(2304, 128)
            SEGT = wqf32(2560, 64)
            DTT = wqf32(2816, 64)
            WST = wqf32(3072, 64)
            ESGT = wqf32(3328, 64)
            ENDR = wqf32(3584, 64)
            B64 = wqf32(3840, 64)
            SGC = [wqf32(4096, 1024), wqf32(8192, 1024)]
            WSB = SPARE[0:NH, 0:128]
            ESGF = SPARE[0:NH, 128:256]

            for qb in range(NBLK):
                tok = 3 + qb * P
                segb = SEG[:, qb * P:(qb + 1) * P]
                dtpb = DTP[:, qb * P:(qb + 1) * P]
                wsb = WSB
                pl.wait("ve", "te", blk_te)
                pl.emit("ve", lambda e, _s=segb, _w=wsb, _q=qb: e.tensor_scalar(
                    _w, _s, SEG[0:NH, (_q + 1) * P - 1:(_q + 1) * P], None,
                    ALU.subtract))
                v_w1 = pl.sig("ve")
                pl.wait("sc", "ve", v_w1)
                pl.wait("sc", "te", blk_te)
                pl.emit("sc", lambda e, _w=wsb: e.activation(_w, _w, AF.Exp, scale=-1.0))
                v_w2s = pl.sig("sc")
                pl.wait("ve", "sc", v_w2s)
                pl.emit("ve", lambda e, _w=wsb, _d=dtpb: e.tensor_mul(_w, _w, _d))
                v_w3 = pl.sig("ve")
                pl.wait("sc", "ve", v_w3)
                pl.emit("sc", lambda e, _s=segb: e.activation(
                    ESGF, _s, AF.Exp))
                v_esg = pl.sig("sc")
                bank_wait("te", 6)
                bank_wait("te", 7)
                pl.wait("te", "sc", v_esg)
                pb7b = pb7[:].bitcast(BF16)
                pl.emit("te", lambda e, _s=segb: e.transpose(
                    pb7[:, 64:128], _s, ID_F32[0:64, 0:64]))
                pl.emit("te", lambda e, _d=dtpb: e.transpose(
                    pb7[:, 128:192], _d, ID_F32[0:64, 0:64]))
                pl.emit("te", lambda e, _w=wsb: e.transpose(
                    pb7[:, 192:256], _w, ID_F32[0:64, 0:64]))
                pl.emit("te", lambda e: e.transpose(
                    pb7[:, 256:320], ESGF, ID_F32[0:64, 0:64]))
                pl.emit("te", lambda e: e.transpose(
                    pb7[0:1, 320:384], ESGF[:, P - 1:P], ID_F32[0:64, 0:64]))
                pl.emit("te", lambda e, _t=tok: e.matmul(
                    pb6[:, 0:128], cv(32, _t, P), cv(33, _t, P),
                    start=True, stop=True))
                v_pre1 = pl.sig("te")
                pl.wait("sc", "te", v_pre1)
                pl.emit("sc", lambda e: e.activation(SEGT, pb7[:, 64:128], AF.Identity))
                pl.emit("sc", lambda e: e.activation(DTT, pb7[:, 128:192], AF.Identity))
                pl.emit("sc", lambda e: e.activation(WST, pb7[:, 192:256], AF.Identity))
                pl.emit("sc", lambda e: e.activation(ESGT, pb7[:, 256:320], AF.Identity))
                pl.emit("sc", lambda e: e.activation(
                    ENDR[0:1, 0:64], pb7[0:1, 320:384], AF.Identity))
                v_pre2 = pl.sig("sc")
                pl.wait("ve", "te", v_pre1)
                pl.emit("ve", lambda e: e.tensor_tensor(CBM, pb6[:, 0:128], LT, ALU.mult))
                v_cbm = pl.sig("ve")
                bank_free[6] = ("ve", v_cbm)
                pl.wait("te", "sc", v_pre2)
                pl.emit("te", lambda e: e.matmul(
                    pb7[:, 384:448], ONES_R[0:1, 0:128], ENDR[0:1, 0:64],
                    start=False, stop=True, skip_group_check=True))
                pl.emit("te", lambda e, _t=tok: e.transpose(
                    pb7b[:, 896:1024], cv(32, _t, P), ID_BF))
                v_pre3 = pl.sig("te")
                pl.wait("sc", "te", v_pre3)
                pl.emit("sc", lambda e: e.activation(B64, pb7[:, 384:448], AF.Identity))
                pl.emit("sc", lambda e: e.activation(BT, pb7b[:, 896:1024], AF.Identity))
                v_pre4 = pl.sig("sc")
                bank_free[7] = ("sc", v_pre4)
                v_xt = 0
                for half in range(4):
                    bk = 4 + (half % 2)
                    bank_wait("te", bk)
                    pbk = banks[bk][:].bitcast(BF16)
                    for k in range(8):
                        ft = half * 8 + k
                        pl.emit("te", lambda e, _p=pbk, _k=k, _f=ft, _t=tok:
                                e.transpose(_p[:, _k * P:(_k + 1) * P],
                                            cv(_f, _t, P), ID_BF))
                    v_t = pl.sig("te")
                    pl.wait("sc", "te", v_t)
                    pl.emit("sc", lambda e, _p=pbk, _h=half: e.activation(
                        XT[:, _h * 1024:(_h + 1) * 1024], _p, AF.Identity))
                    v_xt = pl.sig("sc")
                    bank_free[bk] = ("sc", v_xt)

                relu_sig = {}
                m8_sig = {}
                xw_sig = {}
                hbf_sig = {}
                yfin_sig = {}
                hupd_sig = {}
                ywr_sig = {}
                rmm_sig = {}
                ymm_sig = {}
                stm_sig = {}
                yt_sig = {}
                for g in range(NG):
                    pr = g % 2
                    rbA, rbB = (0, 1) if pr == 0 else (2, 3)
                    y1b = 4 + pr
                    y2b = 6 + pr
                    if g >= 2:
                        pl.wait("sc", "te", rmm_sig[g - 2])
                    seg_rows = dtb[g * 8:(g + 1) * 8, TCH + qb * P: TCH + (qb + 1) * P]
                    pl.wait("sc", "sy_d", v_prep_store)
                    v_sgc = pl.dma(
                        "sc", "sc_d",
                        SGC[pr][0:1, :].rearrange("p (a b) -> p a b", a=8),
                        seg_rows)
                    pl.wait("te", "sc_d", v_sgc)
                    if g >= 2:
                        pl.wait("te", "ve", hupd_sig[g - 2])
                        pl.wait("te", "ve", ywr_sig[g - 2])
                    else:
                        bank_wait("te", rbA)
                        bank_wait("te", rbB)
                    pl.emit("te", lambda e, _b=banks[rbA], _p=pr: e.matmul(
                        _b[:, 0:512], ONES_R[0:1, 0:128], SGC[_p][0:1, 0:512],
                        start=True, stop=True))
                    pl.emit("te", lambda e, _b=banks[rbB], _p=pr: e.matmul(
                        _b[:, 0:512], ONES_R[0:1, 0:128], SGC[_p][0:1, 512:1024],
                        start=True, stop=True))
                    rmm_sig[g] = pl.sig("te")
                    pl.wait("sc", "te", rmm_sig[g])
                    if g >= 2:
                        pl.wait("sc", "ve", m8_sig[g - 2])
                    for rl in range(8):
                        bsel = banks[rbA] if rl < 4 else banks[rbB]
                        pl.emit("sc", lambda e, _b=bsel, _r=rl, _g=g:
                                e.activation(
                                    _b[:, (_r % 4) * P:((_r % 4) + 1) * P],
                                    _b[:, (_r % 4) * P:((_r % 4) + 1) * P],
                                    AF.Relu, scale=-1.0,
                                    bias=SEGT[:, _g * 8 + _r:_g * 8 + _r + 1]))
                    pl.emit("sc", lambda e, _p=pr, _bA=banks[rbA]: e.activation(
                        E8[_p][:, 0:512], _bA[:, 0:512], AF.Exp, scale=-1.0))
                    pl.emit("sc", lambda e, _p=pr, _bB=banks[rbB]: e.activation(
                        E8[_p][:, 512:1024], _bB[:, 0:512], AF.Exp, scale=-1.0))
                    relu_sig[g] = pl.sig("sc")
                    pl.wait("ve", "sc", relu_sig[g])
                    pl.emit("ve", lambda e, _p=pr: e.tensor_tensor(
                        E8[_p], E8[_p],
                        CBM.unsqueeze(1).broadcast_to([P, 8, P]), ALU.mult))
                    pl.emit("ve", lambda e, _p=pr, _g=g: e.tensor_tensor(
                        M8[_p], E8[_p],
                        DTT[:, _g * 8:(_g + 1) * 8].unsqueeze(2)
                        .broadcast_to([P, 8, P]), ALU.mult))
                    pl.emit("ve", lambda e, _p=pr, _g=g: e.tensor_tensor(
                        XW[_p], XT[:, _g * 512:(_g + 1) * 512],
                        WST[:, _g * 8:(_g + 1) * 8].unsqueeze(2)
                        .broadcast_to([P, 8, HD]), ALU.mult))
                    m8_sig[g] = xw_sig[g] = pl.sig("ve")
                    if g >= 1:
                        pl.wait("sc", "ve", hupd_sig[g - 1])
                    else:
                        pl.wait("sc", "ve", h_upd_last)
                        pl.wait("sc", "gp_c", v_hmem)
                    pl.emit("sc", lambda e, _p=pr, _g=g: e.activation(
                        HBF[_p], h_st[:, _g * 512:(_g + 1) * 512], AF.Identity))
                    hbf_sig[g] = pl.sig("sc")
                    pl.wait("te", "ve", m8_sig[g])
                    pl.wait("te", "sc", hbf_sig[g])
                    if g >= 2:
                        pl.wait("te", "ve", yfin_sig[g - 2])
                    for rl in range(8):
                        pl.emit("te", lambda e, _b=banks[y1b], _p=pr, _r=rl, _g=g:
                                e.matmul(_b[:, _r * HD:(_r + 1) * HD],
                                         M8[_p][:, _r * P:(_r + 1) * P],
                                         XT[:, (_g * 8 + _r) * HD:(_g * 8 + _r + 1) * HD],
                                         start=(_r == 0), stop=(_r == 7),
                                         skip_group_check=True))
                    for rl in range(8):
                        pl.emit("te", lambda e, _b=banks[y2b], _p=pr, _r=rl, _t=tok:
                                e.matmul(_b[:, _r * HD:(_r + 1) * HD],
                                         cv(33, _t, P),
                                         HBF[_p][:, _r * HD:(_r + 1) * HD],
                                         start=(_r == 0), stop=(_r == 7),
                                         skip_group_check=True))
                    ymm_sig[g] = pl.sig("te")
                    pl.wait("te", "ve", xw_sig[g])
                    for rl in range(8):
                        pl.emit("te", lambda e, _b=banks[rbA], _p=pr, _r=rl:
                                e.matmul(_b[:, _r * HD:(_r + 1) * HD],
                                         BT, XW[_p][:, _r * HD:(_r + 1) * HD],
                                         start=(_r == 0), stop=(_r == 7),
                                         skip_group_check=True))
                    stm_sig[g] = pl.sig("te")
                    pl.wait("ve", "te", ymm_sig[g])
                    pl.emit("ve", lambda e, _p=pr, _g=g, _b=banks[y2b]:
                            e.tensor_tensor(Y2S[_p], _b[:, 0:512],
                                            ESGT[:, _g * 8:(_g + 1) * 8]
                                            .unsqueeze(2).broadcast_to([P, 8, HD]),
                                            ALU.mult))
                    pl.emit("ve", lambda e, _p=pr, _b=banks[y1b]:
                            e.tensor_tensor(YTM[_p], _b[:, 0:512], Y2S[_p], ALU.add))
                    yfin_sig[g] = pl.sig("ve")
                    pl.wait("ve", "te", stm_sig[g])
                    pl.emit("ve", lambda e, _p=pr, _g=g: e.tensor_tensor(
                        Y2S[_p], h_st[:, _g * 512:(_g + 1) * 512],
                        B64[:, _g * 8:(_g + 1) * 8].unsqueeze(2)
                        .broadcast_to([P, 8, HD]), ALU.mult))
                    pl.emit("ve", lambda e, _p=pr, _g=g, _b=banks[rbA]:
                            e.tensor_tensor(h_st[:, _g * 512:(_g + 1) * 512],
                                            Y2S[_p], _b[:, 0:512], ALU.add))
                    pl.wait("te", "ve", yfin_sig[g])
                    rbB_bf = banks[rbB][:].bitcast(BF16)
                    for k in range(4):
                        pl.emit("te", lambda e, _b=rbB_bf, _p=pr, _k=k:
                                e.transpose(_b[:, 512 + _k * P: 512 + (_k + 1) * P],
                                            YTM[_p][:, _k * P:(_k + 1) * P], ID_BF))
                    yt_sig[g] = pl.sig("te")
                    pl.wait("ve", "te", yt_sig[g])
                    for k in range(4):
                        ft = g * 4 + k
                        pl.emit("ve", lambda e, _b=rbB_bf, _k=k, _f=ft, _t=tok:
                                e.scalar_tensor_tensor(
                                    cv(_f, _t, P), cv(_f, _t, P),
                                    DFM[:, _f:_f + 1],
                                    _b[:, 512 + _k * P: 512 + (_k + 1) * P],
                                    ALU.mult, ALU.add))
                    hupd_sig[g] = ywr_sig[g] = pl.sig("ve")
                h_upd_last = hupd_sig[NG - 1]
                blk_te = yt_sig[NG - 1]
                blk_ve = ywr_sig[NG - 1]
                bank_free[0] = ("ve", hupd_sig[NG - 2])
                bank_free[1] = ("ve", ywr_sig[NG - 2])
                bank_free[2] = ("ve", hupd_sig[NG - 1])
                bank_free[3] = ("ve", ywr_sig[NG - 1])
                bank_free[4] = ("ve", yfin_sig[NG - 2])
                bank_free[5] = ("ve", yfin_sig[NG - 1])
                bank_free[6] = ("ve", yfin_sig[NG - 2])
                bank_free[7] = ("ve", yfin_sig[NG - 1])
            if "y" in want and ch == 0:
                dump("y", convin[:, 0:8192], [("ve", blk_ve)])

            SZ = [w2bf(0, 512), w2bf(1024, 512)]
            GY2 = [w2f32(2048, 512), w2f32(4096, 512)]
            SRMS = w2f32(4096, 1024)
            SSQ = [dtb[:, 0:512], dtb[:, 512:1024]]
            ez_mm = {}
            gy_sig = {}
            g2_sig = {}
            sqs = {}
            ezl = {}
            def e_load(jt):
                buf = jt % 2
                if jt >= 2:
                    pl.wait("gp", "te", ez_mm[jt - 2])
                elif jt == 1:
                    pl.wait("gp", "te", blk_te)
                    pl.wait("gp", "ve", blk_ve)
                else:
                    pl.wait("gp", "te", w_last[buf])
                src = w_in[:, jt * 4096:(jt + 1) * 4096].rearrange(
                    "p (kc j) -> p kc j", kc=NKC)
                dstw = wbufs[buf][:].rearrange("p (kc j) -> p kc j", kc=NKC)
                ezl[jt] = pl.dma("gp", "gp_d", dstw, src)

            e_load(0)
            e_load(1)
            for jt in range(NJ2):
                wb = wbufs[jt % 2]
                pl.wait("te", "gp_d", ezl[jt])
                pl.wait("te", "ve", blk_ve)
                for tg in range(2):
                    bk = (2 * jt + tg) % 4
                    bank_wait("te", bk)
                    for kc in range(NKC):
                        pl.emit("te", lambda e, _b=banks[bk], _w=wb, _kc=kc, _tg=tg:
                                e.matmul(_b[:, 0:512],
                                         _w[:, _kc * P:(_kc + 1) * P],
                                         xrq[:, _kc * TCH + _tg * 512: _kc * TCH + (_tg + 1) * 512],
                                         start=(_kc == 0), stop=(_kc == NKC - 1)))
                    v_mm = pl.sig("te")
                    if tg == 1:
                        ez_mm[jt] = v_mm
                        w_last[jt % 2] = v_mm
                        if jt + 2 < NJ2:
                            e_load(jt + 2)
                    pl.wait("sc", "te", v_mm)
                    pl.wait("sc", "ve", gy_sig.get((jt - 1, tg), 0))
                    pl.emit("sc", lambda e, _b=banks[bk], _jt=jt, _tg=tg:
                            e.activation(SZ[_tg], _b[:, 0:512], AF.Silu,
                                         scale=MUP[:, _jt:_jt + 1]))
                    v_sz = pl.sig("sc")
                    bank_free[bk] = ("sc", v_sz)
                    pl.wait("ve", "sc", v_sz)
                    pl.emit("ve", lambda e, _jt=jt, _tg=tg: e.tensor_tensor(
                        cv(_jt, 3 + _tg * 512, 512), cv(_jt, 3 + _tg * 512, 512),
                        SZ[_tg], ALU.mult))
                    gy_sig[(jt, tg)] = pl.sig("ve")
                    pl.wait("sc", "ve", gy_sig[(jt, tg)])
                    pl.wait("sc", "ve", sqs.get((jt - 1, tg), 0))
                    pl.emit("sc", lambda e, _jt=jt, _tg=tg: e.activation(
                        GY2[_tg], cv(_jt, 3 + _tg * 512, 512), AF.Square))
                    g2_sig[(jt, tg)] = pl.sig("sc")
                    pl.wait("ve", "sc", g2_sig[(jt, tg)])
                    if jt == 0:
                        pl.emit("ve", lambda e, _tg=tg: e.tensor_copy(
                            SSQ[_tg], GY2[_tg]))
                    else:
                        pl.emit("ve", lambda e, _tg=tg: e.tensor_tensor(
                            SSQ[_tg], SSQ[_tg], GY2[_tg], ALU.add))
                    pl.emit("ve", lambda e, _jt=jt, _tg=tg: e.tensor_scalar_mul(
                        cv(_jt, 3 + _tg * 512, 512), cv(_jt, 3 + _tg * 512, 512),
                        NRMW[:, _jt:_jt + 1]))
                    sqs[(jt, tg)] = pl.sig("ve")
            bank_wait("te", 4)
            bank_wait("te", 5)
            pl.wait("te", "ve", sqs[(NJ2 - 1, 1)])
            pl.emit("te", lambda e: e.matmul(pb4[0:1, 0:512], ONES_CF, SSQ[0],
                                             start=True, stop=True))
            pl.emit("te", lambda e: e.matmul(pb5[0:1, 0:512], ONES_CF, SSQ[1],
                                             start=True, stop=True))
            v_ssqr = pl.sig("te")
            pl.wait("sc", "te", v_ssqr)
            pl.emit("sc", lambda e: e.activation(SRMS[0:1, 0:512], pb4[0:1, 0:512], AF.Identity))
            pl.emit("sc", lambda e: e.activation(SRMS[0:1, 512:1024], pb5[0:1, 0:512], AF.Identity))
            v_sr = pl.sig("sc")
            bank_free[4] = ("sc", v_sr)
            bank_free[5] = ("sc", v_sr)
            pl.wait("ve", "sc", v_sr)
            pl.emit("ve", lambda e: e.tensor_scalar(
                SRMS[0:1, 0:1024], SRMS[0:1, 0:1024], 1.0 / DS, EPS,
                ALU.mult, ALU.add))
            v_sr2 = pl.sig("ve")
            pl.wait("sc", "ve", v_sr2)
            pl.emit("sc", lambda e: e.activation(SRMS[0:1, 0:1024], SRMS[0:1, 0:1024], AF.Ln))
            pl.emit("sc", lambda e: e.activation(SRMS[0:1, 0:1024], SRMS[0:1, 0:1024],
                                                 AF.Exp, scale=-0.5))
            v_sr3 = pl.sig("sc")
            pl.wait("te", "sc", v_sr3)
            bank_wait("te", 6)
            for tt in range(NTT):
                pl.emit("te", lambda e, _t=tt: e.matmul(
                    pb6[:, _t:_t + 1], SRMS[0:1, _t * P:(_t + 1) * P],
                    ONES_R[0:1, 0:1], start=(_t == 0), stop=(_t == NTT - 1),
                    skip_group_check=True))
            v_scol = pl.sig("te")
            pl.wait("sc", "te", v_scol)
            for tt in range(NTT):
                pl.emit("sc", lambda e, _t=tt, _c=ch: e.activation(
                    stc(_c, _t, 4), pb6[:, _t:_t + 1], AF.Identity))
            v_scols = pl.sig("sc")
            bank_free[6] = ("sc", v_scols)
            if "gy" in want and ch == 0:
                dump("gy", convin[:, 0:8192], [("sc", v_scols)])
            if "srms" in want and ch == 0:
                dump("srms", stats[:, 0:96], [("sc", v_scols)])

            XNQ = w2f32(0, 4096)
            XQT = wqbf(0, 4096)
            f1_done = 0
            v_f1b_prev = 0
            v_f1c_prev = 0
            v_f1_te_tm = 0
            pl.wait("sy", "sc", v_scols)
            for tt in range(NTT):
                pl.wait("sy", "ve", v_f1c_prev)
                vl = pl.dma("sy", "sy_d", XNQ,
                            x_hbm[t0 + tt * P: t0 + (tt + 1) * P, :])
                pl.wait("ve", "sy_d", vl)
                pl.wait("ve", "sc", v_f1b_prev)
                pl.emit("ve", lambda e, _c=ch, _t=tt: e.tensor_mul(
                    TC[:, 20:21], stc(_c, _t, 1), stc(_c, _t, 2)))
                vsp()
                pl.emit("ve", lambda e, _c=ch, _t=tt: e.tensor_mul(
                    TC[:, 21:22], stc(_c, _t, 0), TC[:, 20:21]))
                vsp()
                pl.emit("ve", lambda e: e.tensor_tensor(
                    TC[:, 21:22], TC[:, 21:22], NEG1C, ALU.mult))
                vsp()
                v_f1a = pl.sig("ve")
                pl.wait("sc", "ve", v_f1a)
                pl.emit("sc", lambda e: e.activation(
                    XNQ, XNQ, AF.Identity, scale=TC[:, 20:21], bias=TC[:, 21:22]))
                v_f1b_prev = pl.sig("sc")
                pl.wait("ve", "sc", v_f1b_prev)
                pl.wait("ve", "te", v_f1_te_tm)
                pl.emit("ve", lambda e: e.tensor_copy(XNQ.bitcast(I32), XNQ))
                pl.emit("ve", lambda e: e.tensor_copy(XNQ, XNQ.bitcast(I32)))
                pl.emit("ve", lambda e: e.tensor_scalar(
                    XQT[:, 0:4096], XNQ, QBHI, -QBHI, ALU.min, ALU.max))
                v_f1c_prev = pl.sig("ve")
                pl.wait("te", "ve", v_f1c_prev)
                f1_done, v_f1_te_tm = tm_to_fm(XQT[:, 0:4096], tt)
            if "xq" in want and ch == 0:
                dump("xq", xrq[:, 0:8192], [("sc", f1_done)])
            v_f1_te = pl.cnt["te"]
            v_f1_ve = pl.cnt["ve"]
            v_f1_sc = f1_done

            WOBUF = w2bf(0, 8192)
            WQB = wqbf(0, 8192)
            GBV = SPARE[:].bitcast(BF16)
            GB = [GBV[:, 0:256], GBV[:, 256:512]]
            ffn_mm = {}
            wo_used = {}
            gb_busy = {0: 0, 1: 0}
            for ob in range(NOB):
                if ob >= 1:
                    pl.wait("sy", "te", ffn_mm[ob - 1])
                else:
                    pl.wait("sy", "sc", v_f1_sc)
                src = wqt_hbm[:, ob * 8192:(ob + 1) * 8192].rearrange(
                    "p (kc o) -> p kc o", kc=NKC)
                v_wq = pl.dma("sy", "sy_d",
                              WQB[:].rearrange("p (kc o) -> p kc o", kc=NKC), src)
                if ob >= 1:
                    pl.wait("gp", "te", wo_used[ob - 1])
                else:
                    pl.wait("gp", "ve", v_f1_ve)
                    pl.wait("gp", "te", v_f1_te)
                srco = w_out[:, ob * 8192:(ob + 1) * 8192].rearrange(
                    "p (kc o) -> p kc o", kc=NKC)
                v_wo = pl.dma("gp", "gp_d",
                              WOBUF[:].rearrange("p (kc o) -> p kc o", kc=NKC),
                              srco)
                if "wqb" in want and ch == 0 and ob == 0:
                    dump("wqb", WQB[:, 0:8192], [("sy_d", v_wq)])
                    dump("wqt1", wqt_hbm[P:2 * P, 0:4096], [("sy_d", v_wq)])
                pl.wait("te", "sy_d", v_wq)
                for tt in range(NTT):
                    bk = 2 + (tt % 2)
                    bank_wait("te", bk)
                    for kc in range(NKC):
                        pl.emit("te", lambda e, _b=banks[bk], _kc=kc, _t=tt:
                                e.matmul(_b[:, 0:OB],
                                         xrq[:, _kc * TCH + _t * P: _kc * TCH + (_t + 1) * P],
                                         WQB[:, _kc * OB:(_kc + 1) * OB],
                                         start=(_kc == 0), stop=(_kc == NKC - 1)))
                    v_fmm = pl.sig("te")
                    if tt == NTT - 1:
                        ffn_mm[ob] = v_fmm
                    pl.wait("sc", "te", v_fmm)
                    if "psb00" in want and ch == 0 and ob == 0 and tt == 0:
                        pl.emit("sc", lambda e, _b=banks[bk]: e.activation(
                            w2f32(16384, 256), _b[:, 0:OB], AF.Identity))
                        dump("psb00", w2f32(16384, 256), [("sc", pl.sig("sc"))])
                    pl.wait("sc", "ve", gb_busy[tt % 2])
                    pl.emit("sc", lambda e, _b=banks[bk], _t=tt, _c=ch, _g=GB[tt % 2]:
                            e.activation(_g, _b[:, 0:OB], AF.Silu,
                                         scale=stc(_c, _t, 3)))
                    v_gb = pl.sig("sc")
                    bank_free[bk] = ("sc", v_gb)
                    pl.wait("te", "gp_d", v_wo)
                    bko = tt % 2
                    bank_wait("te", bko)
                    for kc in range(NKC):
                        pl.emit("te", lambda e, _b=banks[bko], _kc=kc, _t=tt:
                                e.matmul(_b[:, 0:OB],
                                         cv(_kc, 3 + _t * P, P),
                                         WOBUF[:, _kc * OB:(_kc + 1) * OB],
                                         start=(_kc == 0), stop=(_kc == NKC - 1)))
                    v_omm = pl.sig("te")
                    if tt == NTT - 1:
                        wo_used[ob] = v_omm
                    pl.wait("ve", "te", v_omm)
                    pl.wait("ve", "sc", v_gb)
                    pl.wait("ve", "gp_d", res_slot[tt % 2])
                    rslot = resb[:, (tt % 2) * OB:(tt % 2 + 1) * OB]
                    if "only_f" in want:
                        pl.emit("ve", lambda e, _b=banks[bk], _t=tt, _c=ch,
                                _r=rslot: e.scalar_tensor_tensor(
                            _r, _b[:, 0:OB], stc(_c, _t, 3),
                            ONES_CF.broadcast_to([P, OB]), ALU.mult, ALU.mult))
                    elif "only_s" in want:
                        pl.emit("ve", lambda e, _b=banks[bko], _t=tt, _c=ch,
                                _r=rslot: e.scalar_tensor_tensor(
                            _r, _b[:, 0:OB], stc(_c, _t, 4),
                            ONES_CF.broadcast_to([P, OB]), ALU.mult, ALU.mult))
                    elif "only_g" in want:
                        pl.emit("ve", lambda e, _b=banks[bko], _t=tt, _c=ch,
                                _g=GB[tt % 2], _r=rslot: e.scalar_tensor_tensor(
                            _r, _b[:, 0:OB], 0.0, _g, ALU.mult, ALU.add))
                    else:
                        pl.emit("ve", lambda e, _b=banks[bko], _t=tt, _c=ch,
                                _g=GB[tt % 2], _r=rslot: e.scalar_tensor_tensor(
                            _r, _b[:, 0:OB], stc(_c, _t, 4), _g, ALU.mult, ALU.mult))
                    v_res = pl.sig("ve")
                    gb_busy[tt % 2] = v_res
                    bank_free[bko] = ("ve", v_res)
                    pl.wait("gp", "ve", v_res)
                    pl.wait("gp", "sc_d", v_prefill)
                    dstout = out_hbm[t0 + tt * P: t0 + (tt + 1) * P,
                                     ob * OB:(ob + 1) * OB]
                    res_slot[tt % 2] = pl.dma("gp", "gp_d", dstout, rslot,
                                              accum_op=ALU.add)
            v_f2_te_last = pl.cnt["te"]
            v_f2_ve_last = pl.cnt["ve"]
            v_f2_gpd_last = pl.cnt["gp_d"]

        pl.emit("gp", lambda e: e.wait_ge(s_gpd, pl.cnt["gp_d"]))
        pl.emit("sy", lambda e: e.wait_ge(s_syd, pl.cnt["sy_d"]))
        pl.emit("sc", lambda e: e.wait_ge(s_scd, pl.cnt["sc_d"]))

        for s in sems.values():
            nc.sync.sem_clear(s)
        nc.all_engine_barrier()
        with nc.Block() as block:
            @block.sync
            def _(e):
                for fn in pl.ops["sy"]:
                    fn(e)

            @block.gpsimd
            def _(e):
                for fn in pl.ops["gp"]:
                    fn(e)

            @block.tensor
            def _(e):
                for fn in pl.ops["te"]:
                    fn(e)

            @block.scalar
            def _(e):
                for fn in pl.ops["sc"]:
                    fn(e)

            @block.vector
            def _(e):
                for fn in pl.ops["ve"]:
                    fn(e)

    return nc, dbg_names


def _consts():
    c = {}
    c["id_bf"] = np.eye(P, dtype=ml_dtypes.bfloat16)
    c["id_f32"] = np.ascontiguousarray(np.eye(P, dtype=np.float32)[:, 0:64])
    c["ones_r"] = np.ones((1, P), np.float32)
    c["ones_cf"] = np.ones((P, 1), np.float32)
    c["ones_bc"] = np.ones((P, 1), ml_dtypes.bfloat16)
    c["epsc"] = np.full((P, 1), 1e-5, np.float32)
    c["neg1c"] = np.full((P, 1), -1.0, np.float32)
    lt = (np.arange(P)[:, None] <= np.arange(P)[None, :])
    c["lt_bf"] = lt.astype(ml_dtypes.bfloat16)
    return c


def layout_params(ins):
    d = {}
    wi = np.zeros((H, 67 * P), np.float32)
    wi[:, :PROJ] = np.asarray(ins["L0_attn_in_proj_w"], np.float32)
    d["w_in"] = np.ascontiguousarray(
        wi.reshape(NKC, P, 67, P).transpose(1, 2, 0, 3)).reshape(P, 67 * NKC * P)
    wo = np.asarray(ins["L0_attn_out_proj_w"], np.float32)
    d["w_out"] = np.ascontiguousarray(
        wo.reshape(NKC, P, 16, OB).transpose(1, 2, 0, 3)).reshape(P, 16 * NKC * OB)
    d["w_ffn"] = np.ascontiguousarray(ins["L0_ffn_weight"], np.float32)
    d["ln_w_fm"] = np.ascontiguousarray(
        np.asarray(ins["L0_attn_ln_w"], np.float32).reshape(NKC, P).T)
    d["norm_w_fm"] = np.ascontiguousarray(
        np.asarray(ins["L0_attn_norm_w"], np.float32).reshape(NKC, P).T)
    mup = np.zeros(67 * P, np.float32)
    mup[:PROJ] = np.asarray(ins["L0_attn_mup_vector"], np.float32)
    d["mup_fm"] = np.ascontiguousarray(mup.reshape(67, P).T)
    d["conv_w_fm"] = np.ascontiguousarray(
        np.asarray(ins["L0_attn_conv_w"], np.float32)
        .reshape(NCV, P, 4).transpose(1, 0, 2).reshape(P, NCV * 4))
    d["d_fm"] = np.ascontiguousarray(
        np.repeat(np.asarray(ins["L0_attn_D"], np.float32), HD)
        .reshape(DS // P, P).T)
    d["dt_bias_c"] = np.ascontiguousarray(
        np.asarray(ins["L0_attn_dt_bias"], np.float32).reshape(NH, 1))
    d["a_log_c"] = np.ascontiguousarray(
        np.asarray(ins["L0_attn_A_log"], np.float32).reshape(NH, 1))
    d.update(_consts())
    return d


def kernel(**inputs):
    x_full = np.asarray(inputs["L0_attn_hidden_states"], np.float32)
    params = layout_params({k: np.asarray(v) for k, v in inputs.items()})
    nc, _ = build_program()
    in_maps = []
    for c in range(8):
        m = dict(params)
        m["x"] = np.ascontiguousarray(x_full[c])
        in_maps.append(m)
    res = run_bass_kernel_spmd(nc, in_maps, core_ids=list(range(8)))
    return np.stack([res.results[c]["out"] for c in range(8)], axis=0)
